# version 25
# baseline (speedup 1.0000x reference)
import numpy as np
from contextlib import ExitStack

import concourse.bass as bass
import concourse.mybir as mybir
from concourse.bass_utils import run_bass_kernel_spmd

F32 = mybir.dt.float32
BF16 = mybir.dt.bfloat16
AF = mybir.ActivationFunctionType
ALU = mybir.AluOpType

T = 2048
D = 2048
H = 2560
KD = D // 128
KH = H // 128
TT = 512
NT = T // TT
C32 = float(np.float32(1.5 * 2.0 ** 23))
C64 = float(np.float32(1.5 * 2.0 ** 52))
X_AX = mybir.AxisListType.X


class Buf:
    __slots__ = ("t", "w", "rs")

    def __init__(self, t):
        self.t = t
        self.w = None
        self.rs = []


class Emit:

    ENGSEC = {"pe": "tensor", "dve": "vector", "act": "scalar",
              "pool": "gpsimd", "sp": "sync"}

    def __init__(self, nc, blk, sems):
        self.nc = nc
        self.blk = blk
        self.sems = sems
        self.cnt = {k: 0 for k in sems}
        self.waited = {e: {} for e in self.ENGSEC}
        self.pend = {e: [] for e in self.ENGSEC}
        self.qcnt = {}

    def _flush(self):
        for engname in self.ENGSEC:
            thunks = self.pend[engname]
            if not thunks:
                continue
            self.pend[engname] = []
            sec = getattr(self.blk, self.ENGSEC[engname])

            def body(eng, thunks=thunks):
                for th in thunks:
                    th(eng)
            sec(body)

    def _op(self, engname, fn, reads=(), writes=(), sem=None, inc=1,
            war=(), wend=()):
        sem = sem or engname
        toks = []
        for b in reads:
            if b.w:
                toks.append(b.w)
        for b in writes:
            if b.w:
                toks.append(b.w)
            toks.extend(b.rs)
        for b in war:
            if b.w:
                toks.append(b.w)
            toks.extend(b.rs)
        need = {}
        for s, v in toks:
            if v > need.get(s, 0):
                need[s] = v
        wmap = self.waited[engname]
        need = {s: v for s, v in need.items()
                if v > wmap.get(s, 0) and not (s == sem and False)}
        self.cnt[sem] += inc
        tok = (sem, self.cnt[sem])
        semh = self.sems[sem]
        semmap = self.sems

        def th(eng):
            for s, v in need.items():
                eng.wait_ge(semmap[s], v)
            instr = fn(eng)
            instr.then_inc(semh, inc)
        for s, v in need.items():
            if v > wmap.get(s, 0):
                wmap[s] = v
        self.pend[engname].append(th)
        for b in reads:
            b.rs.append(tok)
        for b in writes:
            b.w = tok
            b.rs = []
        for b in wend:
            b.w = tok
            b.rs = []
        return tok

    def pe(self, fn, reads=(), writes=(), war=(), wend=()):
        return self._op("pe", fn, reads, writes, war=war, wend=wend)

    def dve(self, fn, reads=(), writes=()):
        return self._op("dve", fn, reads, writes)

    def act(self, fn, reads=(), writes=()):
        return self._op("act", fn, reads, writes)

    def pool(self, fn, reads=(), writes=()):
        return self._op("pool", fn, reads, writes)

    def dma(self, q, fn, reads=(), writes=(), war=(), wend=()):
        j = self.qcnt.get(q, 0)
        self.qcnt[q] = j + 1
        semname = "d%s%d" % (q, j % 4)
        prev = self.cnt[semname]
        if prev:
            self.barrier_wait(q, [(semname, prev)])
        return self._op(q, fn, reads, writes, sem=semname, inc=16, war=war,
                        wend=wend)

    def barrier_wait(self, engname, toks):
        need = {}
        for s, v in toks:
            if v and v > need.get(s, 0):
                need[s] = v
        wmap = self.waited[engname]
        need = {s: v for s, v in need.items() if v > wmap.get(s, 0)}
        if not need:
            return
        semmap = self.sems

        def th(eng):
            for s, v in need.items():
                eng.wait_ge(semmap[s], v)
        for s, v in need.items():
            wmap[s] = v
        self.pend[engname].append(th)


def build_nc():
    nc = bass.Bass("TRN2", target_bir_lowering=False)
    dt = nc.dram_tensor
    xN = dt("xN", [T, D], F32, kind="ExternalInput").ap()
    Wl = dt("Wl", [D, H], F32, kind="ExternalInput").ap()
    Wr = dt("Wr", [D, H], F32, kind="ExternalInput").ap()
    Wa = dt("Wa", [H, H], F32, kind="ExternalInput").ap()
    Wi = dt("Wi", [H, H], F32, kind="ExternalInput").ap()
    Wo = dt("Wo", [H, D], F32, kind="ExternalInput").ap()
    WfT = dt("WfT", [D, D], F32, kind="ExternalInput").ap()
    vbl = dt("vbl", [H], F32, kind="ExternalInput").ap()
    vbr = dt("vbr", [H], F32, kind="ExternalInput").ap()
    vba = dt("vba", [H], F32, kind="ExternalInput").ap()
    vbi = dt("vbi", [H], F32, kind="ExternalInput").ap()
    vcb = dt("vcb", [H], F32, kind="ExternalInput").ap()
    vlam = dt("vlam", [H], F32, kind="ExternalInput").ap()
    vcw = dt("vcw", [4, H], F32, kind="ExternalInput").ap()
    vbo = dt("vbo", [D], F32, kind="ExternalInput").ap()
    OUT = dt("out", [T, D], F32, kind="ExternalOutput").ap()
    s_x1 = Buf(dt("s_x1", [T, D], F32, kind="ExternalOutput").ap())
    s_zn = Buf(dt("s_zn", [T, D], BF16, kind="ExternalOutput").ap())
    s_left = Buf(dt("s_left", [H, T], BF16, kind="ExternalOutput").ap())
    s_u = Buf(dt("s_u", [H, T], BF16, kind="ExternalOutput").ap())
    s_gat = Buf(dt("s_gat", [H, T], BF16, kind="ExternalOutput").ap())
    s_x2 = Buf(dt("s_x2", [T, D], F32, kind="ExternalOutput").ap())
    s_x2r = [Buf(s_x2.t) for _ in range(16)]
    s_wqT = Buf(dt("s_wqT", [D, D], BF16, kind="ExternalOutput").ap())
    out_b = Buf(OUT)

    with ExitStack() as ctx:
        E_ = ctx.enter_context
        sb = nc.sbuf_tensor
        bigA = E_(sb("bigA", [128, KH * T], BF16))
        bigB = E_(sb("bigB", [128, KD * TT], F32))
        wg = E_(sb("wg", [128, 16384], BF16))
        xq = wg[:, 0:KD * TT]
        stg = E_(sb("stg", [128, 4096], BF16))
        sc0 = E_(sb("sc0", [128, T + 8], F32))
        sc1 = E_(sb("sc1", [128, T + 8], F32))
        sc2 = E_(sb("sc2", [128, T], F32))
        sc3 = E_(sb("sc3", [128, T], F32))
        tiny4 = E_(sb("tiny4", [1, 4], F32))
        zero2 = E_(sb("zero2", [128, 2], F32))
        epsb = E_(sb("epsb", [128, 1], F32))
        t1a = E_(sb("t1a", [128, TT], F32))
        t1b = E_(sb("t1b", [128, TT], F32))
        xsa = E_(sb("xsa", [128, TT], F32))
        xsb = E_(sb("xsb", [128, TT], F32))
        borow = E_(sb("borow", [1, D], F32))
        stt8 = E_(sb("stt8", [128, 80], F32))
        vecs = E_(sb("vecs", [128, 256], F32))
        acc1 = E_(sb("acc1", [128, 1], F32))
        red1 = E_(sb("red1", [128, 1], F32))
        onesc = E_(sb("onesc", [128, 1], F32))
        onesr = E_(sb("onesr", [1, 128], F32))
        zero1 = E_(sb("zero1", [128, 1], F32))
        g0b = E_(sb("g0b", [128, 1], F32))
        ghb = E_(sb("ghb", [128, 1], F32))
        g1b = E_(sb("g1b", [128, 1], F32))
        h1b = E_(sb("h1b", [128, 1], F32))
        alr = E_(sb("alr", [1, 1], F32))
        ra128 = E_(sb("ra128", [128, 1], F32))
        ha128 = E_(sb("ha128", [128, 1], F32))
        identb = E_(sb("identb", [128, 128], BF16))
        pmm = [E_(nc.psum_tensor("pmm%d" % i, [128, 512], F32)) for i in range(4)]
        pstat = E_(nc.psum_tensor("pstat", [1, 512], F32))
        pbc = E_(nc.psum_tensor("pbc", [128, 512], F32))
        ptrb = [E_(nc.psum_tensor("ptrb%d" % i, [128, 128], BF16))
                for i in range(2)]
        semnames = (["pe", "dve", "act", "pool"] +
                    ["d%s%d" % (q, i) for q in ("sp", "act", "pool")
                     for i in range(4)])
        sems = {n: E_(nc.semaphore(n)) for n in semnames}
        blk = E_(nc.Block())
        E = Emit(nc, blk, sems)

        bigA_b, bigB_b, wg_b, xq_b = Buf(bigA), Buf(bigB), Buf(wg), Buf(wg)
        wgh = [Buf(wg), Buf(wg)]
        stg_b = Buf(stg)

        def merge(dst, *srcs):
            for sbuf_ in srcs:
                if sbuf_.w:
                    dst.rs.append(sbuf_.w)
                dst.rs.extend(sbuf_.rs)
        sc = [Buf(x) for x in (sc0, sc1, sc2, sc3)]
        wfb = [bigB_b, bigB_b]
        t1_ = [Buf(t1a), Buf(t1b)]
        xs_ = [Buf(xsa), Buf(xsb)]
        borow_b = Buf(borow)
        stt8_b = Buf(stt8)
        vecs_b = Buf(vecs)
        tiny_b = Buf(tiny4)
        zero2_b = Buf(zero2)
        acc1_b, red1_b = Buf(acc1), Buf(red1)
        onesc_b, onesr_b, zero1_b = Buf(onesc), Buf(onesr), Buf(zero1)
        bigBh = [Buf(bigB), Buf(bigB)]
        g0_b, gh_b, g1_b, h1_b = Buf(g0b), Buf(ghb), Buf(g1b), Buf(h1b)
        eps_b = Buf(epsb)
        alr_b, ra_b, ha_b = Buf(alr), Buf(ra128), Buf(ha128)
        idb_b = Buf(identb)
        pmm_b = [Buf(p) for p in pmm]
        pstat_b, pbc_b = Buf(pstat), Buf(pbc)
        ptrb_b = [Buf(p) for p in ptrb]

        VBL, VBR, VHBA, VHBI, VCB = 0, 20, 40, 60, 80
        VMS2, VMS1, VBO, VCW, VSP = 100, 120, 140, 156, 236

        def vcol(off, j):
            return vecs[:, off + j:off + j + 1]

        E.dma("sp", lambda e: e.dma_start(
            out=vecs[:, VBL:VBL + 20], in_=vbl.rearrange("(c p) -> p c", p=128),
            allow_slow_non_contiguous=True),
            writes=[vecs_b])
        E.dma("sp", lambda e: e.dma_start(
            out=vecs[:, VBR:VBR + 20], in_=vbr.rearrange("(c p) -> p c", p=128),
            allow_slow_non_contiguous=True),
            writes=[vecs_b])
        E.dma("sp", lambda e: e.dma_start(
            out=vecs[:, VHBA:VHBA + 20], in_=vba.rearrange("(c p) -> p c", p=128),
            allow_slow_non_contiguous=True),
            writes=[vecs_b])
        E.dma("sp", lambda e: e.dma_start(
            out=vecs[:, VHBI:VHBI + 20], in_=vbi.rearrange("(c p) -> p c", p=128),
            allow_slow_non_contiguous=True),
            writes=[vecs_b])
        E.dma("sp", lambda e: e.dma_start(
            out=vecs[:, VCB:VCB + 20], in_=vcb.rearrange("(c p) -> p c", p=128),
            allow_slow_non_contiguous=True),
            writes=[vecs_b])
        E.dma("sp", lambda e: e.dma_start(
            out=vecs[:, VMS1:VMS1 + 20], in_=vlam.rearrange("(c p) -> p c", p=128),
            allow_slow_non_contiguous=True),
            writes=[vecs_b])
        E.dma("sp", lambda e: e.dma_start(
            out=borow[:], in_=vbo.rearrange("(u d) -> u d", u=1)),
            writes=[borow_b])
        E.dma("sp", lambda e: e.dma_start(
            out=vecs[:, VCW:VCW + 80],
            in_=vcw.rearrange("k (c p) -> p (k c)", p=128),
            allow_slow_non_contiguous=True),
            writes=[vecs_b])
        E.act(lambda e: e.activation(out=vecs[:, VSP:VSP + 20],
                                     in_=vecs[:, VMS1:VMS1 + 20],
                                     func=AF.Exp, scale=-1.0),
              reads=[vecs_b], writes=[vecs_b])
        E.dve(lambda e: e.tensor_scalar(vecs[:, VSP:VSP + 20],
                                        vecs[:, VSP:VSP + 20], 1.0, None,
                                        ALU.add),
              reads=[vecs_b], writes=[vecs_b])
        E.act(lambda e: e.activation(out=vecs[:, VSP:VSP + 20],
                                     in_=vecs[:, VSP:VSP + 20], func=AF.Ln),
              reads=[vecs_b], writes=[vecs_b])
        E.dve(lambda e: e.tensor_scalar(vecs[:, VMS1:VMS1 + 20],
                                        vecs[:, VSP:VSP + 20], -8.0, None, ALU.mult),
              reads=[vecs_b], writes=[vecs_b])
        E.dve(lambda e: e.tensor_scalar(vecs[:, VMS2:VMS2 + 20],
                                        vecs[:, VSP:VSP + 20], -4.0, None, ALU.mult),
              reads=[vecs_b], writes=[vecs_b])
        E.dve(lambda e: e.tensor_scalar(vecs[:, VHBA:VHBA + 20],
                                        vecs[:, VHBA:VHBA + 20], 0.5, None, ALU.mult),
              reads=[vecs_b], writes=[vecs_b])
        E.dve(lambda e: e.tensor_scalar(vecs[:, VHBI:VHBI + 20],
                                        vecs[:, VHBI:VHBI + 20], 0.5, None, ALU.mult),
              reads=[vecs_b], writes=[vecs_b])
        E.dve(lambda e: e.memset(onesc[:], 1.0), writes=[onesc_b])
        E.dve(lambda e: e.memset(onesr[:], 1.0), writes=[onesr_b])
        E.dve(lambda e: e.memset(zero1[:], 0.0), writes=[zero1_b])
        E.dve(lambda e: e.memset(zero2[:], 0.0), writes=[zero2_b])
        E.dve(lambda e: e.memset(g0b[:], C32), writes=[g0_b])
        E.dve(lambda e: e.memset(epsb[:], 1e-5), writes=[eps_b])
        E.dve(lambda e: e.tensor_scalar(ghb[:], g0b[:], 0.5, None, ALU.add),
              reads=[g0_b], writes=[gh_b])
        E.dve(lambda e: e.tensor_tensor(g1b[:], ghb[:], g0b[:], ALU.subtract),
              reads=[gh_b, g0_b], writes=[g1_b])
        E.dve(lambda e: e.tensor_scalar(g1b[:], g1b[:], 2.0, None, ALU.mult),
              reads=[g1_b], writes=[g1_b])
        E.dve(lambda e: e.tensor_scalar(h1b[:], g1b[:], -1.0, 1.0,
                                        ALU.mult, ALU.add),
              reads=[g1_b], writes=[h1_b])
        E.pool(lambda e: e.memset(identb[:], 0.0), writes=[idb_b])
        E.pool(lambda e: e.affine_select(out=identb[:], in_=identb[:],
                                         compare_op=ALU.not_equal, fill=1.0,
                                         base=0, pattern=[[-1, 128]],
                                         channel_multiplier=1),
               reads=[idb_b], writes=[idb_b])

        for c in range(KD):
            b = bigBh[c % 2]
            hsl = slice((c % 2) * D, (c % 2) * D + D)
            E.dma("sp" if c % 2 == 0 else "act",
                  lambda e, c=c, hsl=hsl: e.dma_start(
                      out=bigB[:, hsl], in_=WfT[c * 128:(c + 1) * 128, :]),
                  writes=[b])
            E.dve(lambda e, b=b, hsl=hsl: e.tensor_reduce(
                red1[:], bigB[:, hsl], axis=X_AX, op=ALU.add,
                apply_absolute_value=True), reads=[b], writes=[red1_b])
            if c == 0:
                E.dve(lambda e: e.tensor_copy(acc1[:], red1[:]),
                      reads=[red1_b], writes=[acc1_b])
            else:
                E.dve(lambda e: e.tensor_tensor(acc1[:], acc1[:], red1[:], ALU.add),
                      reads=[acc1_b, red1_b], writes=[acc1_b])
        E.pe(lambda e: e.matmul(pstat[0:1, 0:1], onesc[:], acc1[:],
                                start=True, stop=True),
             reads=[onesc_b, acc1_b], writes=[pstat_b])
        E.act(lambda e: e.activation(out=alr[:], in_=pstat[0:1, 0:1],
                                     func=AF.Copy, scale=1.0 / (D * D)),
              reads=[pstat_b], writes=[alr_b])
        E.dve(lambda e: e.tensor_scalar(alr[:], alr[:], 1e-8, None, ALU.max),
              reads=[alr_b], writes=[alr_b])
        E.act(lambda e: e.activation(out=tiny4[0:1, 0:1], in_=alr[:],
                                     func=AF.Copy, scale=0.5 / 127.0),
              reads=[alr_b], writes=[tiny_b])
        E.pe(lambda e: e.matmul(pbc[0:128, 0:1], onesr[:], tiny4[0:1, 0:1],
                                start=True, stop=True),
             reads=[tiny_b, onesr_b], writes=[pbc_b])
        E.act(lambda e: e.activation(out=ha128[:], in_=pbc[0:128, 0:1],
                                     func=AF.Copy), reads=[pbc_b], writes=[ha_b])
        E.dve(lambda e: e.reciprocal(tiny4[0:1, 1:2], alr[:]),
              reads=[alr_b], writes=[tiny_b])
        E.pe(lambda e: e.matmul(pbc[0:128, 0:1], onesr[:], tiny4[0:1, 1:2],
                                start=True, stop=True),
             reads=[onesr_b, tiny_b], writes=[pbc_b])
        E.act(lambda e: e.activation(out=ra128[:], in_=pbc[0:128, 0:1],
                                     func=AF.Copy), reads=[pbc_b], writes=[ra_b])

        def emit_quant(c):
            bq = bigBh[c % 2]
            hof = (c % 2) * D
            E.dma("sp" if c % 2 == 0 else "act",
                  lambda e, c=c, hof=hof: e.dma_start(
                      out=bigB[:, hof:hof + D],
                      in_=WfT[c * 128:(c + 1) * 128, :]),
                  writes=[bq])
            E.act(lambda e, bq=bq, hof=hof: e.activation(
                out=sc1[:, 0:D], in_=bigB[:, hof:hof + D], func=AF.Copy,
                scale=ra128[:]), reads=[bq, ra_b], writes=[sc[1]])
            E.dve(lambda e: e.tensor_scalar(sc2[:], sc1[:, 0:D], C32, -C32,
                                            ALU.add, ALU.add),
                  reads=[sc[1]], writes=[sc[2]])
            E.dve(lambda e: e.tensor_scalar(sc3[:], sc1[:, 0:D], C64, -C64,
                                            ALU.add, ALU.add),
                  reads=[sc[1]], writes=[sc[3]])
            E.dve(lambda e: e.scalar_tensor_tensor(sc3[:], sc2[:], h1b[:], sc3[:],
                                                   ALU.mult, ALU.add),
                  reads=[sc[2], h1_b, sc[3]], writes=[sc[3]])
            E.dve(lambda e, c=c: e.tensor_scalar(
                bigA[:, c * D:(c + 1) * D], sc3[:], 1.0, -1.0, ALU.min, ALU.max),
                reads=[sc[3]], writes=[bigA_c[c]])
            E.dma("act", lambda e, c=c: e.dma_start(
                out=s_wqT.t[c * 128:(c + 1) * 128, :],
                in_=bigA[:, c * D:(c + 1) * D]),
                reads=[bigA_c[c]], writes=[s_wqT])
        merge(bigB_b, bigBh[0], bigBh[1])

        TB = 256
        NTF = T // TB
        stt8_p = [stt8_b, Buf(stt8)]
        stg_p = [stg_b, Buf(stg)]
        bigB_h = [Buf(bigB), Buf(bigB)]
        xq_h = [Buf(wg), Buf(wg)]

        def sc8(ctx, j):
            return stt8[:, ctx * 32 + j * 2: ctx * 32 + j * 2 + 2]

        def sc8b(ctx, j, tb):
            return stt8[:, ctx * 32 + j * 2 + tb: ctx * 32 + j * 2 + tb + 1]

        rnd_p = [Buf(wg), Buf(wg)]
        bigA_c = [Buf(bigA) for _ in range(KD)]

        def ffn_parts(src_kind, with_ln2, dst_hbm, zn_out):

            def pre_mm(k):
                hp = (k + 1) % 2
                ctx = hp
                bB = bigB_h[hp]
                sb8 = stt8_p[ctx]
                hof = hp * 2 * D
                tsl = slice(k * TB, (k + 1) * TB)
                src = xN if src_kind == "x" else s_x2.t
                if src_kind == "x":
                    rd = []
                else:
                    rd = [s_x2r[2 * k], s_x2r[2 * k + 1]]
                bh = [Buf(bigB), Buf(bigB)]
                for tb in range(2):
                    rowsl = slice(k * TB + tb * 128, k * TB + (tb + 1) * 128)
                    E.dma("sp", lambda e, rowsl=rowsl, src=src, hof=hof, tb=tb:
                          e.dma_start(
                              out=bigB[:, hof + tb * D:hof + (tb + 1) * D],
                              in_=src[rowsl, :]),
                          reads=rd, war=[bB], wend=[bh[tb]])
                for tb in range(2):
                    po = tb
                    bsl = slice(hof + tb * D, hof + (tb + 1) * D)
                    E.dve(lambda e, bsl=bsl, ctx=ctx, tb=tb: e.reduce_max(
                        sc8b(ctx, 2, tb), bigB[:, bsl], axis=X_AX),
                        reads=[bh[tb]], writes=[sb8])
                    E.dve(lambda e, bsl=bsl, ctx=ctx, tb=tb: e.tensor_reduce(
                        sc8b(ctx, 3, tb), bigB[:, bsl], axis=X_AX, op=ALU.min),
                        reads=[bh[tb]], writes=[sb8])
                    E.act(lambda e, bsl=bsl, ctx=ctx, po=po, tb=tb:
                          e.activation(
                              out=stg[:, po * D:(po + 1) * D],
                              in_=bigB[:, bsl], func=AF.Square,
                              accum_out=sc8b(ctx, 1, tb)),
                          reads=[bh[tb]], writes=[stg_p[po], sb8])
                    E.act(lambda e, bsl=bsl, ctx=ctx, po=po, tb=tb:
                          e.activation(
                              out=stg[:, po * D:(po + 1) * D],
                              in_=bigB[:, bsl], func=AF.Copy,
                              accum_out=sc8b(ctx, 0, tb)),
                          reads=[bh[tb]], writes=[stg_p[po], sb8])
                merge(bB, bh[0], bh[1])
                E.dve(lambda e, ctx=ctx: e.tensor_scalar(
                    sc8(ctx, 4), sc8(ctx, 0), 1.0 / D, None, ALU.mult),
                    reads=[sb8], writes=[sb8])
                E.dve(lambda e, ctx=ctx: e.tensor_scalar(
                    sc8(ctx, 10), sc8(ctx, 1), 1.0 / D, None, ALU.mult),
                    reads=[sb8], writes=[sb8])
                E.dve(lambda e, ctx=ctx: e.tensor_tensor(
                    sc8(ctx, 11), sc8(ctx, 4), sc8(ctx, 4), ALU.mult),
                    reads=[sb8], writes=[sb8])
                E.dve(lambda e, ctx=ctx: e.tensor_tensor(
                    sc8(ctx, 11), sc8(ctx, 11), sc8(ctx, 10), ALU.subtract),
                    reads=[sb8], writes=[sb8])
                E.act(lambda e, ctx=ctx: e.activation(
                    out=sc8(ctx, 11), in_=sc8(ctx, 11), func=AF.Ln,
                    scale=-1.0, bias=epsb[:]),
                    reads=[sb8, eps_b], writes=[sb8])
                E.act(lambda e, ctx=ctx: e.activation(
                    out=sc8(ctx, 5), in_=sc8(ctx, 11), func=AF.Exp,
                    scale=-0.5), reads=[sb8], writes=[sb8])
                E.dve(lambda e, ctx=ctx: e.tensor_tensor(
                    sc8(ctx, 2), sc8(ctx, 2), sc8(ctx, 4), ALU.subtract),
                    reads=[sb8], writes=[sb8])
                E.dve(lambda e, ctx=ctx: e.tensor_tensor(
                    sc8(ctx, 3), sc8(ctx, 4), sc8(ctx, 3), ALU.subtract),
                    reads=[sb8], writes=[sb8])
                E.dve(lambda e, ctx=ctx: e.tensor_tensor(
                    sc8(ctx, 6), sc8(ctx, 2), sc8(ctx, 3), ALU.max),
                    reads=[sb8], writes=[sb8])
                E.dve(lambda e, ctx=ctx: e.tensor_scalar(
                    sc8(ctx, 6), sc8(ctx, 6), 1.0 / 127.0, 7.9e-33,
                    ALU.mult, ALU.max),
                    reads=[sb8], writes=[sb8])
                E.dve(lambda e, ctx=ctx: e.reciprocal(
                    sc8(ctx, 7), sc8(ctx, 6)), reads=[sb8], writes=[sb8])
                E.dve(lambda e, ctx=ctx: e.tensor_tensor(
                    sc8(ctx, 8), sc8(ctx, 4), sc8(ctx, 7), ALU.mult),
                    reads=[sb8], writes=[sb8])
                E.dve(lambda e, ctx=ctx: e.tensor_scalar(
                    sc8(ctx, 8), sc8(ctx, 8), -1.0, None, ALU.mult),
                    reads=[sb8], writes=[sb8])
                E.dve(lambda e, ctx=ctx: e.tensor_tensor(
                    sc8(ctx, 10), sc8(ctx, 5), sc8(ctx, 6), ALU.mult),
                    reads=[sb8], writes=[sb8])
                E.dve(lambda e, ctx=ctx: e.scalar_tensor_tensor(
                    sc8(ctx, 9), sc8(ctx, 10), ha128[:], zero2[:, 0:2],
                    ALU.mult, ALU.subtract),
                    reads=[sb8, ha_b, zero2_b], writes=[sb8])
                E.dve(lambda e, ctx=ctx: e.tensor_scalar(
                    sc8(ctx, 9), sc8(ctx, 9), 127.0, None, ALU.mult),
                    reads=[sb8], writes=[sb8])
                for tb in range(2):
                    po = tb
                    rG = rnd_p[po]
                    bsl = slice(hof + tb * D, hof + (tb + 1) * D)
                    rsl = slice(8192 + po * D, 8192 + (po + 1) * D)
                    zt = sc1 if po == 0 else sc2
                    zb = sc[1] if po == 0 else sc[2]
                    E.act(lambda e, bsl=bsl, ctx=ctx, tb=tb, zt=zt:
                          e.activation(
                              out=zt[:, 0:D], in_=bigB[:, bsl],
                              func=AF.Identity, scale=sc8b(ctx, 7, tb),
                              bias=sc8b(ctx, 8, tb)),
                          reads=[bB, sb8], writes=[zb])
                    ab = sc[3] if po == 0 else sc[0]
                    at = sc3 if po == 0 else sc0
                    E.dve(lambda e, zt=zt, at=at: e.tensor_scalar(
                        at[:, 0:D], zt[:, 0:D], C32, -C32, ALU.add, ALU.add),
                        reads=[zb], writes=[ab])
                    E.dve(lambda e, zt=zt: e.tensor_scalar(
                        zt[:, 0:D], zt[:, 0:D], C64, -C64, ALU.add, ALU.add),
                        reads=[zb], writes=[zb])
                    E.dve(lambda e, rsl=rsl, at=at, zt=zt:
                          e.scalar_tensor_tensor(
                              wg[:, rsl], at[:, 0:D], h1b[:], zt[:, 0:D],
                              ALU.mult, ALU.add),
                          reads=[ab, h1_b, zb], writes=[rG])

            def tr(k, jgrp):
                hp = (k + 1) % 2
                xQ = xq_h[hp]
                xof = hp * KD * TB
                base = jgrp * 4
                for u in range(base, base + 4):
                    tb, i = divmod(u, KD)
                    po = tb
                    rG = rnd_p[po]
                    rsl0 = 8192 + po * D
                    pb = ptrb_b[u % 2]
                    E.pe(lambda e, rsl0=rsl0, i=i, pb=pb: e.transpose(
                        pb.t[:], wg[:, rsl0 + i * 128: rsl0 + (i + 1) * 128],
                        identb[:]),
                        reads=[rG, idb_b], writes=[pb])
                    off = xof + i * TB + tb * 128
                    E.dve(lambda e, off=off, pb=pb: e.tensor_copy(
                        wg[:, off:off + 128], pb.t[:]),
                        reads=[pb], writes=[xQ])

            def mm(k, embed=None):
                hp = (k + 1) % 2
                ctx = hp
                sb8 = stt8_p[ctx]
                bB = bigB_h[hp]
                xQ = xq_h[hp]
                hof = hp * 2 * D
                xof = hp * KD * TB
                tsl = slice(k * TB, (k + 1) * TB)
                grp = 0
                for tb in range(2):
                    po = tb
                    for os in range(4):
                        p = pmm_b[(tb * 4 + os) % 4]
                        for i in range(KD):
                            E.pe(lambda e, i=i, tb=tb, os=os, p=p, xof=xof:
                                 e.matmul(
                                     p.t[:],
                                     wg[:, xof + i * TB + tb * 128:
                                         xof + i * TB + tb * 128 + 128],
                                     bigA[:, i * D + os * 512:
                                          i * D + (os + 1) * 512],
                                     start=(i == 0), stop=(i == KD - 1)),
                                 reads=[xQ, bigA_c[i]],
                                 war=[p] if i == 0 else (),
                                 wend=[p] if i == KD - 1 else ())
                        if embed is not None and grp >= 4:
                            embed((grp - 4) * 2)
                            embed((grp - 4) * 2 + 1)
                        grp += 1
                        xo = slice(hof + tb * D + os * 512,
                                   hof + tb * D + (os + 1) * 512)
                        if with_ln2:
                            E.dve(lambda e, p=p, ctx=ctx, tb=tb, os=os, xo=xo:
                                  e.scalar_tensor_tensor(
                                      bigB[:, xo], p.t[:], sc8b(ctx, 9, tb),
                                      bigB[:, xo], ALU.mult, ALU.add,
                                      accum_out=sc8b(ctx, 16 + os, tb)),
                                  reads=[p, sb8, bB], writes=[bB, sb8])
                            sqs = slice(tb * D + os * 512,
                                        tb * D + (os + 1) * 512)
                            E.act(lambda e, ctx=ctx, tb=tb, os=os, xo=xo,
                                  sqs=sqs: e.activation(
                                      out=stg[:, sqs], in_=bigB[:, xo],
                                      func=AF.Square,
                                      accum_out=sc8b(ctx, 12 + os, tb)),
                                  reads=[bB], writes=[stg_p[tb], sb8])
                        else:
                            E.dve(lambda e, p=p, ctx=ctx, tb=tb, xo=xo:
                                  e.scalar_tensor_tensor(
                                      bigB[:, xo], p.t[:], sc8b(ctx, 9, tb),
                                      bigB[:, xo], ALU.mult, ALU.add),
                                  reads=[p, sb8, bB], writes=[bB])
                E.dma("pool" if k < NTF - 1 else "sp",
                      lambda e, tsl=tsl, hof=hof: e.dma_start(
                          out=dst_hbm.t[tsl, :].rearrange(
                              "(b p) d -> p b d", p=128),
                          in_=bigB[:, hof:hof + 2 * D].rearrange(
                              "p (b d) -> p b d", d=D)),
                      reads=[bB], writes=[dst_hbm])
                if with_ln2:
                    E.dve(lambda e, ctx=ctx: e.tensor_tensor(
                        sc8(ctx, 16), sc8(ctx, 16), sc8(ctx, 17), ALU.add),
                        reads=[sb8], writes=[sb8])
                    E.dve(lambda e, ctx=ctx: e.tensor_tensor(
                        sc8(ctx, 18), sc8(ctx, 18), sc8(ctx, 19), ALU.add),
                        reads=[sb8], writes=[sb8])
                    E.dve(lambda e, ctx=ctx: e.tensor_tensor(
                        sc8(ctx, 0), sc8(ctx, 16), sc8(ctx, 18), ALU.add),
                        reads=[sb8], writes=[sb8])
                    E.dve(lambda e, ctx=ctx: e.tensor_tensor(
                        sc8(ctx, 12), sc8(ctx, 12), sc8(ctx, 13), ALU.add),
                        reads=[sb8], writes=[sb8])
                    E.dve(lambda e, ctx=ctx: e.tensor_tensor(
                        sc8(ctx, 14), sc8(ctx, 14), sc8(ctx, 15), ALU.add),
                        reads=[sb8], writes=[sb8])
                    E.dve(lambda e, ctx=ctx: e.tensor_tensor(
                        sc8(ctx, 1), sc8(ctx, 12), sc8(ctx, 14), ALU.add),
                        reads=[sb8], writes=[sb8])
                    E.dve(lambda e, ctx=ctx: e.tensor_scalar(
                        sc8(ctx, 4), sc8(ctx, 0), 1.0 / D, None, ALU.mult),
                        reads=[sb8], writes=[sb8])
                    E.dve(lambda e, ctx=ctx: e.tensor_scalar(
                        sc8(ctx, 10), sc8(ctx, 1), 1.0 / D, None, ALU.mult),
                        reads=[sb8], writes=[sb8])
                    E.dve(lambda e, ctx=ctx: e.tensor_tensor(
                        sc8(ctx, 11), sc8(ctx, 4), sc8(ctx, 4), ALU.mult),
                        reads=[sb8], writes=[sb8])
                    E.dve(lambda e, ctx=ctx: e.tensor_tensor(
                        sc8(ctx, 11), sc8(ctx, 11), sc8(ctx, 10), ALU.subtract),
                        reads=[sb8], writes=[sb8])
                    E.act(lambda e, ctx=ctx: e.activation(
                        out=sc8(ctx, 11), in_=sc8(ctx, 11), func=AF.Ln,
                        scale=-1.0, bias=epsb[:]),
                        reads=[sb8, eps_b], writes=[sb8])
                    E.act(lambda e, ctx=ctx: e.activation(
                        out=sc8(ctx, 5), in_=sc8(ctx, 11), func=AF.Exp,
                        scale=-0.5), reads=[sb8], writes=[sb8])
                    E.dve(lambda e, ctx=ctx: e.tensor_tensor(
                        sc8(ctx, 8), sc8(ctx, 4), sc8(ctx, 5), ALU.mult),
                        reads=[sb8], writes=[sb8])
                    E.dve(lambda e, ctx=ctx: e.tensor_scalar(
                        sc8(ctx, 8), sc8(ctx, 8), -1.0, None, ALU.mult),
                        reads=[sb8], writes=[sb8])
                    for tb in range(2):
                        po = tb
                        bsl = slice(hof + tb * D, hof + (tb + 1) * D)
                        ssl = slice(po * D, (po + 1) * D)
                        E.act(lambda e, bsl=bsl, ssl=ssl, ctx=ctx, tb=tb:
                              e.activation(
                                  out=stg[:, ssl], in_=bigB[:, bsl],
                                  func=AF.Identity, scale=sc8b(ctx, 5, tb),
                                  bias=sc8b(ctx, 8, tb)),
                              reads=[bB, sb8], writes=[stg_p[tb]])
                    E.dma("pool" if k < NTF - 1 else "sp",
                      lambda e, tsl=tsl: e.dma_start(
                          out=zn_out.t[tsl, :].rearrange(
                              "(b p) d -> p b d", p=128),
                          in_=stg[:, 0:2 * D].rearrange(
                              "p (b d) -> p b d", d=D)),
                      reads=[stg_p[0], stg_p[1]], writes=[zn_out])


            return pre_mm, tr, mm

        def ffn_phase(src_kind, with_ln2, dst_hbm, zn_out):
            pre_mm, tr, mm = ffn_parts(src_kind, with_ln2, dst_hbm, zn_out)
            pre_mm(0)
            for j in range(8):
                tr(0, j)
            pre_mm(1)
            for k in range(NTF):
                nxt = k + 1
                if nxt < NTF:
                    mm(k, embed=lambda j, nxt=nxt: tr(nxt, j))
                else:
                    mm(k)
                if k + 2 < NTF:
                    pre_mm(k + 2)

        pre1, tr1, mm1 = ffn_parts("x", True, s_x1, s_zn)
        for c in range(KD):
            emit_quant(c)
            if c == 9:
                pre1(0)
            if c == 12:
                for j in range(8):
                    tr1(0, j)
        merge(bigB_h[0], bigB_b, bigBh[0], bigBh[1])
        pre1(1)
        for k in range(NTF):
            nxt = k + 1
            if nxt < NTF:
                mm1(k, embed=lambda j, nxt=nxt: tr1(nxt, j))
            else:
                mm1(k)
            if k + 2 < NTF:
                pre1(k + 2)
        merge(stg_b, stg_p[1])

        merge(wgh[0], xq_b, *xq_h, *rnd_p)
        merge(wgh[1], xq_b, *xq_h, *rnd_p)
        merge(stg_b, stg_p[0], stg_p[1])
        merge(stg_p[0], stg_b)
        merge(stg_p[1], stg_b)
        for c in range(KD):
            E.dma("act" if c % 2 else "sp", lambda e, c=c: e.dma_start(
                out=bigA[:, c * T:(c + 1) * T],
                in_=s_zn.t[:, c * 128:(c + 1) * 128], transpose=True),
                reads=[s_zn], writes=[bigA_b], war=[bigA_c[c]])

        def wl_phase(W, evac, out_hbm, conv):
            ng = H // 512
            for g in range(ng):
                half = g % 2
                wslice = slice(half * 8192, half * 8192 + 8192)
                E.dma("pool", lambda e, g=g, wslice=wslice, W=W: e.dma_start(
                    out=wg[:, wslice].rearrange("p (c h) -> p c h", h=512),
                    in_=W[:, g * 512:(g + 1) * 512].rearrange(
                        "(c p) h -> p c h", p=128)),
                    writes=[wgh[half]])
                for hb in range(4):
                    blkidx = g * 4 + hb
                    for k in range(NT):
                        p = pmm_b[k % 4]
                        for i in range(KD):
                            off = half * 8192 + i * 512 + hb * 128
                            E.pe(lambda e, off=off, i=i, k=k, p=p: e.matmul(
                                p.t[:], wg[:, off:off + 128],
                                bigA[:, i * T + k * TT: i * T + (k + 1) * TT],
                                start=(i == 0), stop=(i == KD - 1)),
                                reads=[wgh[half], bigA_b],
                                war=[p] if i == 0 else (),
                                wend=[p] if i == KD - 1 else ())
                        evac(blkidx, k, p)
                    if conv is not None:
                        conv(blkidx)

        def left_evac(blkidx, k, p):
            st = stg[:, (k % 4) * TT:(k % 4) * TT + TT]
            E.act(lambda e, blkidx=blkidx, p=p, st=st: e.activation(
                out=st, in_=p.t[:], func=AF.Gelu_apprx_tanh,
                bias=vcol(VBL, blkidx)),
                reads=[p, vecs_b], writes=[stg_b])
            E.dma("sp", lambda e, blkidx=blkidx, k=k, st=st: e.dma_start(
                out=s_left.t[blkidx * 128:(blkidx + 1) * 128,
                              k * TT:(k + 1) * TT], in_=st),
                reads=[stg_b], writes=[s_left])

        wl_phase(Wl, left_evac, s_left, None)

        def rin_evac(blkidx, k, p):
            E.act(lambda e, blkidx=blkidx, k=k, p=p: e.activation(
                out=sc0[:, 3 + k * TT: 3 + (k + 1) * TT], in_=p.t[:],
                func=AF.Identity, bias=vcol(VBR, blkidx)),
                reads=[p, vecs_b], writes=[sc[0]])

        ng = H // 512
        for g in range(ng):
            half = g % 2
            wslice = slice(half * 8192, half * 8192 + 8192)
            E.dma("pool", lambda e, g=g, wslice=wslice: e.dma_start(
                out=wg[:, wslice].rearrange("p (c h) -> p c h", h=512),
                in_=Wr[:, g * 512:(g + 1) * 512].rearrange(
                    "(c p) h -> p c h", p=128)),
                writes=[wgh[half]])
            for hb in range(4):
                blkidx = g * 4 + hb
                bp = blkidx % 2
                rinT = sc0 if bp == 0 else sc1
                rinB = sc[0] if bp == 0 else sc[1]
                uacc = sc2 if bp == 0 else sc3
                uaccB = sc[2] if bp == 0 else sc[3]
                E.dve(lambda e, rinT=rinT: e.memset(rinT[:, 0:3], 0.0),
                      writes=[rinB])
                for k in range(NT):
                    p = pmm_b[k % 4]
                    for i in range(KD):
                        off = half * 8192 + i * 512 + hb * 128
                        E.pe(lambda e, off=off, i=i, k=k, p=p: e.matmul(
                            p.t[:], wg[:, off:off + 128],
                            bigA[:, i * T + k * TT: i * T + (k + 1) * TT],
                            start=(i == 0), stop=(i == KD - 1)),
                            reads=[wgh[half], bigA_b],
                            war=[p] if i == 0 else (),
                            wend=[p] if i == KD - 1 else ())
                    E.act(lambda e, blkidx=blkidx, k=k, p=p, rinT=rinT:
                          e.activation(
                              out=rinT[:, 3 + k * TT: 3 + (k + 1) * TT],
                              in_=p.t[:], func=AF.Identity,
                              bias=vcol(VBR, blkidx)),
                          reads=[p, vecs_b], writes=[rinB])
                E.act(lambda e, blkidx=blkidx, rinT=rinT, uacc=uacc:
                      e.activation(
                          out=uacc[:, 0:T], in_=rinT[:, 0:T], func=AF.Identity,
                          scale=vcol(VCW, blkidx), bias=vcol(VCB, blkidx)),
                      reads=[rinB, vecs_b], writes=[uaccB])
                for kk in (1, 2, 3):
                    E.dve(lambda e, kk=kk, blkidx=blkidx, rinT=rinT, uacc=uacc:
                          e.scalar_tensor_tensor(
                              uacc[:, 0:T], rinT[:, kk:kk + T],
                              vcol(VCW, kk * 20 + blkidx),
                              uacc[:, 0:T], ALU.mult, ALU.add),
                          reads=[rinB, vecs_b, uaccB], writes=[uaccB])
                stu = stg[:, bp * T:bp * T + T]
                E.dve(lambda e, stu=stu, uacc=uacc: e.tensor_copy(
                    stu, uacc[:, 0:T]),
                    reads=[uaccB], writes=[stg_p[bp]])
                E.dma("sp", lambda e, blkidx=blkidx, stu=stu: e.dma_start(
                    out=s_u.t[blkidx * 128:(blkidx + 1) * 128, :], in_=stu),
                    reads=[stg_p[bp]], writes=[s_u])

        merge(stg_p[0], stg_b, stg_p[1])
        merge(stg_p[1], stg_b, stg_p[0])
        merge(rnd_p[0], wgh[0], wgh[1])
        E.dma("act", lambda e: e.dma_start(
            out=bigA[:, 0:KH * T].rearrange("p (c t) -> p c t", t=T),
            in_=s_u.t[:].rearrange("(c p) t -> p c t", p=128)),
            reads=[s_u], writes=[bigA_b])
        for b in range(KH):
            half = b % 2
            wa_sl = slice(half * 5120, half * 5120 + 2560)
            wi_sl = slice(half * 5120 + 2560, half * 5120 + 5120)
            E.dma("pool", lambda e, b=b, wa_sl=wa_sl: e.dma_start(
                out=wg[:, wa_sl].rearrange("p (c h) -> p c h", h=128),
                in_=Wa[:, b * 128:(b + 1) * 128].rearrange(
                    "(c p) h -> p c h", p=128)), writes=[wgh[half]])
            E.dma("pool", lambda e, b=b, wi_sl=wi_sl: e.dma_start(
                out=wg[:, wi_sl].rearrange("p (c h) -> p c h", h=128),
                in_=Wi[:, b * 128:(b + 1) * 128].rearrange(
                    "(c p) h -> p c h", p=128)), writes=[wgh[half]])
            stl = stg[:, (b % 2) * T:(b % 2) * T + T]
            stlb = stg_p[b % 2]
            E.dma("sp", lambda e, b=b, stl=stl: e.dma_start(
                out=stl, in_=s_left.t[b * 128:(b + 1) * 128, :]),
                reads=[s_left], writes=[stlb])
            for k in range(NT):
                pr = pmm_b[(2 * k) % 4]
                pi = pmm_b[(2 * k + 1) % 4]
                for c in range(KH):
                    off = half * 5120 + c * 128
                    E.pe(lambda e, off=off, c=c, pr=pr, k=k: e.matmul(
                        pr.t[:], wg[:, off:off + 128],
                        bigA[:, c * T + k * TT: c * T + (k + 1) * TT],
                        start=(c == 0), stop=(c == KH - 1)),
                        reads=[wgh[half], bigA_b],
                        war=[pr] if c == 0 else (),
                        wend=[pr] if c == KH - 1 else ())
                for c in range(KH):
                    off = half * 5120 + 2560 + c * 128
                    E.pe(lambda e, off=off, c=c, pi=pi, k=k: e.matmul(
                        pi.t[:], wg[:, off:off + 128],
                        bigA[:, c * T + k * TT: c * T + (k + 1) * TT],
                        start=(c == 0), stop=(c == KH - 1)),
                        reads=[wgh[half], bigA_b],
                        war=[pi] if c == 0 else (),
                        wend=[pi] if c == KH - 1 else ())
                ksl = slice(k * TT, (k + 1) * TT)
                E.act(lambda e, b=b, pr=pr, ksl=ksl: e.activation(
                    out=sc0[:, ksl], in_=pr.t[:], func=AF.Tanh, scale=0.5,
                    bias=vcol(VHBA, b)), reads=[pr, vecs_b], writes=[sc[0]])
                E.act(lambda e, b=b, pi=pi, ksl=ksl: e.activation(
                    out=sc1[:, ksl], in_=pi.t[:], func=AF.Tanh, scale=0.5,
                    bias=vcol(VHBI, b)), reads=[pi, vecs_b], writes=[sc[1]])
            E.act(lambda e, b=b: e.activation(
                out=sc2[:], in_=sc0[:, 0:T], func=AF.Exp,
                scale=vcol(VMS2, b), bias=vcol(VMS2, b)),
                reads=[sc[0], vecs_b], writes=[sc[2]])
            E.act(lambda e, b=b: e.activation(
                out=sc3[:], in_=sc0[:, 0:T], func=AF.Exp,
                scale=vcol(VMS1, b), bias=vcol(VMS1, b)),
                reads=[sc[0], vecs_b], writes=[sc[3]])
            E.dve(lambda e: e.tensor_scalar(sc3[:], sc3[:], -1.0, 1.0,
                                            ALU.mult, ALU.add),
                  reads=[sc[3]], writes=[sc[3]])
            E.dve(lambda e: e.tensor_scalar(sc3[:], sc3[:], 1e-12, None,
                                            ALU.max),
                  reads=[sc[3]], writes=[sc[3]])
            E.act(lambda e: e.activation(out=sc0[:, 0:T], in_=sc3[:],
                                         func=AF.Sqrt, scale=0.25),
                  reads=[sc[3]], writes=[sc[0]])
            E.dve(lambda e: e.tensor_scalar(sc1[:, 0:T], sc1[:, 0:T], 1.0,
                                            None, ALU.add),
                  reads=[sc[1]], writes=[sc[1]])
            E.dve(lambda e, b=b: e.tensor_tensor(
                sc1[:, 0:T], sc1[:, 0:T], bigA[:, b * T:(b + 1) * T],
                ALU.mult),
                reads=[sc[1], bigA_b], writes=[sc[1]])
            E.dve(lambda e: e.tensor_tensor(sc1[:, 0:T], sc1[:, 0:T],
                                            sc0[:, 0:T], ALU.mult),
                  reads=[sc[1], sc[0]], writes=[sc[1]])
            E.dve(lambda e: e.tensor_tensor_scan(sc3[:], sc2[:], sc1[:, 0:T],
                                                 zero1[:], ALU.mult, ALU.add),
                  reads=[sc[2], sc[1], zero1_b], writes=[sc[3]])
            stgo = wg[:, 10240:10240 + T]
            E.dve(lambda e, stl=stl, stgo=stgo, stlb=stlb: e.tensor_tensor(
                stgo, sc3[:], stl, ALU.mult),
                reads=[sc[3], stlb], writes=[rnd_p[0]])
            E.dma("sp", lambda e, b=b, stgo=stgo: e.dma_start(
                out=s_gat.t[b * 128:(b + 1) * 128, :], in_=stgo),
                reads=[rnd_p[0]], writes=[s_gat])

        E.dma("act", lambda e: e.dma_start(
            out=bigA[:, 0:KH * T].rearrange("p (c t) -> p c t", t=T),
            in_=s_gat.t[:].rearrange("(c p) t -> p c t", p=128)),
            reads=[s_gat], writes=[bigA_b])
        for os in range(4):
            osl = slice(os * 512, (os + 1) * 512)
            E.dma("pool", lambda e, osl=osl: e.dma_start(
                out=wg[:, 0:KH * 512].rearrange("p (c h) -> p c h", h=512),
                in_=Wo[:, osl].rearrange("(c p) h -> p c h", p=128)),
                writes=[wgh[0], wgh[1]], war=[rnd_p[0]])
            for tb in range(16):
                p = pmm_b[tb % 4]
                xb = xs_[tb % 2]
                E.dma("sp", lambda e, tb=tb, osl=osl, xb=xb: e.dma_start(
                    out=xb.t[:], in_=s_x1.t[tb * 128:(tb + 1) * 128, osl]),
                    reads=[s_x1], writes=[xb])
                for c in range(KH):
                    E.pe(lambda e, c=c, tb=tb, p=p: e.matmul(
                        p.t[:], bigA[:, c * T + tb * 128: c * T + tb * 128 + 128],
                        wg[:, c * 512:(c + 1) * 512],
                        start=(c == 0), stop=False),
                        reads=[bigA_b, wgh[0]],
                        war=[p] if c == 0 else ())
                E.pe(lambda e, osl=osl, p=p: e.matmul(
                    p.t[:], onesr[:], borow[0:1, osl],
                    start=False, stop=True),
                    reads=[onesr_b, borow_b], wend=[p])
                tbuf = t1_[tb % 2]
                E.dve(lambda e, p=p, xb=xb, tbuf=tbuf: e.tensor_tensor(
                    tbuf.t[:], p.t[:], xb.t[:], ALU.add),
                    reads=[p, xb], writes=[tbuf])
                E.dma("pool", lambda e, tb=tb, osl=osl, tbuf=tbuf: e.dma_start(
                    out=s_x2.t[tb * 128:(tb + 1) * 128, osl], in_=tbuf.t[:]),
                    reads=[tbuf], writes=[s_x2r[tb]])

        for _xb in xq_h:
            merge(_xb, wgh[0], wgh[1])
        for _xb in rnd_p:
            merge(_xb, wgh[0], wgh[1])
        merge(stg_p[0], stg_b)
        merge(stg_p[1], stg_b)
        E.dma("act", lambda e: e.dma_start(
            out=bigA[:, 0:KD * D].rearrange("p (c o) -> p c o", o=D),
            in_=s_wqT.t[:].rearrange("(c p) o -> p c o", p=128)),
            reads=[s_wqT], writes=[bigA_b] + bigA_c)
        ffn_phase("x2", False, out_b, None)

        final = [(n, E.cnt[n]) for n in E.cnt]
        E.barrier_wait("sp", final)
        E._flush()

    return nc


def kernel(L0_attn_x, L0_attn_Wl, L0_attn_bl, L0_attn_Wr, L0_attn_br,
           L0_attn_conv_w, L0_attn_conv_b, L0_attn_Wa, L0_attn_ba,
           L0_attn_Wi, L0_attn_bi, L0_attn_lam, L0_attn_Wo, L0_attn_bo,
           L0_ffn_weight):
    x = np.asarray(L0_attn_x)
    n = 8
    nc = build_nc()
    WfT = np.ascontiguousarray(np.asarray(L0_ffn_weight).T)
    cw = np.ascontiguousarray(np.asarray(L0_attn_conv_w).reshape(4, H))
    common = {
        "Wl": np.ascontiguousarray(L0_attn_Wl),
        "Wr": np.ascontiguousarray(L0_attn_Wr),
        "Wa": np.ascontiguousarray(L0_attn_Wa),
        "Wi": np.ascontiguousarray(L0_attn_Wi),
        "Wo": np.ascontiguousarray(L0_attn_Wo),
        "WfT": WfT,
        "vbl": np.ascontiguousarray(L0_attn_bl),
        "vbr": np.ascontiguousarray(L0_attn_br),
        "vba": np.ascontiguousarray(L0_attn_ba),
        "vbi": np.ascontiguousarray(L0_attn_bi),
        "vcb": np.ascontiguousarray(L0_attn_conv_b),
        "vlam": np.ascontiguousarray(L0_attn_lam),
        "vcw": cw,
        "vbo": np.ascontiguousarray(L0_attn_bo),
    }
    in_maps = []
    for b in range(n):
        m = dict(common)
        m["xN"] = np.ascontiguousarray(x[b])
        in_maps.append(m)
    res = run_bass_kernel_spmd(nc, in_maps, core_ids=list(range(n)))
    out = np.empty((n, T, D), np.float32)
    for b in range(n):
        out[b] = res.results[b]["out"]
    return out


# version 26
# speedup vs baseline: 1.0191x; 1.0191x over previous
import numpy as np
from contextlib import ExitStack

import concourse.bass as bass
import concourse.mybir as mybir
from concourse.bass_utils import run_bass_kernel_spmd

F32 = mybir.dt.float32
BF16 = mybir.dt.bfloat16
AF = mybir.ActivationFunctionType
ALU = mybir.AluOpType

T = 2048
D = 2048
H = 2560
KD = D // 128
KH = H // 128
TT = 512
NT = T // TT
C32 = float(np.float32(1.5 * 2.0 ** 23))
C64 = float(np.float32(1.5 * 2.0 ** 52))
X_AX = mybir.AxisListType.X


class Buf:
    __slots__ = ("t", "w", "rs")

    def __init__(self, t):
        self.t = t
        self.w = None
        self.rs = []


class Emit:

    ENGSEC = {"pe": "tensor", "dve": "vector", "act": "scalar",
              "pool": "gpsimd", "sp": "sync"}

    def __init__(self, nc, blk, sems):
        self.nc = nc
        self.blk = blk
        self.sems = sems
        self.cnt = {k: 0 for k in sems}
        self.waited = {e: {} for e in self.ENGSEC}
        self.pend = {e: [] for e in self.ENGSEC}
        self.qcnt = {}

    def _flush(self):
        for engname in self.ENGSEC:
            thunks = self.pend[engname]
            if not thunks:
                continue
            self.pend[engname] = []
            sec = getattr(self.blk, self.ENGSEC[engname])

            def body(eng, thunks=thunks):
                for th in thunks:
                    th(eng)
            sec(body)

    def _op(self, engname, fn, reads=(), writes=(), sem=None, inc=1,
            war=(), wend=()):
        sem = sem or engname
        toks = []
        for b in reads:
            if b.w:
                toks.append(b.w)
        for b in writes:
            if b.w:
                toks.append(b.w)
            toks.extend(b.rs)
        for b in war:
            if b.w:
                toks.append(b.w)
            toks.extend(b.rs)
        need = {}
        for s, v in toks:
            if v > need.get(s, 0):
                need[s] = v
        wmap = self.waited[engname]
        need = {s: v for s, v in need.items()
                if v > wmap.get(s, 0) and not (s == sem and False)}
        self.cnt[sem] += inc
        tok = (sem, self.cnt[sem])
        semh = self.sems[sem]
        semmap = self.sems

        def th(eng):
            for s, v in need.items():
                eng.wait_ge(semmap[s], v)
            instr = fn(eng)
            instr.then_inc(semh, inc)
        for s, v in need.items():
            if v > wmap.get(s, 0):
                wmap[s] = v
        self.pend[engname].append(th)
        for b in reads:
            b.rs.append(tok)
        for b in writes:
            b.w = tok
            b.rs = []
        for b in wend:
            b.w = tok
            b.rs = []
        return tok

    def pe(self, fn, reads=(), writes=(), war=(), wend=()):
        return self._op("pe", fn, reads, writes, war=war, wend=wend)

    def dve(self, fn, reads=(), writes=()):
        return self._op("dve", fn, reads, writes)

    def act(self, fn, reads=(), writes=()):
        return self._op("act", fn, reads, writes)

    def pool(self, fn, reads=(), writes=()):
        return self._op("pool", fn, reads, writes)

    def dma(self, q, fn, reads=(), writes=(), war=(), wend=()):
        j = self.qcnt.get(q, 0)
        self.qcnt[q] = j + 1
        semname = "d%s%d" % (q, j % 4)
        prev = self.cnt[semname]
        if prev:
            self.barrier_wait(q, [(semname, prev)])
        return self._op(q, fn, reads, writes, sem=semname, inc=16, war=war,
                        wend=wend)

    def barrier_wait(self, engname, toks):
        need = {}
        for s, v in toks:
            if v and v > need.get(s, 0):
                need[s] = v
        wmap = self.waited[engname]
        need = {s: v for s, v in need.items() if v > wmap.get(s, 0)}
        if not need:
            return
        semmap = self.sems

        def th(eng):
            for s, v in need.items():
                eng.wait_ge(semmap[s], v)
        for s, v in need.items():
            wmap[s] = v
        self.pend[engname].append(th)


def build_nc():
    nc = bass.Bass("TRN2", target_bir_lowering=False)
    dt = nc.dram_tensor
    xN = dt("xN", [T, D], F32, kind="ExternalInput").ap()
    Wl = dt("Wl", [D, H], F32, kind="ExternalInput").ap()
    Wr = dt("Wr", [D, H], F32, kind="ExternalInput").ap()
    Wa = dt("Wa", [H, H], F32, kind="ExternalInput").ap()
    Wi = dt("Wi", [H, H], F32, kind="ExternalInput").ap()
    Wo = dt("Wo", [H, D], F32, kind="ExternalInput").ap()
    WfT = dt("WfT", [D, D], F32, kind="ExternalInput").ap()
    vbl = dt("vbl", [H], F32, kind="ExternalInput").ap()
    vbr = dt("vbr", [H], F32, kind="ExternalInput").ap()
    vba = dt("vba", [H], F32, kind="ExternalInput").ap()
    vbi = dt("vbi", [H], F32, kind="ExternalInput").ap()
    vcb = dt("vcb", [H], F32, kind="ExternalInput").ap()
    vlam = dt("vlam", [H], F32, kind="ExternalInput").ap()
    vcw = dt("vcw", [4, H], F32, kind="ExternalInput").ap()
    vbo = dt("vbo", [D], F32, kind="ExternalInput").ap()
    OUT = dt("out", [T, D], F32, kind="ExternalOutput").ap()
    s_x1 = Buf(dt("s_x1", [T, D], F32, kind="ExternalOutput").ap())
    s_zn = Buf(dt("s_zn", [T, D], BF16, kind="ExternalOutput").ap())
    s_left = Buf(dt("s_left", [H, T], BF16, kind="ExternalOutput").ap())
    s_u = Buf(dt("s_u", [H, T], BF16, kind="ExternalOutput").ap())
    s_gat = Buf(dt("s_gat", [H, T], BF16, kind="ExternalOutput").ap())
    s_x2 = Buf(dt("s_x2", [T, D], F32, kind="ExternalOutput").ap())
    s_x2r = [Buf(s_x2.t) for _ in range(16)]
    s_wqT = Buf(dt("s_wqT", [D, D], BF16, kind="ExternalOutput").ap())
    out_b = Buf(OUT)

    with ExitStack() as ctx:
        E_ = ctx.enter_context
        sb = nc.sbuf_tensor
        bigA = E_(sb("bigA", [128, KH * T], BF16))
        bigB = E_(sb("bigB", [128, KD * TT], F32))
        wg = E_(sb("wg", [128, 16384], BF16))
        xq = wg[:, 0:KD * TT]
        stg = E_(sb("stg", [128, 4096], BF16))
        sc0 = E_(sb("sc0", [128, T + 8], F32))
        sc1 = E_(sb("sc1", [128, T + 8], F32))
        sc2 = E_(sb("sc2", [128, T], F32))
        sc3 = E_(sb("sc3", [128, T], F32))
        tiny4 = E_(sb("tiny4", [1, 4], F32))
        zero2 = E_(sb("zero2", [128, 2], F32))
        epsb = E_(sb("epsb", [128, 1], F32))
        t1a = E_(sb("t1a", [128, TT], F32))
        t1b = E_(sb("t1b", [128, TT], F32))
        xsa = E_(sb("xsa", [128, TT], F32))
        xsb = E_(sb("xsb", [128, TT], F32))
        borow = E_(sb("borow", [1, D], F32))
        stt8 = E_(sb("stt8", [128, 80], F32))
        vecs = E_(sb("vecs", [128, 256], F32))
        acc1 = E_(sb("acc1", [128, 1], F32))
        red1 = E_(sb("red1", [128, 1], F32))
        onesc = E_(sb("onesc", [128, 1], F32))
        onesr = E_(sb("onesr", [1, 128], F32))
        zero1 = E_(sb("zero1", [128, 1], F32))
        g0b = E_(sb("g0b", [128, 1], F32))
        ghb = E_(sb("ghb", [128, 1], F32))
        g1b = E_(sb("g1b", [128, 1], F32))
        h1b = E_(sb("h1b", [128, 1], F32))
        alr = E_(sb("alr", [1, 1], F32))
        ra128 = E_(sb("ra128", [128, 1], F32))
        ha128 = E_(sb("ha128", [128, 1], F32))
        identb = E_(sb("identb", [128, 128], BF16))
        pmm = [E_(nc.psum_tensor("pmm%d" % i, [128, 512], F32)) for i in range(4)]
        pstat = E_(nc.psum_tensor("pstat", [1, 512], F32))
        pbc = E_(nc.psum_tensor("pbc", [128, 512], F32))
        ptrb = [E_(nc.psum_tensor("ptrb%d" % i, [128, 128], BF16))
                for i in range(2)]
        semnames = (["pe", "dve", "act", "pool"] +
                    ["d%s%d" % (q, i) for q in ("sp", "act", "pool")
                     for i in range(4)])
        sems = {n: E_(nc.semaphore(n)) for n in semnames}
        blk = E_(nc.Block())
        E = Emit(nc, blk, sems)

        bigA_b, bigB_b, wg_b, xq_b = Buf(bigA), Buf(bigB), Buf(wg), Buf(wg)
        wgh = [Buf(wg), Buf(wg)]
        stg_b = Buf(stg)

        def merge(dst, *srcs):
            for sbuf_ in srcs:
                if sbuf_.w:
                    dst.rs.append(sbuf_.w)
                dst.rs.extend(sbuf_.rs)
        sc = [Buf(x) for x in (sc0, sc1, sc2, sc3)]
        wfb = [bigB_b, bigB_b]
        t1_ = [Buf(t1a), Buf(t1b)]
        xs_ = [Buf(xsa), Buf(xsb)]
        borow_b = Buf(borow)
        stt8_b = Buf(stt8)
        vecs_b = Buf(vecs)
        tiny_b = Buf(tiny4)
        zero2_b = Buf(zero2)
        acc1_b, red1_b = Buf(acc1), Buf(red1)
        onesc_b, onesr_b, zero1_b = Buf(onesc), Buf(onesr), Buf(zero1)
        bigBh = [Buf(bigB), Buf(bigB)]
        g0_b, gh_b, g1_b, h1_b = Buf(g0b), Buf(ghb), Buf(g1b), Buf(h1b)
        eps_b = Buf(epsb)
        alr_b, ra_b, ha_b = Buf(alr), Buf(ra128), Buf(ha128)
        idb_b = Buf(identb)
        pmm_b = [Buf(p) for p in pmm]
        pstat_b, pbc_b = Buf(pstat), Buf(pbc)
        ptrb_b = [Buf(p) for p in ptrb]

        VBL, VBR, VHBA, VHBI, VCB = 0, 20, 40, 60, 80
        VMS2, VMS1, VBO, VCW, VSP = 100, 120, 140, 156, 236

        def vcol(off, j):
            return vecs[:, off + j:off + j + 1]

        E.dma("sp", lambda e: e.dma_start(
            out=vecs[:, VBL:VBL + 20], in_=vbl.rearrange("(c p) -> p c", p=128),
            allow_slow_non_contiguous=True),
            writes=[vecs_b])
        E.dma("sp", lambda e: e.dma_start(
            out=vecs[:, VBR:VBR + 20], in_=vbr.rearrange("(c p) -> p c", p=128),
            allow_slow_non_contiguous=True),
            writes=[vecs_b])
        E.dma("sp", lambda e: e.dma_start(
            out=vecs[:, VHBA:VHBA + 20], in_=vba.rearrange("(c p) -> p c", p=128),
            allow_slow_non_contiguous=True),
            writes=[vecs_b])
        E.dma("sp", lambda e: e.dma_start(
            out=vecs[:, VHBI:VHBI + 20], in_=vbi.rearrange("(c p) -> p c", p=128),
            allow_slow_non_contiguous=True),
            writes=[vecs_b])
        E.dma("sp", lambda e: e.dma_start(
            out=vecs[:, VCB:VCB + 20], in_=vcb.rearrange("(c p) -> p c", p=128),
            allow_slow_non_contiguous=True),
            writes=[vecs_b])
        E.dma("sp", lambda e: e.dma_start(
            out=vecs[:, VMS1:VMS1 + 20], in_=vlam.rearrange("(c p) -> p c", p=128),
            allow_slow_non_contiguous=True),
            writes=[vecs_b])
        E.dma("sp", lambda e: e.dma_start(
            out=borow[:], in_=vbo.rearrange("(u d) -> u d", u=1)),
            writes=[borow_b])
        E.dma("sp", lambda e: e.dma_start(
            out=vecs[:, VCW:VCW + 80],
            in_=vcw.rearrange("k (c p) -> p (k c)", p=128),
            allow_slow_non_contiguous=True),
            writes=[vecs_b])
        E.act(lambda e: e.activation(out=vecs[:, VSP:VSP + 20],
                                     in_=vecs[:, VMS1:VMS1 + 20],
                                     func=AF.Exp, scale=-1.0),
              reads=[vecs_b], writes=[vecs_b])
        E.dve(lambda e: e.tensor_scalar(vecs[:, VSP:VSP + 20],
                                        vecs[:, VSP:VSP + 20], 1.0, None,
                                        ALU.add),
              reads=[vecs_b], writes=[vecs_b])
        E.act(lambda e: e.activation(out=vecs[:, VSP:VSP + 20],
                                     in_=vecs[:, VSP:VSP + 20], func=AF.Ln),
              reads=[vecs_b], writes=[vecs_b])
        E.dve(lambda e: e.tensor_scalar(vecs[:, VMS1:VMS1 + 20],
                                        vecs[:, VSP:VSP + 20], -8.0, None, ALU.mult),
              reads=[vecs_b], writes=[vecs_b])
        E.dve(lambda e: e.tensor_scalar(vecs[:, VMS2:VMS2 + 20],
                                        vecs[:, VSP:VSP + 20], -4.0, None, ALU.mult),
              reads=[vecs_b], writes=[vecs_b])
        E.dve(lambda e: e.tensor_scalar(vecs[:, VHBA:VHBA + 20],
                                        vecs[:, VHBA:VHBA + 20], 0.5, None, ALU.mult),
              reads=[vecs_b], writes=[vecs_b])
        E.dve(lambda e: e.tensor_scalar(vecs[:, VHBI:VHBI + 20],
                                        vecs[:, VHBI:VHBI + 20], 0.5, None, ALU.mult),
              reads=[vecs_b], writes=[vecs_b])
        E.dve(lambda e: e.memset(onesc[:], 1.0), writes=[onesc_b])
        E.dve(lambda e: e.memset(onesr[:], 1.0), writes=[onesr_b])
        E.dve(lambda e: e.memset(zero1[:], 0.0), writes=[zero1_b])
        E.dve(lambda e: e.memset(zero2[:], 0.0), writes=[zero2_b])
        E.dve(lambda e: e.memset(g0b[:], C32), writes=[g0_b])
        E.dve(lambda e: e.memset(epsb[:], 1e-5), writes=[eps_b])
        E.dve(lambda e: e.tensor_scalar(ghb[:], g0b[:], 0.5, None, ALU.add),
              reads=[g0_b], writes=[gh_b])
        E.dve(lambda e: e.tensor_tensor(g1b[:], ghb[:], g0b[:], ALU.subtract),
              reads=[gh_b, g0_b], writes=[g1_b])
        E.dve(lambda e: e.tensor_scalar(g1b[:], g1b[:], 2.0, None, ALU.mult),
              reads=[g1_b], writes=[g1_b])
        E.dve(lambda e: e.tensor_scalar(h1b[:], g1b[:], -1.0, 1.0,
                                        ALU.mult, ALU.add),
              reads=[g1_b], writes=[h1_b])
        E.pool(lambda e: e.memset(identb[:], 0.0), writes=[idb_b])
        E.pool(lambda e: e.affine_select(out=identb[:], in_=identb[:],
                                         compare_op=ALU.not_equal, fill=1.0,
                                         base=0, pattern=[[-1, 128]],
                                         channel_multiplier=1),
               reads=[idb_b], writes=[idb_b])

        for c in range(KD):
            b = bigBh[c % 2]
            hsl = slice((c % 2) * D, (c % 2) * D + D)
            E.dma("sp", lambda e, c=c, hsl=hsl: e.dma_start(
                out=bigB[:, hsl], in_=WfT[c * 128:(c + 1) * 128, :]), writes=[b])
            E.dve(lambda e, b=b, hsl=hsl: e.tensor_reduce(
                red1[:], bigB[:, hsl], axis=X_AX, op=ALU.add,
                apply_absolute_value=True), reads=[b], writes=[red1_b])
            if c == 0:
                E.dve(lambda e: e.tensor_copy(acc1[:], red1[:]),
                      reads=[red1_b], writes=[acc1_b])
            else:
                E.dve(lambda e: e.tensor_tensor(acc1[:], acc1[:], red1[:], ALU.add),
                      reads=[acc1_b, red1_b], writes=[acc1_b])
        E.pe(lambda e: e.matmul(pstat[0:1, 0:1], onesc[:], acc1[:],
                                start=True, stop=True),
             reads=[onesc_b, acc1_b], writes=[pstat_b])
        E.act(lambda e: e.activation(out=alr[:], in_=pstat[0:1, 0:1],
                                     func=AF.Copy, scale=1.0 / (D * D)),
              reads=[pstat_b], writes=[alr_b])
        E.dve(lambda e: e.tensor_scalar(alr[:], alr[:], 1e-8, None, ALU.max),
              reads=[alr_b], writes=[alr_b])
        E.act(lambda e: e.activation(out=tiny4[0:1, 0:1], in_=alr[:],
                                     func=AF.Copy, scale=0.5 / 127.0),
              reads=[alr_b], writes=[tiny_b])
        E.pe(lambda e: e.matmul(pbc[0:128, 0:1], onesr[:], tiny4[0:1, 0:1],
                                start=True, stop=True),
             reads=[tiny_b, onesr_b], writes=[pbc_b])
        E.act(lambda e: e.activation(out=ha128[:], in_=pbc[0:128, 0:1],
                                     func=AF.Copy), reads=[pbc_b], writes=[ha_b])
        E.dve(lambda e: e.reciprocal(tiny4[0:1, 1:2], alr[:]),
              reads=[alr_b], writes=[tiny_b])
        E.pe(lambda e: e.matmul(pbc[0:128, 0:1], onesr[:], tiny4[0:1, 1:2],
                                start=True, stop=True),
             reads=[onesr_b, tiny_b], writes=[pbc_b])
        E.act(lambda e: e.activation(out=ra128[:], in_=pbc[0:128, 0:1],
                                     func=AF.Copy), reads=[pbc_b], writes=[ra_b])

        def emit_quant(c):
            bq = bigBh[c % 2]
            hof = (c % 2) * D
            E.dma("sp", lambda e, c=c, hof=hof: e.dma_start(
                out=bigB[:, hof:hof + D], in_=WfT[c * 128:(c + 1) * 128, :]),
                writes=[bq])
            E.act(lambda e, bq=bq, hof=hof: e.activation(
                out=sc1[:, 0:D], in_=bigB[:, hof:hof + D], func=AF.Copy,
                scale=ra128[:]), reads=[bq, ra_b], writes=[sc[1]])
            E.dve(lambda e: e.tensor_scalar(sc2[:], sc1[:, 0:D], C32, -C32,
                                            ALU.add, ALU.add),
                  reads=[sc[1]], writes=[sc[2]])
            E.dve(lambda e: e.tensor_scalar(sc3[:], sc1[:, 0:D], C64, -C64,
                                            ALU.add, ALU.add),
                  reads=[sc[1]], writes=[sc[3]])
            E.dve(lambda e: e.scalar_tensor_tensor(sc3[:], sc2[:], h1b[:], sc3[:],
                                                   ALU.mult, ALU.add),
                  reads=[sc[2], h1_b, sc[3]], writes=[sc[3]])
            E.dve(lambda e, c=c: e.tensor_scalar(
                bigA[:, c * D:(c + 1) * D], sc3[:], 1.0, -1.0, ALU.min, ALU.max),
                reads=[sc[3]], writes=[bigA_c[c]])
            E.dma("act", lambda e, c=c: e.dma_start(
                out=s_wqT.t[c * 128:(c + 1) * 128, :],
                in_=bigA[:, c * D:(c + 1) * D]),
                reads=[bigA_c[c]], writes=[s_wqT])
        merge(bigB_b, bigBh[0], bigBh[1])

        TB = 256
        NTF = T // TB
        stt8_p = [stt8_b, Buf(stt8)]
        stg_p = [stg_b, Buf(stg)]
        bigB_h = [Buf(bigB), Buf(bigB)]
        xq_h = [Buf(wg), Buf(wg)]

        def sc8(ctx, j):
            return stt8[:, ctx * 32 + j * 2: ctx * 32 + j * 2 + 2]

        def sc8b(ctx, j, tb):
            return stt8[:, ctx * 32 + j * 2 + tb: ctx * 32 + j * 2 + tb + 1]

        rnd_p = [Buf(wg), Buf(wg)]
        bigA_c = [Buf(bigA) for _ in range(KD)]

        def ffn_parts(src_kind, with_ln2, dst_hbm, zn_out):

            def pre_mm(k):
                hp = (k + 1) % 2
                ctx = hp
                bB = bigB_h[hp]
                sb8 = stt8_p[ctx]
                hof = hp * 2 * D
                tsl = slice(k * TB, (k + 1) * TB)
                src = xN if src_kind == "x" else s_x2.t
                if src_kind == "x":
                    rd = []
                else:
                    rd = [s_x2r[2 * k], s_x2r[2 * k + 1]]
                bh = [Buf(bigB), Buf(bigB)]
                for tb in range(2):
                    rowsl = slice(k * TB + tb * 128, k * TB + (tb + 1) * 128)
                    E.dma("sp", lambda e, rowsl=rowsl, src=src, hof=hof, tb=tb:
                          e.dma_start(
                              out=bigB[:, hof + tb * D:hof + (tb + 1) * D],
                              in_=src[rowsl, :]),
                          reads=rd, war=[bB], wend=[bh[tb]])
                for tb in range(2):
                    po = tb
                    bsl = slice(hof + tb * D, hof + (tb + 1) * D)
                    E.dve(lambda e, bsl=bsl, ctx=ctx, tb=tb: e.reduce_max(
                        sc8b(ctx, 2, tb), bigB[:, bsl], axis=X_AX),
                        reads=[bh[tb]], writes=[sb8])
                    E.dve(lambda e, bsl=bsl, ctx=ctx, tb=tb: e.tensor_reduce(
                        sc8b(ctx, 3, tb), bigB[:, bsl], axis=X_AX, op=ALU.min),
                        reads=[bh[tb]], writes=[sb8])
                    E.act(lambda e, bsl=bsl, ctx=ctx, po=po, tb=tb:
                          e.activation(
                              out=stg[:, po * D:(po + 1) * D],
                              in_=bigB[:, bsl], func=AF.Square,
                              accum_out=sc8b(ctx, 1, tb)),
                          reads=[bh[tb]], writes=[stg_p[po], sb8])
                    E.act(lambda e, bsl=bsl, ctx=ctx, po=po, tb=tb:
                          e.activation(
                              out=stg[:, po * D:(po + 1) * D],
                              in_=bigB[:, bsl], func=AF.Copy,
                              accum_out=sc8b(ctx, 0, tb)),
                          reads=[bh[tb]], writes=[stg_p[po], sb8])
                merge(bB, bh[0], bh[1])
                E.dve(lambda e, ctx=ctx: e.tensor_scalar(
                    sc8(ctx, 4), sc8(ctx, 0), 1.0 / D, None, ALU.mult),
                    reads=[sb8], writes=[sb8])
                E.dve(lambda e, ctx=ctx: e.tensor_scalar(
                    sc8(ctx, 10), sc8(ctx, 1), 1.0 / D, None, ALU.mult),
                    reads=[sb8], writes=[sb8])
                E.dve(lambda e, ctx=ctx: e.tensor_tensor(
                    sc8(ctx, 11), sc8(ctx, 4), sc8(ctx, 4), ALU.mult),
                    reads=[sb8], writes=[sb8])
                E.dve(lambda e, ctx=ctx: e.tensor_tensor(
                    sc8(ctx, 11), sc8(ctx, 11), sc8(ctx, 10), ALU.subtract),
                    reads=[sb8], writes=[sb8])
                E.act(lambda e, ctx=ctx: e.activation(
                    out=sc8(ctx, 11), in_=sc8(ctx, 11), func=AF.Ln,
                    scale=-1.0, bias=epsb[:]),
                    reads=[sb8, eps_b], writes=[sb8])
                E.act(lambda e, ctx=ctx: e.activation(
                    out=sc8(ctx, 5), in_=sc8(ctx, 11), func=AF.Exp,
                    scale=-0.5), reads=[sb8], writes=[sb8])
                E.dve(lambda e, ctx=ctx: e.tensor_tensor(
                    sc8(ctx, 2), sc8(ctx, 2), sc8(ctx, 4), ALU.subtract),
                    reads=[sb8], writes=[sb8])
                E.dve(lambda e, ctx=ctx: e.tensor_tensor(
                    sc8(ctx, 3), sc8(ctx, 4), sc8(ctx, 3), ALU.subtract),
                    reads=[sb8], writes=[sb8])
                E.dve(lambda e, ctx=ctx: e.tensor_tensor(
                    sc8(ctx, 6), sc8(ctx, 2), sc8(ctx, 3), ALU.max),
                    reads=[sb8], writes=[sb8])
                E.dve(lambda e, ctx=ctx: e.tensor_scalar(
                    sc8(ctx, 6), sc8(ctx, 6), 1.0 / 127.0, 7.9e-33,
                    ALU.mult, ALU.max),
                    reads=[sb8], writes=[sb8])
                E.dve(lambda e, ctx=ctx: e.reciprocal(
                    sc8(ctx, 7), sc8(ctx, 6)), reads=[sb8], writes=[sb8])
                E.dve(lambda e, ctx=ctx: e.tensor_tensor(
                    sc8(ctx, 8), sc8(ctx, 4), sc8(ctx, 7), ALU.mult),
                    reads=[sb8], writes=[sb8])
                E.dve(lambda e, ctx=ctx: e.tensor_scalar(
                    sc8(ctx, 8), sc8(ctx, 8), -1.0, None, ALU.mult),
                    reads=[sb8], writes=[sb8])
                E.dve(lambda e, ctx=ctx: e.tensor_tensor(
                    sc8(ctx, 10), sc8(ctx, 5), sc8(ctx, 6), ALU.mult),
                    reads=[sb8], writes=[sb8])
                E.dve(lambda e, ctx=ctx: e.scalar_tensor_tensor(
                    sc8(ctx, 9), sc8(ctx, 10), ha128[:], zero2[:, 0:2],
                    ALU.mult, ALU.subtract),
                    reads=[sb8, ha_b, zero2_b], writes=[sb8])
                E.dve(lambda e, ctx=ctx: e.tensor_scalar(
                    sc8(ctx, 9), sc8(ctx, 9), 127.0, None, ALU.mult),
                    reads=[sb8], writes=[sb8])
                for tb in range(2):
                    po = tb
                    rG = rnd_p[po]
                    bsl = slice(hof + tb * D, hof + (tb + 1) * D)
                    rsl = slice(8192 + po * D, 8192 + (po + 1) * D)
                    zt = sc1 if po == 0 else sc2
                    zb = sc[1] if po == 0 else sc[2]
                    E.act(lambda e, bsl=bsl, ctx=ctx, tb=tb, zt=zt:
                          e.activation(
                              out=zt[:, 0:D], in_=bigB[:, bsl],
                              func=AF.Identity, scale=sc8b(ctx, 7, tb),
                              bias=sc8b(ctx, 8, tb)),
                          reads=[bB, sb8], writes=[zb])
                    ab = sc[3] if po == 0 else sc[0]
                    at = sc3 if po == 0 else sc0
                    E.dve(lambda e, zt=zt, at=at: e.tensor_scalar(
                        at[:, 0:D], zt[:, 0:D], C32, -C32, ALU.add, ALU.add),
                        reads=[zb], writes=[ab])
                    E.dve(lambda e, zt=zt: e.tensor_scalar(
                        zt[:, 0:D], zt[:, 0:D], C64, -C64, ALU.add, ALU.add),
                        reads=[zb], writes=[zb])
                    E.dve(lambda e, rsl=rsl, at=at, zt=zt:
                          e.scalar_tensor_tensor(
                              wg[:, rsl], at[:, 0:D], h1b[:], zt[:, 0:D],
                              ALU.mult, ALU.add),
                          reads=[ab, h1_b, zb], writes=[rG])

            def tr(k, jgrp):
                hp = (k + 1) % 2
                xQ = xq_h[hp]
                xof = hp * KD * TB
                base = jgrp * 4
                for u in range(base, base + 4):
                    tb, i = divmod(u, KD)
                    po = tb
                    rG = rnd_p[po]
                    rsl0 = 8192 + po * D
                    pb = ptrb_b[u % 2]
                    E.pe(lambda e, rsl0=rsl0, i=i, pb=pb: e.transpose(
                        pb.t[:], wg[:, rsl0 + i * 128: rsl0 + (i + 1) * 128],
                        identb[:]),
                        reads=[rG, idb_b], writes=[pb])
                    off = xof + i * TB + tb * 128
                    E.act(lambda e, off=off, pb=pb: e.activation(
                        out=wg[:, off:off + 128], in_=pb.t[:], func=AF.Copy),
                        reads=[pb], writes=[xQ])

            def mm(k, embed=None):
                hp = (k + 1) % 2
                ctx = hp
                sb8 = stt8_p[ctx]
                bB = bigB_h[hp]
                xQ = xq_h[hp]
                hof = hp * 2 * D
                xof = hp * KD * TB
                tsl = slice(k * TB, (k + 1) * TB)
                grp = 0
                for tb in range(2):
                    po = tb
                    for os in range(4):
                        p = pmm_b[(tb * 4 + os) % 4]
                        for i in range(KD):
                            E.pe(lambda e, i=i, tb=tb, os=os, p=p, xof=xof:
                                 e.matmul(
                                     p.t[:],
                                     wg[:, xof + i * TB + tb * 128:
                                         xof + i * TB + tb * 128 + 128],
                                     bigA[:, i * D + os * 512:
                                          i * D + (os + 1) * 512],
                                     start=(i == 0), stop=(i == KD - 1)),
                                 reads=[xQ, bigA_c[i]],
                                 war=[p] if i == 0 else (),
                                 wend=[p] if i == KD - 1 else ())
                        if embed is not None and grp >= 4:
                            embed((grp - 4) * 2)
                            embed((grp - 4) * 2 + 1)
                        grp += 1
                        xo = slice(hof + tb * D + os * 512,
                                   hof + tb * D + (os + 1) * 512)
                        if with_ln2:
                            E.dve(lambda e, p=p, ctx=ctx, tb=tb, os=os, xo=xo:
                                  e.scalar_tensor_tensor(
                                      bigB[:, xo], p.t[:], sc8b(ctx, 9, tb),
                                      bigB[:, xo], ALU.mult, ALU.add,
                                      accum_out=sc8b(ctx, 16 + os, tb)),
                                  reads=[p, sb8, bB], writes=[bB, sb8])
                            sqs = slice(tb * D + os * 512,
                                        tb * D + (os + 1) * 512)
                            E.act(lambda e, ctx=ctx, tb=tb, os=os, xo=xo,
                                  sqs=sqs: e.activation(
                                      out=stg[:, sqs], in_=bigB[:, xo],
                                      func=AF.Square,
                                      accum_out=sc8b(ctx, 12 + os, tb)),
                                  reads=[bB], writes=[stg_p[tb], sb8])
                        else:
                            E.dve(lambda e, p=p, ctx=ctx, tb=tb, xo=xo:
                                  e.scalar_tensor_tensor(
                                      bigB[:, xo], p.t[:], sc8b(ctx, 9, tb),
                                      bigB[:, xo], ALU.mult, ALU.add),
                                  reads=[p, sb8, bB], writes=[bB])
                E.dma("pool" if k < NTF - 1 else "sp",
                      lambda e, tsl=tsl, hof=hof: e.dma_start(
                          out=dst_hbm.t[tsl, :].rearrange(
                              "(b p) d -> p b d", p=128),
                          in_=bigB[:, hof:hof + 2 * D].rearrange(
                              "p (b d) -> p b d", d=D)),
                      reads=[bB], writes=[dst_hbm])
                if with_ln2:
                    E.dve(lambda e, ctx=ctx: e.tensor_tensor(
                        sc8(ctx, 16), sc8(ctx, 16), sc8(ctx, 17), ALU.add),
                        reads=[sb8], writes=[sb8])
                    E.dve(lambda e, ctx=ctx: e.tensor_tensor(
                        sc8(ctx, 18), sc8(ctx, 18), sc8(ctx, 19), ALU.add),
                        reads=[sb8], writes=[sb8])
                    E.dve(lambda e, ctx=ctx: e.tensor_tensor(
                        sc8(ctx, 0), sc8(ctx, 16), sc8(ctx, 18), ALU.add),
                        reads=[sb8], writes=[sb8])
                    E.dve(lambda e, ctx=ctx: e.tensor_tensor(
                        sc8(ctx, 12), sc8(ctx, 12), sc8(ctx, 13), ALU.add),
                        reads=[sb8], writes=[sb8])
                    E.dve(lambda e, ctx=ctx: e.tensor_tensor(
                        sc8(ctx, 14), sc8(ctx, 14), sc8(ctx, 15), ALU.add),
                        reads=[sb8], writes=[sb8])
                    E.dve(lambda e, ctx=ctx: e.tensor_tensor(
                        sc8(ctx, 1), sc8(ctx, 12), sc8(ctx, 14), ALU.add),
                        reads=[sb8], writes=[sb8])
                    E.dve(lambda e, ctx=ctx: e.tensor_scalar(
                        sc8(ctx, 4), sc8(ctx, 0), 1.0 / D, None, ALU.mult),
                        reads=[sb8], writes=[sb8])
                    E.dve(lambda e, ctx=ctx: e.tensor_scalar(
                        sc8(ctx, 10), sc8(ctx, 1), 1.0 / D, None, ALU.mult),
                        reads=[sb8], writes=[sb8])
                    E.dve(lambda e, ctx=ctx: e.tensor_tensor(
                        sc8(ctx, 11), sc8(ctx, 4), sc8(ctx, 4), ALU.mult),
                        reads=[sb8], writes=[sb8])
                    E.dve(lambda e, ctx=ctx: e.tensor_tensor(
                        sc8(ctx, 11), sc8(ctx, 11), sc8(ctx, 10), ALU.subtract),
                        reads=[sb8], writes=[sb8])
                    E.act(lambda e, ctx=ctx: e.activation(
                        out=sc8(ctx, 11), in_=sc8(ctx, 11), func=AF.Ln,
                        scale=-1.0, bias=epsb[:]),
                        reads=[sb8, eps_b], writes=[sb8])
                    E.act(lambda e, ctx=ctx: e.activation(
                        out=sc8(ctx, 5), in_=sc8(ctx, 11), func=AF.Exp,
                        scale=-0.5), reads=[sb8], writes=[sb8])
                    E.dve(lambda e, ctx=ctx: e.tensor_tensor(
                        sc8(ctx, 8), sc8(ctx, 4), sc8(ctx, 5), ALU.mult),
                        reads=[sb8], writes=[sb8])
                    E.dve(lambda e, ctx=ctx: e.tensor_scalar(
                        sc8(ctx, 8), sc8(ctx, 8), -1.0, None, ALU.mult),
                        reads=[sb8], writes=[sb8])
                    for tb in range(2):
                        po = tb
                        bsl = slice(hof + tb * D, hof + (tb + 1) * D)
                        ssl = slice(po * D, (po + 1) * D)
                        E.act(lambda e, bsl=bsl, ssl=ssl, ctx=ctx, tb=tb:
                              e.activation(
                                  out=stg[:, ssl], in_=bigB[:, bsl],
                                  func=AF.Identity, scale=sc8b(ctx, 5, tb),
                                  bias=sc8b(ctx, 8, tb)),
                              reads=[bB, sb8], writes=[stg_p[tb]])
                    E.dma("pool" if k < NTF - 1 else "sp",
                      lambda e, tsl=tsl: e.dma_start(
                          out=zn_out.t[tsl, :].rearrange(
                              "(b p) d -> p b d", p=128),
                          in_=stg[:, 0:2 * D].rearrange(
                              "p (b d) -> p b d", d=D)),
                      reads=[stg_p[0], stg_p[1]], writes=[zn_out])


            return pre_mm, tr, mm

        def ffn_phase(src_kind, with_ln2, dst_hbm, zn_out):
            pre_mm, tr, mm = ffn_parts(src_kind, with_ln2, dst_hbm, zn_out)
            pre_mm(0)
            for j in range(8):
                tr(0, j)
            pre_mm(1)
            for k in range(NTF):
                nxt = k + 1
                if nxt < NTF:
                    mm(k, embed=lambda j, nxt=nxt: tr(nxt, j))
                else:
                    mm(k)
                if k + 2 < NTF:
                    pre_mm(k + 2)

        pre1, tr1, mm1 = ffn_parts("x", True, s_x1, s_zn)
        for c in range(KD):
            emit_quant(c)
            if c == 9:
                pre1(0)
            if c == 12:
                for j in range(8):
                    tr1(0, j)
        merge(bigB_h[0], bigB_b, bigBh[0], bigBh[1])
        pre1(1)
        for k in range(NTF):
            nxt = k + 1
            if nxt < NTF:
                mm1(k, embed=lambda j, nxt=nxt: tr1(nxt, j))
            else:
                mm1(k)
            if k + 2 < NTF:
                pre1(k + 2)
        merge(stg_b, stg_p[1])

        merge(wgh[0], xq_b, *xq_h, *rnd_p)
        merge(wgh[1], xq_b, *xq_h, *rnd_p)
        merge(stg_b, stg_p[0], stg_p[1])
        merge(stg_p[0], stg_b)
        merge(stg_p[1], stg_b)
        for c in range(KD):
            E.dma("act" if c % 2 else "sp", lambda e, c=c: e.dma_start(
                out=bigA[:, c * T:(c + 1) * T],
                in_=s_zn.t[:, c * 128:(c + 1) * 128], transpose=True),
                reads=[s_zn], writes=[bigA_b], war=[bigA_c[c]])

        def wl_phase(W, evac, out_hbm, conv):
            ng = H // 512
            for g in range(ng):
                half = g % 2
                wslice = slice(half * 8192, half * 8192 + 8192)
                E.dma("pool", lambda e, g=g, wslice=wslice, W=W: e.dma_start(
                    out=wg[:, wslice].rearrange("p (c h) -> p c h", h=512),
                    in_=W[:, g * 512:(g + 1) * 512].rearrange(
                        "(c p) h -> p c h", p=128)),
                    writes=[wgh[half]])
                for hb in range(4):
                    blkidx = g * 4 + hb
                    for k in range(NT):
                        p = pmm_b[k % 4]
                        for i in range(KD):
                            off = half * 8192 + i * 512 + hb * 128
                            E.pe(lambda e, off=off, i=i, k=k, p=p: e.matmul(
                                p.t[:], wg[:, off:off + 128],
                                bigA[:, i * T + k * TT: i * T + (k + 1) * TT],
                                start=(i == 0), stop=(i == KD - 1)),
                                reads=[wgh[half], bigA_b],
                                war=[p] if i == 0 else (),
                                wend=[p] if i == KD - 1 else ())
                        evac(blkidx, k, p)
                    if conv is not None:
                        conv(blkidx)

        def left_evac(blkidx, k, p):
            st = stg[:, (k % 4) * TT:(k % 4) * TT + TT]
            E.act(lambda e, blkidx=blkidx, p=p, st=st: e.activation(
                out=st, in_=p.t[:], func=AF.Gelu_apprx_tanh,
                bias=vcol(VBL, blkidx)),
                reads=[p, vecs_b], writes=[stg_b])
            E.dma("sp", lambda e, blkidx=blkidx, k=k, st=st: e.dma_start(
                out=s_left.t[blkidx * 128:(blkidx + 1) * 128,
                              k * TT:(k + 1) * TT], in_=st),
                reads=[stg_b], writes=[s_left])

        wl_phase(Wl, left_evac, s_left, None)

        def rin_evac(blkidx, k, p):
            E.act(lambda e, blkidx=blkidx, k=k, p=p: e.activation(
                out=sc0[:, 3 + k * TT: 3 + (k + 1) * TT], in_=p.t[:],
                func=AF.Identity, bias=vcol(VBR, blkidx)),
                reads=[p, vecs_b], writes=[sc[0]])

        ng = H // 512
        for g in range(ng):
            half = g % 2
            wslice = slice(half * 8192, half * 8192 + 8192)
            E.dma("pool", lambda e, g=g, wslice=wslice: e.dma_start(
                out=wg[:, wslice].rearrange("p (c h) -> p c h", h=512),
                in_=Wr[:, g * 512:(g + 1) * 512].rearrange(
                    "(c p) h -> p c h", p=128)),
                writes=[wgh[half]])
            for hb in range(4):
                blkidx = g * 4 + hb
                bp = blkidx % 2
                rinT = sc0 if bp == 0 else sc1
                rinB = sc[0] if bp == 0 else sc[1]
                uacc = sc2 if bp == 0 else sc3
                uaccB = sc[2] if bp == 0 else sc[3]
                E.dve(lambda e, rinT=rinT: e.memset(rinT[:, 0:3], 0.0),
                      writes=[rinB])
                for k in range(NT):
                    p = pmm_b[k % 4]
                    for i in range(KD):
                        off = half * 8192 + i * 512 + hb * 128
                        E.pe(lambda e, off=off, i=i, k=k, p=p: e.matmul(
                            p.t[:], wg[:, off:off + 128],
                            bigA[:, i * T + k * TT: i * T + (k + 1) * TT],
                            start=(i == 0), stop=(i == KD - 1)),
                            reads=[wgh[half], bigA_b],
                            war=[p] if i == 0 else (),
                            wend=[p] if i == KD - 1 else ())
                    E.act(lambda e, blkidx=blkidx, k=k, p=p, rinT=rinT:
                          e.activation(
                              out=rinT[:, 3 + k * TT: 3 + (k + 1) * TT],
                              in_=p.t[:], func=AF.Identity,
                              bias=vcol(VBR, blkidx)),
                          reads=[p, vecs_b], writes=[rinB])
                E.act(lambda e, blkidx=blkidx, rinT=rinT, uacc=uacc:
                      e.activation(
                          out=uacc[:, 0:T], in_=rinT[:, 0:T], func=AF.Identity,
                          scale=vcol(VCW, blkidx), bias=vcol(VCB, blkidx)),
                      reads=[rinB, vecs_b], writes=[uaccB])
                for kk in (1, 2, 3):
                    E.dve(lambda e, kk=kk, blkidx=blkidx, rinT=rinT, uacc=uacc:
                          e.scalar_tensor_tensor(
                              uacc[:, 0:T], rinT[:, kk:kk + T],
                              vcol(VCW, kk * 20 + blkidx),
                              uacc[:, 0:T], ALU.mult, ALU.add),
                          reads=[rinB, vecs_b, uaccB], writes=[uaccB])
                stu = stg[:, bp * T:bp * T + T]
                E.dve(lambda e, stu=stu, uacc=uacc: e.tensor_copy(
                    stu, uacc[:, 0:T]),
                    reads=[uaccB], writes=[stg_p[bp]])
                E.dma("sp", lambda e, blkidx=blkidx, stu=stu: e.dma_start(
                    out=s_u.t[blkidx * 128:(blkidx + 1) * 128, :], in_=stu),
                    reads=[stg_p[bp]], writes=[s_u])

        merge(stg_p[0], stg_b, stg_p[1])
        merge(stg_p[1], stg_b, stg_p[0])
        merge(rnd_p[0], wgh[0], wgh[1])
        E.dma("act", lambda e: e.dma_start(
            out=bigA[:, 0:KH * T].rearrange("p (c t) -> p c t", t=T),
            in_=s_u.t[:].rearrange("(c p) t -> p c t", p=128)),
            reads=[s_u], writes=[bigA_b])
        for b in range(KH):
            half = b % 2
            wa_sl = slice(half * 5120, half * 5120 + 2560)
            wi_sl = slice(half * 5120 + 2560, half * 5120 + 5120)
            E.dma("pool", lambda e, b=b, wa_sl=wa_sl: e.dma_start(
                out=wg[:, wa_sl].rearrange("p (c h) -> p c h", h=128),
                in_=Wa[:, b * 128:(b + 1) * 128].rearrange(
                    "(c p) h -> p c h", p=128)), writes=[wgh[half]])
            E.dma("pool", lambda e, b=b, wi_sl=wi_sl: e.dma_start(
                out=wg[:, wi_sl].rearrange("p (c h) -> p c h", h=128),
                in_=Wi[:, b * 128:(b + 1) * 128].rearrange(
                    "(c p) h -> p c h", p=128)), writes=[wgh[half]])
            stl = stg[:, (b % 2) * T:(b % 2) * T + T]
            stlb = stg_p[b % 2]
            E.dma("sp", lambda e, b=b, stl=stl: e.dma_start(
                out=stl, in_=s_left.t[b * 128:(b + 1) * 128, :]),
                reads=[s_left], writes=[stlb])
            for k in range(NT):
                pr = pmm_b[(2 * k) % 4]
                pi = pmm_b[(2 * k + 1) % 4]
                for c in range(KH):
                    off = half * 5120 + c * 128
                    E.pe(lambda e, off=off, c=c, pr=pr, k=k: e.matmul(
                        pr.t[:], wg[:, off:off + 128],
                        bigA[:, c * T + k * TT: c * T + (k + 1) * TT],
                        start=(c == 0), stop=(c == KH - 1)),
                        reads=[wgh[half], bigA_b],
                        war=[pr] if c == 0 else (),
                        wend=[pr] if c == KH - 1 else ())
                for c in range(KH):
                    off = half * 5120 + 2560 + c * 128
                    E.pe(lambda e, off=off, c=c, pi=pi, k=k: e.matmul(
                        pi.t[:], wg[:, off:off + 128],
                        bigA[:, c * T + k * TT: c * T + (k + 1) * TT],
                        start=(c == 0), stop=(c == KH - 1)),
                        reads=[wgh[half], bigA_b],
                        war=[pi] if c == 0 else (),
                        wend=[pi] if c == KH - 1 else ())
                ksl = slice(k * TT, (k + 1) * TT)
                E.act(lambda e, b=b, pr=pr, ksl=ksl: e.activation(
                    out=sc0[:, ksl], in_=pr.t[:], func=AF.Tanh, scale=0.5,
                    bias=vcol(VHBA, b)), reads=[pr, vecs_b], writes=[sc[0]])
                E.act(lambda e, b=b, pi=pi, ksl=ksl: e.activation(
                    out=sc1[:, ksl], in_=pi.t[:], func=AF.Tanh, scale=0.5,
                    bias=vcol(VHBI, b)), reads=[pi, vecs_b], writes=[sc[1]])
            E.act(lambda e, b=b: e.activation(
                out=sc2[:], in_=sc0[:, 0:T], func=AF.Exp,
                scale=vcol(VMS2, b), bias=vcol(VMS2, b)),
                reads=[sc[0], vecs_b], writes=[sc[2]])
            E.act(lambda e, b=b: e.activation(
                out=sc3[:], in_=sc0[:, 0:T], func=AF.Exp,
                scale=vcol(VMS1, b), bias=vcol(VMS1, b)),
                reads=[sc[0], vecs_b], writes=[sc[3]])
            E.dve(lambda e: e.tensor_scalar(sc3[:], sc3[:], -1.0, 1.0,
                                            ALU.mult, ALU.add),
                  reads=[sc[3]], writes=[sc[3]])
            E.dve(lambda e: e.tensor_scalar(sc3[:], sc3[:], 1e-12, None,
                                            ALU.max),
                  reads=[sc[3]], writes=[sc[3]])
            E.act(lambda e: e.activation(out=sc0[:, 0:T], in_=sc3[:],
                                         func=AF.Sqrt, scale=0.25),
                  reads=[sc[3]], writes=[sc[0]])
            E.dve(lambda e: e.tensor_scalar(sc1[:, 0:T], sc1[:, 0:T], 1.0,
                                            None, ALU.add),
                  reads=[sc[1]], writes=[sc[1]])
            E.dve(lambda e, b=b: e.tensor_tensor(
                sc1[:, 0:T], sc1[:, 0:T], bigA[:, b * T:(b + 1) * T],
                ALU.mult),
                reads=[sc[1], bigA_b], writes=[sc[1]])
            E.dve(lambda e: e.tensor_tensor(sc1[:, 0:T], sc1[:, 0:T],
                                            sc0[:, 0:T], ALU.mult),
                  reads=[sc[1], sc[0]], writes=[sc[1]])
            E.dve(lambda e: e.tensor_tensor_scan(sc3[:], sc2[:], sc1[:, 0:T],
                                                 zero1[:], ALU.mult, ALU.add),
                  reads=[sc[2], sc[1], zero1_b], writes=[sc[3]])
            stgo = wg[:, 10240:10240 + T]
            E.dve(lambda e, stl=stl, stgo=stgo, stlb=stlb: e.tensor_tensor(
                stgo, sc3[:], stl, ALU.mult),
                reads=[sc[3], stlb], writes=[rnd_p[0]])
            E.dma("sp", lambda e, b=b, stgo=stgo: e.dma_start(
                out=s_gat.t[b * 128:(b + 1) * 128, :], in_=stgo),
                reads=[rnd_p[0]], writes=[s_gat])

        E.dma("act", lambda e: e.dma_start(
            out=bigA[:, 0:KH * T].rearrange("p (c t) -> p c t", t=T),
            in_=s_gat.t[:].rearrange("(c p) t -> p c t", p=128)),
            reads=[s_gat], writes=[bigA_b])
        for os in range(4):
            osl = slice(os * 512, (os + 1) * 512)
            E.dma("pool", lambda e, osl=osl: e.dma_start(
                out=wg[:, 0:KH * 512].rearrange("p (c h) -> p c h", h=512),
                in_=Wo[:, osl].rearrange("(c p) h -> p c h", p=128)),
                writes=[wgh[0], wgh[1]], war=[rnd_p[0]])
            for tb in range(16):
                p = pmm_b[tb % 4]
                xb = xs_[tb % 2]
                E.dma("sp", lambda e, tb=tb, osl=osl, xb=xb: e.dma_start(
                    out=xb.t[:], in_=s_x1.t[tb * 128:(tb + 1) * 128, osl]),
                    reads=[s_x1], writes=[xb])
                for c in range(KH):
                    E.pe(lambda e, c=c, tb=tb, p=p: e.matmul(
                        p.t[:], bigA[:, c * T + tb * 128: c * T + tb * 128 + 128],
                        wg[:, c * 512:(c + 1) * 512],
                        start=(c == 0), stop=False),
                        reads=[bigA_b, wgh[0]],
                        war=[p] if c == 0 else ())
                E.pe(lambda e, osl=osl, p=p: e.matmul(
                    p.t[:], onesr[:], borow[0:1, osl],
                    start=False, stop=True),
                    reads=[onesr_b, borow_b], wend=[p])
                tbuf = t1_[tb % 2]
                E.dve(lambda e, p=p, xb=xb, tbuf=tbuf: e.tensor_tensor(
                    tbuf.t[:], p.t[:], xb.t[:], ALU.add),
                    reads=[p, xb], writes=[tbuf])
                E.dma("pool", lambda e, tb=tb, osl=osl, tbuf=tbuf: e.dma_start(
                    out=s_x2.t[tb * 128:(tb + 1) * 128, osl], in_=tbuf.t[:]),
                    reads=[tbuf], writes=[s_x2r[tb]])

        for _xb in xq_h:
            merge(_xb, wgh[0], wgh[1])
        for _xb in rnd_p:
            merge(_xb, wgh[0], wgh[1])
        merge(stg_p[0], stg_b)
        merge(stg_p[1], stg_b)
        E.dma("act", lambda e: e.dma_start(
            out=bigA[:, 0:KD * D].rearrange("p (c o) -> p c o", o=D),
            in_=s_wqT.t[:].rearrange("(c p) o -> p c o", p=128)),
            reads=[s_wqT], writes=[bigA_b] + bigA_c)
        ffn_phase("x2", False, out_b, None)

        final = [(n, E.cnt[n]) for n in E.cnt]
        E.barrier_wait("sp", final)
        E._flush()

    return nc


def kernel(L0_attn_x, L0_attn_Wl, L0_attn_bl, L0_attn_Wr, L0_attn_br,
           L0_attn_conv_w, L0_attn_conv_b, L0_attn_Wa, L0_attn_ba,
           L0_attn_Wi, L0_attn_bi, L0_attn_lam, L0_attn_Wo, L0_attn_bo,
           L0_ffn_weight):
    x = np.asarray(L0_attn_x)
    n = 8
    nc = build_nc()
    WfT = np.ascontiguousarray(np.asarray(L0_ffn_weight).T)
    cw = np.ascontiguousarray(np.asarray(L0_attn_conv_w).reshape(4, H))
    common = {
        "Wl": np.ascontiguousarray(L0_attn_Wl),
        "Wr": np.ascontiguousarray(L0_attn_Wr),
        "Wa": np.ascontiguousarray(L0_attn_Wa),
        "Wi": np.ascontiguousarray(L0_attn_Wi),
        "Wo": np.ascontiguousarray(L0_attn_Wo),
        "WfT": WfT,
        "vbl": np.ascontiguousarray(L0_attn_bl),
        "vbr": np.ascontiguousarray(L0_attn_br),
        "vba": np.ascontiguousarray(L0_attn_ba),
        "vbi": np.ascontiguousarray(L0_attn_bi),
        "vcb": np.ascontiguousarray(L0_attn_conv_b),
        "vlam": np.ascontiguousarray(L0_attn_lam),
        "vcw": cw,
        "vbo": np.ascontiguousarray(L0_attn_bo),
    }
    in_maps = []
    for b in range(n):
        m = dict(common)
        m["xN"] = np.ascontiguousarray(x[b])
        in_maps.append(m)
    res = run_bass_kernel_spmd(nc, in_maps, core_ids=list(range(n)))
    out = np.empty((n, T, D), np.float32)
    for b in range(n):
        out[b] = res.results[b]["out"]
    return out


# version 27
# speedup vs baseline: 1.0295x; 1.0102x over previous
import numpy as np
from contextlib import ExitStack

import concourse.bass as bass
import concourse.mybir as mybir
from concourse.bass_utils import run_bass_kernel_spmd

F32 = mybir.dt.float32
BF16 = mybir.dt.bfloat16
AF = mybir.ActivationFunctionType
ALU = mybir.AluOpType

T = 2048
D = 2048
H = 2560
KD = D // 128
KH = H // 128
TT = 512
NT = T // TT
C32 = float(np.float32(1.5 * 2.0 ** 23))
C64 = float(np.float32(1.5 * 2.0 ** 52))
X_AX = mybir.AxisListType.X


class Buf:
    __slots__ = ("t", "w", "rs")

    def __init__(self, t):
        self.t = t
        self.w = None
        self.rs = []


class Emit:

    ENGSEC = {"pe": "tensor", "dve": "vector", "act": "scalar",
              "pool": "gpsimd", "sp": "sync"}

    def __init__(self, nc, blk, sems):
        self.nc = nc
        self.blk = blk
        self.sems = sems
        self.cnt = {k: 0 for k in sems}
        self.waited = {e: {} for e in self.ENGSEC}
        self.pend = {e: [] for e in self.ENGSEC}
        self.qcnt = {}

    def _flush(self):
        for engname in self.ENGSEC:
            thunks = self.pend[engname]
            if not thunks:
                continue
            self.pend[engname] = []
            sec = getattr(self.blk, self.ENGSEC[engname])

            def body(eng, thunks=thunks):
                for th in thunks:
                    th(eng)
            sec(body)

    def _op(self, engname, fn, reads=(), writes=(), sem=None, inc=1,
            war=(), wend=()):
        sem = sem or engname
        toks = []
        for b in reads:
            if b.w:
                toks.append(b.w)
        for b in writes:
            if b.w:
                toks.append(b.w)
            toks.extend(b.rs)
        for b in war:
            if b.w:
                toks.append(b.w)
            toks.extend(b.rs)
        need = {}
        for s, v in toks:
            if v > need.get(s, 0):
                need[s] = v
        wmap = self.waited[engname]
        need = {s: v for s, v in need.items()
                if v > wmap.get(s, 0) and not (s == sem and False)}
        self.cnt[sem] += inc
        tok = (sem, self.cnt[sem])
        semh = self.sems[sem]
        semmap = self.sems

        def th(eng):
            for s, v in need.items():
                eng.wait_ge(semmap[s], v)
            instr = fn(eng)
            instr.then_inc(semh, inc)
        for s, v in need.items():
            if v > wmap.get(s, 0):
                wmap[s] = v
        self.pend[engname].append(th)
        for b in reads:
            b.rs.append(tok)
        for b in writes:
            b.w = tok
            b.rs = []
        for b in wend:
            b.w = tok
            b.rs = []
        return tok

    def pe(self, fn, reads=(), writes=(), war=(), wend=()):
        return self._op("pe", fn, reads, writes, war=war, wend=wend)

    def dve(self, fn, reads=(), writes=()):
        return self._op("dve", fn, reads, writes)

    def act(self, fn, reads=(), writes=()):
        return self._op("act", fn, reads, writes)

    def pool(self, fn, reads=(), writes=()):
        return self._op("pool", fn, reads, writes)

    def dma(self, q, fn, reads=(), writes=(), war=(), wend=()):
        j = self.qcnt.get(q, 0)
        self.qcnt[q] = j + 1
        semname = "d%s%d" % (q, j % 4)
        prev = self.cnt[semname]
        if prev:
            self.barrier_wait(q, [(semname, prev)])
        return self._op(q, fn, reads, writes, sem=semname, inc=16, war=war,
                        wend=wend)

    def barrier_wait(self, engname, toks):
        need = {}
        for s, v in toks:
            if v and v > need.get(s, 0):
                need[s] = v
        wmap = self.waited[engname]
        need = {s: v for s, v in need.items() if v > wmap.get(s, 0)}
        if not need:
            return
        semmap = self.sems

        def th(eng):
            for s, v in need.items():
                eng.wait_ge(semmap[s], v)
        for s, v in need.items():
            wmap[s] = v
        self.pend[engname].append(th)


def build_nc():
    nc = bass.Bass("TRN2", target_bir_lowering=False)
    dt = nc.dram_tensor
    xN = dt("xN", [T, D], F32, kind="ExternalInput").ap()
    Wl = dt("Wl", [D, H], F32, kind="ExternalInput").ap()
    Wr = dt("Wr", [D, H], F32, kind="ExternalInput").ap()
    Wa = dt("Wa", [H, H], F32, kind="ExternalInput").ap()
    Wi = dt("Wi", [H, H], F32, kind="ExternalInput").ap()
    Wo = dt("Wo", [H, D], F32, kind="ExternalInput").ap()
    WfT = dt("WfT", [D, D], F32, kind="ExternalInput").ap()
    vbl = dt("vbl", [H], F32, kind="ExternalInput").ap()
    vbr = dt("vbr", [H], F32, kind="ExternalInput").ap()
    vba = dt("vba", [H], F32, kind="ExternalInput").ap()
    vbi = dt("vbi", [H], F32, kind="ExternalInput").ap()
    vcb = dt("vcb", [H], F32, kind="ExternalInput").ap()
    vlam = dt("vlam", [H], F32, kind="ExternalInput").ap()
    vcw = dt("vcw", [4, H], F32, kind="ExternalInput").ap()
    vbo = dt("vbo", [D], F32, kind="ExternalInput").ap()
    OUT = dt("out", [T, D], F32, kind="ExternalOutput").ap()
    s_x1 = Buf(dt("s_x1", [T, D], F32, kind="ExternalOutput").ap())
    s_zn = Buf(dt("s_zn", [T, D], BF16, kind="ExternalOutput").ap())
    s_left = Buf(dt("s_left", [H, T], BF16, kind="ExternalOutput").ap())
    s_u = Buf(dt("s_u", [H, T], BF16, kind="ExternalOutput").ap())
    s_gat = Buf(dt("s_gat", [H, T], BF16, kind="ExternalOutput").ap())
    s_x2 = Buf(dt("s_x2", [T, D], F32, kind="ExternalOutput").ap())
    s_x2r = [Buf(s_x2.t) for _ in range(16)]
    s_wqT = Buf(dt("s_wqT", [D, D], BF16, kind="ExternalOutput").ap())
    out_b = Buf(OUT)

    with ExitStack() as ctx:
        E_ = ctx.enter_context
        sb = nc.sbuf_tensor
        bigA = E_(sb("bigA", [128, KH * T], BF16))
        bigB = E_(sb("bigB", [128, KD * TT], F32))
        wg = E_(sb("wg", [128, 16384], BF16))
        xq = wg[:, 0:KD * TT]
        stg = E_(sb("stg", [128, 4096], BF16))
        sc0 = E_(sb("sc0", [128, T + 8], F32))
        sc1 = E_(sb("sc1", [128, T + 8], F32))
        sc2 = E_(sb("sc2", [128, T], F32))
        sc3 = E_(sb("sc3", [128, T], F32))
        tiny4 = E_(sb("tiny4", [1, 4], F32))
        zero2 = E_(sb("zero2", [128, 2], F32))
        epsb = E_(sb("epsb", [128, 1], F32))
        t1a = E_(sb("t1a", [128, TT], F32))
        t1b = E_(sb("t1b", [128, TT], F32))
        xsa = E_(sb("xsa", [128, TT], F32))
        xsb = E_(sb("xsb", [128, TT], F32))
        borow = E_(sb("borow", [1, D], F32))
        stt8 = E_(sb("stt8", [128, 80], F32))
        vecs = E_(sb("vecs", [128, 256], F32))
        acc1 = E_(sb("acc1", [128, 1], F32))
        red1 = E_(sb("red1", [128, 1], F32))
        onesc = E_(sb("onesc", [128, 1], F32))
        onesr = E_(sb("onesr", [1, 128], F32))
        zero1 = E_(sb("zero1", [128, 1], F32))
        g0b = E_(sb("g0b", [128, 1], F32))
        ghb = E_(sb("ghb", [128, 1], F32))
        g1b = E_(sb("g1b", [128, 1], F32))
        h1b = E_(sb("h1b", [128, 1], F32))
        alr = E_(sb("alr", [1, 1], F32))
        ra128 = E_(sb("ra128", [128, 1], F32))
        ha128 = E_(sb("ha128", [128, 1], F32))
        identb = E_(sb("identb", [128, 128], BF16))
        pmm = [E_(nc.psum_tensor("pmm%d" % i, [128, 512], F32)) for i in range(4)]
        pstat = E_(nc.psum_tensor("pstat", [1, 512], F32))
        pbc = E_(nc.psum_tensor("pbc", [128, 512], F32))
        ptrb = [E_(nc.psum_tensor("ptrb%d" % i, [128, 128], BF16))
                for i in range(2)]
        semnames = (["pe", "dve", "act", "pool"] +
                    ["d%s%d" % (q, i) for q in ("sp", "act", "pool")
                     for i in range(4)])
        sems = {n: E_(nc.semaphore(n)) for n in semnames}
        blk = E_(nc.Block())
        E = Emit(nc, blk, sems)

        bigA_b, bigB_b, wg_b, xq_b = Buf(bigA), Buf(bigB), Buf(wg), Buf(wg)
        wgh = [Buf(wg), Buf(wg)]
        stg_b = Buf(stg)

        def merge(dst, *srcs):
            for sbuf_ in srcs:
                if sbuf_.w:
                    dst.rs.append(sbuf_.w)
                dst.rs.extend(sbuf_.rs)
        sc = [Buf(x) for x in (sc0, sc1, sc2, sc3)]
        wfb = [bigB_b, bigB_b]
        t1_ = [Buf(t1a), Buf(t1b)]
        xs_ = [Buf(xsa), Buf(xsb)]
        borow_b = Buf(borow)
        stt8_b = Buf(stt8)
        vecs_b = Buf(vecs)
        tiny_b = Buf(tiny4)
        zero2_b = Buf(zero2)
        acc1_b, red1_b = Buf(acc1), Buf(red1)
        onesc_b, onesr_b, zero1_b = Buf(onesc), Buf(onesr), Buf(zero1)
        bigBh = [Buf(bigB), Buf(bigB)]
        g0_b, gh_b, g1_b, h1_b = Buf(g0b), Buf(ghb), Buf(g1b), Buf(h1b)
        eps_b = Buf(epsb)
        alr_b, ra_b, ha_b = Buf(alr), Buf(ra128), Buf(ha128)
        idb_b = Buf(identb)
        pmm_b = [Buf(p) for p in pmm]
        pstat_b, pbc_b = Buf(pstat), Buf(pbc)
        ptrb_b = [Buf(p) for p in ptrb]

        VBL, VBR, VHBA, VHBI, VCB = 0, 20, 40, 60, 80
        VMS2, VMS1, VBO, VCW, VSP = 100, 120, 140, 156, 236

        def vcol(off, j):
            return vecs[:, off + j:off + j + 1]

        E.dma("sp", lambda e: e.dma_start(
            out=vecs[:, VBL:VBL + 20], in_=vbl.rearrange("(c p) -> p c", p=128),
            allow_slow_non_contiguous=True),
            writes=[vecs_b])
        E.dma("sp", lambda e: e.dma_start(
            out=vecs[:, VBR:VBR + 20], in_=vbr.rearrange("(c p) -> p c", p=128),
            allow_slow_non_contiguous=True),
            writes=[vecs_b])
        E.dma("sp", lambda e: e.dma_start(
            out=vecs[:, VHBA:VHBA + 20], in_=vba.rearrange("(c p) -> p c", p=128),
            allow_slow_non_contiguous=True),
            writes=[vecs_b])
        E.dma("sp", lambda e: e.dma_start(
            out=vecs[:, VHBI:VHBI + 20], in_=vbi.rearrange("(c p) -> p c", p=128),
            allow_slow_non_contiguous=True),
            writes=[vecs_b])
        E.dma("sp", lambda e: e.dma_start(
            out=vecs[:, VCB:VCB + 20], in_=vcb.rearrange("(c p) -> p c", p=128),
            allow_slow_non_contiguous=True),
            writes=[vecs_b])
        E.dma("sp", lambda e: e.dma_start(
            out=vecs[:, VMS1:VMS1 + 20], in_=vlam.rearrange("(c p) -> p c", p=128),
            allow_slow_non_contiguous=True),
            writes=[vecs_b])
        E.dma("sp", lambda e: e.dma_start(
            out=borow[:], in_=vbo.rearrange("(u d) -> u d", u=1)),
            writes=[borow_b])
        E.dma("sp", lambda e: e.dma_start(
            out=vecs[:, VCW:VCW + 80],
            in_=vcw.rearrange("k (c p) -> p (k c)", p=128),
            allow_slow_non_contiguous=True),
            writes=[vecs_b])
        E.act(lambda e: e.activation(out=vecs[:, VSP:VSP + 20],
                                     in_=vecs[:, VMS1:VMS1 + 20],
                                     func=AF.Exp, scale=-1.0),
              reads=[vecs_b], writes=[vecs_b])
        E.dve(lambda e: e.tensor_scalar(vecs[:, VSP:VSP + 20],
                                        vecs[:, VSP:VSP + 20], 1.0, None,
                                        ALU.add),
              reads=[vecs_b], writes=[vecs_b])
        E.act(lambda e: e.activation(out=vecs[:, VSP:VSP + 20],
                                     in_=vecs[:, VSP:VSP + 20], func=AF.Ln),
              reads=[vecs_b], writes=[vecs_b])
        E.dve(lambda e: e.tensor_scalar(vecs[:, VMS1:VMS1 + 20],
                                        vecs[:, VSP:VSP + 20], -8.0, None, ALU.mult),
              reads=[vecs_b], writes=[vecs_b])
        E.dve(lambda e: e.tensor_scalar(vecs[:, VMS2:VMS2 + 20],
                                        vecs[:, VSP:VSP + 20], -4.0, None, ALU.mult),
              reads=[vecs_b], writes=[vecs_b])
        E.dve(lambda e: e.tensor_scalar(vecs[:, VHBA:VHBA + 20],
                                        vecs[:, VHBA:VHBA + 20], 0.5, None, ALU.mult),
              reads=[vecs_b], writes=[vecs_b])
        E.dve(lambda e: e.tensor_scalar(vecs[:, VHBI:VHBI + 20],
                                        vecs[:, VHBI:VHBI + 20], 0.5, None, ALU.mult),
              reads=[vecs_b], writes=[vecs_b])
        E.dve(lambda e: e.memset(onesc[:], 1.0), writes=[onesc_b])
        E.dve(lambda e: e.memset(onesr[:], 1.0), writes=[onesr_b])
        E.dve(lambda e: e.memset(zero1[:], 0.0), writes=[zero1_b])
        E.dve(lambda e: e.memset(zero2[:], 0.0), writes=[zero2_b])
        E.dve(lambda e: e.memset(g0b[:], C32), writes=[g0_b])
        E.dve(lambda e: e.memset(epsb[:], 1e-5), writes=[eps_b])
        E.dve(lambda e: e.tensor_scalar(ghb[:], g0b[:], 0.5, None, ALU.add),
              reads=[g0_b], writes=[gh_b])
        E.dve(lambda e: e.tensor_tensor(g1b[:], ghb[:], g0b[:], ALU.subtract),
              reads=[gh_b, g0_b], writes=[g1_b])
        E.dve(lambda e: e.tensor_scalar(g1b[:], g1b[:], 2.0, None, ALU.mult),
              reads=[g1_b], writes=[g1_b])
        E.dve(lambda e: e.tensor_scalar(h1b[:], g1b[:], -1.0, 1.0,
                                        ALU.mult, ALU.add),
              reads=[g1_b], writes=[h1_b])
        E.pool(lambda e: e.memset(identb[:], 0.0), writes=[idb_b])
        E.pool(lambda e: e.affine_select(out=identb[:], in_=identb[:],
                                         compare_op=ALU.not_equal, fill=1.0,
                                         base=0, pattern=[[-1, 128]],
                                         channel_multiplier=1),
               reads=[idb_b], writes=[idb_b])

        for c in range(KD):
            b = bigBh[c % 2]
            hsl = slice((c % 2) * D, (c % 2) * D + D)
            E.dma("sp", lambda e, c=c, hsl=hsl: e.dma_start(
                out=bigB[:, hsl], in_=WfT[c * 128:(c + 1) * 128, :]), writes=[b])
            E.dve(lambda e, b=b, hsl=hsl: e.tensor_reduce(
                red1[:], bigB[:, hsl], axis=X_AX, op=ALU.add,
                apply_absolute_value=True), reads=[b], writes=[red1_b])
            if c == 0:
                E.dve(lambda e: e.tensor_copy(acc1[:], red1[:]),
                      reads=[red1_b], writes=[acc1_b])
            else:
                E.dve(lambda e: e.tensor_tensor(acc1[:], acc1[:], red1[:], ALU.add),
                      reads=[acc1_b, red1_b], writes=[acc1_b])
        E.pe(lambda e: e.matmul(pstat[0:1, 0:1], onesc[:], acc1[:],
                                start=True, stop=True),
             reads=[onesc_b, acc1_b], writes=[pstat_b])
        E.act(lambda e: e.activation(out=alr[:], in_=pstat[0:1, 0:1],
                                     func=AF.Copy, scale=1.0 / (D * D)),
              reads=[pstat_b], writes=[alr_b])
        E.dve(lambda e: e.tensor_scalar(alr[:], alr[:], 1e-8, None, ALU.max),
              reads=[alr_b], writes=[alr_b])
        E.act(lambda e: e.activation(out=tiny4[0:1, 0:1], in_=alr[:],
                                     func=AF.Copy, scale=0.5 / 127.0),
              reads=[alr_b], writes=[tiny_b])
        E.pe(lambda e: e.matmul(pbc[0:128, 0:1], onesr[:], tiny4[0:1, 0:1],
                                start=True, stop=True),
             reads=[tiny_b, onesr_b], writes=[pbc_b])
        E.act(lambda e: e.activation(out=ha128[:], in_=pbc[0:128, 0:1],
                                     func=AF.Copy), reads=[pbc_b], writes=[ha_b])
        E.dve(lambda e: e.reciprocal(tiny4[0:1, 1:2], alr[:]),
              reads=[alr_b], writes=[tiny_b])
        E.pe(lambda e: e.matmul(pbc[0:128, 0:1], onesr[:], tiny4[0:1, 1:2],
                                start=True, stop=True),
             reads=[onesr_b, tiny_b], writes=[pbc_b])
        E.act(lambda e: e.activation(out=ra128[:], in_=pbc[0:128, 0:1],
                                     func=AF.Copy), reads=[pbc_b], writes=[ra_b])

        def emit_quant(c):
            bq = bigBh[c % 2]
            hof = (c % 2) * D
            E.dma("sp", lambda e, c=c, hof=hof: e.dma_start(
                out=bigB[:, hof:hof + D], in_=WfT[c * 128:(c + 1) * 128, :]),
                writes=[bq])
            E.act(lambda e, bq=bq, hof=hof: e.activation(
                out=sc1[:, 0:D], in_=bigB[:, hof:hof + D], func=AF.Copy,
                scale=ra128[:]), reads=[bq, ra_b], writes=[sc[1]])
            E.dve(lambda e: e.tensor_scalar(sc2[:], sc1[:, 0:D], C32, -C32,
                                            ALU.add, ALU.add),
                  reads=[sc[1]], writes=[sc[2]])
            E.dve(lambda e: e.tensor_scalar(sc3[:], sc1[:, 0:D], C64, -C64,
                                            ALU.add, ALU.add),
                  reads=[sc[1]], writes=[sc[3]])
            E.dve(lambda e: e.scalar_tensor_tensor(sc3[:], sc2[:], h1b[:], sc3[:],
                                                   ALU.mult, ALU.add),
                  reads=[sc[2], h1_b, sc[3]], writes=[sc[3]])
            E.dve(lambda e, c=c: e.tensor_scalar(
                bigA[:, c * D:(c + 1) * D], sc3[:], 1.0, -1.0, ALU.min, ALU.max),
                reads=[sc[3]], writes=[bigA_c[c]])
            E.dma("act", lambda e, c=c: e.dma_start(
                out=s_wqT.t[c * 128:(c + 1) * 128, :],
                in_=bigA[:, c * D:(c + 1) * D]),
                reads=[bigA_c[c]], writes=[s_wqT])
        merge(bigB_b, bigBh[0], bigBh[1])

        TB = 256
        NTF = T // TB
        stt8_p = [stt8_b, Buf(stt8)]
        stg_p = [stg_b, Buf(stg)]
        bigB_h = [Buf(bigB), Buf(bigB)]
        xq_h = [Buf(wg), Buf(wg)]

        def sc8(ctx, j):
            return stt8[:, ctx * 32 + j * 2: ctx * 32 + j * 2 + 2]

        def sc8b(ctx, j, tb):
            return stt8[:, ctx * 32 + j * 2 + tb: ctx * 32 + j * 2 + tb + 1]

        rnd_p = [Buf(wg), Buf(wg)]
        bigA_c = [Buf(bigA) for _ in range(KD)]

        def ffn_parts(src_kind, with_ln2, dst_hbm, zn_out):

            def pre_mm(k):
                hp = (k + 1) % 2
                ctx = hp
                bB = bigB_h[hp]
                sb8 = stt8_p[ctx]
                hof = hp * 2 * D
                tsl = slice(k * TB, (k + 1) * TB)
                src = xN if src_kind == "x" else s_x2.t
                if src_kind == "x":
                    rd = []
                else:
                    rd = [s_x2r[2 * k], s_x2r[2 * k + 1]]
                bh = [Buf(bigB), Buf(bigB)]
                for tb in range(2):
                    rowsl = slice(k * TB + tb * 128, k * TB + (tb + 1) * 128)
                    E.dma("sp", lambda e, rowsl=rowsl, src=src, hof=hof, tb=tb:
                          e.dma_start(
                              out=bigB[:, hof + tb * D:hof + (tb + 1) * D],
                              in_=src[rowsl, :]),
                          reads=rd, war=[bB], wend=[bh[tb]])
                for tb in range(2):
                    po = tb
                    bsl = slice(hof + tb * D, hof + (tb + 1) * D)
                    E.dve(lambda e, bsl=bsl, ctx=ctx, tb=tb: e.reduce_max(
                        sc8b(ctx, 2, tb), bigB[:, bsl], axis=X_AX),
                        reads=[bh[tb]], writes=[sb8])
                    E.dve(lambda e, bsl=bsl, ctx=ctx, tb=tb: e.tensor_reduce(
                        sc8b(ctx, 3, tb), bigB[:, bsl], axis=X_AX, op=ALU.min),
                        reads=[bh[tb]], writes=[sb8])
                    E.act(lambda e, bsl=bsl, ctx=ctx, po=po, tb=tb:
                          e.activation(
                              out=stg[:, po * D:(po + 1) * D],
                              in_=bigB[:, bsl], func=AF.Square,
                              accum_out=sc8b(ctx, 1, tb)),
                          reads=[bh[tb]], writes=[stg_p[po], sb8])
                    E.act(lambda e, bsl=bsl, ctx=ctx, po=po, tb=tb:
                          e.activation(
                              out=stg[:, po * D:(po + 1) * D],
                              in_=bigB[:, bsl], func=AF.Copy,
                              accum_out=sc8b(ctx, 0, tb)),
                          reads=[bh[tb]], writes=[stg_p[po], sb8])
                merge(bB, bh[0], bh[1])
                E.dve(lambda e, ctx=ctx: e.tensor_scalar(
                    sc8(ctx, 4), sc8(ctx, 0), 1.0 / D, None, ALU.mult),
                    reads=[sb8], writes=[sb8])
                E.dve(lambda e, ctx=ctx: e.tensor_scalar(
                    sc8(ctx, 10), sc8(ctx, 1), 1.0 / D, None, ALU.mult),
                    reads=[sb8], writes=[sb8])
                E.dve(lambda e, ctx=ctx: e.tensor_tensor(
                    sc8(ctx, 11), sc8(ctx, 4), sc8(ctx, 4), ALU.mult),
                    reads=[sb8], writes=[sb8])
                E.dve(lambda e, ctx=ctx: e.tensor_tensor(
                    sc8(ctx, 11), sc8(ctx, 11), sc8(ctx, 10), ALU.subtract),
                    reads=[sb8], writes=[sb8])
                E.act(lambda e, ctx=ctx: e.activation(
                    out=sc8(ctx, 11), in_=sc8(ctx, 11), func=AF.Ln,
                    scale=-1.0, bias=epsb[:]),
                    reads=[sb8, eps_b], writes=[sb8])
                E.act(lambda e, ctx=ctx: e.activation(
                    out=sc8(ctx, 5), in_=sc8(ctx, 11), func=AF.Exp,
                    scale=-0.5), reads=[sb8], writes=[sb8])
                E.dve(lambda e, ctx=ctx: e.tensor_tensor(
                    sc8(ctx, 2), sc8(ctx, 2), sc8(ctx, 4), ALU.subtract),
                    reads=[sb8], writes=[sb8])
                E.dve(lambda e, ctx=ctx: e.tensor_tensor(
                    sc8(ctx, 3), sc8(ctx, 4), sc8(ctx, 3), ALU.subtract),
                    reads=[sb8], writes=[sb8])
                E.dve(lambda e, ctx=ctx: e.tensor_tensor(
                    sc8(ctx, 6), sc8(ctx, 2), sc8(ctx, 3), ALU.max),
                    reads=[sb8], writes=[sb8])
                E.dve(lambda e, ctx=ctx: e.tensor_scalar(
                    sc8(ctx, 6), sc8(ctx, 6), 1.0 / 127.0, 7.9e-33,
                    ALU.mult, ALU.max),
                    reads=[sb8], writes=[sb8])
                E.dve(lambda e, ctx=ctx: e.reciprocal(
                    sc8(ctx, 7), sc8(ctx, 6)), reads=[sb8], writes=[sb8])
                E.dve(lambda e, ctx=ctx: e.tensor_tensor(
                    sc8(ctx, 8), sc8(ctx, 4), sc8(ctx, 7), ALU.mult),
                    reads=[sb8], writes=[sb8])
                E.dve(lambda e, ctx=ctx: e.tensor_scalar(
                    sc8(ctx, 8), sc8(ctx, 8), -1.0, None, ALU.mult),
                    reads=[sb8], writes=[sb8])
                E.dve(lambda e, ctx=ctx: e.tensor_tensor(
                    sc8(ctx, 10), sc8(ctx, 5), sc8(ctx, 6), ALU.mult),
                    reads=[sb8], writes=[sb8])
                E.dve(lambda e, ctx=ctx: e.scalar_tensor_tensor(
                    sc8(ctx, 9), sc8(ctx, 10), ha128[:], zero2[:, 0:2],
                    ALU.mult, ALU.subtract),
                    reads=[sb8, ha_b, zero2_b], writes=[sb8])
                E.dve(lambda e, ctx=ctx: e.tensor_scalar(
                    sc8(ctx, 9), sc8(ctx, 9), 127.0, None, ALU.mult),
                    reads=[sb8], writes=[sb8])
                for tb in range(2):
                    po = tb
                    rG = rnd_p[po]
                    bsl = slice(hof + tb * D, hof + (tb + 1) * D)
                    rsl = slice(8192 + po * D, 8192 + (po + 1) * D)
                    zt = sc1 if po == 0 else sc2
                    zb = sc[1] if po == 0 else sc[2]
                    E.act(lambda e, bsl=bsl, ctx=ctx, tb=tb, zt=zt:
                          e.activation(
                              out=zt[:, 0:D], in_=bigB[:, bsl],
                              func=AF.Identity, scale=sc8b(ctx, 7, tb),
                              bias=sc8b(ctx, 8, tb)),
                          reads=[bB, sb8], writes=[zb])
                    ab = sc[3] if po == 0 else sc[0]
                    at = sc3 if po == 0 else sc0
                    E.dve(lambda e, zt=zt, at=at: e.tensor_scalar(
                        at[:, 0:D], zt[:, 0:D], C32, -C32, ALU.add, ALU.add),
                        reads=[zb], writes=[ab])
                    E.dve(lambda e, zt=zt: e.tensor_scalar(
                        zt[:, 0:D], zt[:, 0:D], C64, -C64, ALU.add, ALU.add),
                        reads=[zb], writes=[zb])
                    E.dve(lambda e, rsl=rsl, at=at, zt=zt:
                          e.scalar_tensor_tensor(
                              wg[:, rsl], at[:, 0:D], h1b[:], zt[:, 0:D],
                              ALU.mult, ALU.add),
                          reads=[ab, h1_b, zb], writes=[rG])

            def tr(k, jgrp):
                hp = (k + 1) % 2
                xQ = xq_h[hp]
                xof = hp * KD * TB
                base = jgrp * 4
                for u in range(base, base + 4):
                    tb, i = divmod(u, KD)
                    po = tb
                    rG = rnd_p[po]
                    rsl0 = 8192 + po * D
                    pb = ptrb_b[u % 2]
                    E.pe(lambda e, rsl0=rsl0, i=i, pb=pb: e.transpose(
                        pb.t[:], wg[:, rsl0 + i * 128: rsl0 + (i + 1) * 128],
                        identb[:]),
                        reads=[rG, idb_b], writes=[pb])
                    off = xof + i * TB + tb * 128
                    E.act(lambda e, off=off, pb=pb: e.activation(
                        out=wg[:, off:off + 128], in_=pb.t[:], func=AF.Copy),
                        reads=[pb], writes=[xQ])

            def mm(k, embed=None):
                hp = (k + 1) % 2
                ctx = hp
                sb8 = stt8_p[ctx]
                bB = bigB_h[hp]
                xQ = xq_h[hp]
                hof = hp * 2 * D
                xof = hp * KD * TB
                tsl = slice(k * TB, (k + 1) * TB)
                grp = 0
                for tb in range(2):
                    po = tb
                    for os in range(4):
                        p = pmm_b[(tb * 4 + os) % 4]
                        for i in range(KD):
                            E.pe(lambda e, i=i, tb=tb, os=os, p=p, xof=xof:
                                 e.matmul(
                                     p.t[:],
                                     wg[:, xof + i * TB + tb * 128:
                                         xof + i * TB + tb * 128 + 128],
                                     bigA[:, i * D + os * 512:
                                          i * D + (os + 1) * 512],
                                     start=(i == 0), stop=(i == KD - 1)),
                                 reads=[xQ, bigA_c[i]],
                                 war=[p] if i == 0 else (),
                                 wend=[p] if i == KD - 1 else ())
                        if embed is not None and grp >= 4:
                            embed((grp - 4) * 2)
                            embed((grp - 4) * 2 + 1)
                        grp += 1
                        xo = slice(hof + tb * D + os * 512,
                                   hof + tb * D + (os + 1) * 512)
                        if with_ln2:
                            E.dve(lambda e, p=p, ctx=ctx, tb=tb, os=os, xo=xo:
                                  e.scalar_tensor_tensor(
                                      bigB[:, xo], p.t[:], sc8b(ctx, 9, tb),
                                      bigB[:, xo], ALU.mult, ALU.add,
                                      accum_out=sc8b(ctx, 16 + os, tb)),
                                  reads=[p, sb8, bB], writes=[bB, sb8])
                            sqs = slice(tb * D + os * 512,
                                        tb * D + (os + 1) * 512)
                            E.act(lambda e, ctx=ctx, tb=tb, os=os, xo=xo,
                                  sqs=sqs: e.activation(
                                      out=stg[:, sqs], in_=bigB[:, xo],
                                      func=AF.Square,
                                      accum_out=sc8b(ctx, 12 + os, tb)),
                                  reads=[bB], writes=[stg_p[tb], sb8])
                        else:
                            E.dve(lambda e, p=p, ctx=ctx, tb=tb, xo=xo:
                                  e.scalar_tensor_tensor(
                                      bigB[:, xo], p.t[:], sc8b(ctx, 9, tb),
                                      bigB[:, xo], ALU.mult, ALU.add),
                                  reads=[p, sb8, bB], writes=[bB])
                E.dma("pool" if k < NTF - 1 else "sp",
                      lambda e, tsl=tsl, hof=hof: e.dma_start(
                          out=dst_hbm.t[tsl, :].rearrange(
                              "(b p) d -> p b d", p=128),
                          in_=bigB[:, hof:hof + 2 * D].rearrange(
                              "p (b d) -> p b d", d=D)),
                      reads=[bB], writes=[dst_hbm])
                if with_ln2:
                    E.dve(lambda e, ctx=ctx: e.tensor_tensor(
                        sc8(ctx, 16), sc8(ctx, 16), sc8(ctx, 17), ALU.add),
                        reads=[sb8], writes=[sb8])
                    E.dve(lambda e, ctx=ctx: e.tensor_tensor(
                        sc8(ctx, 18), sc8(ctx, 18), sc8(ctx, 19), ALU.add),
                        reads=[sb8], writes=[sb8])
                    E.dve(lambda e, ctx=ctx: e.tensor_tensor(
                        sc8(ctx, 0), sc8(ctx, 16), sc8(ctx, 18), ALU.add),
                        reads=[sb8], writes=[sb8])
                    E.dve(lambda e, ctx=ctx: e.tensor_tensor(
                        sc8(ctx, 12), sc8(ctx, 12), sc8(ctx, 13), ALU.add),
                        reads=[sb8], writes=[sb8])
                    E.dve(lambda e, ctx=ctx: e.tensor_tensor(
                        sc8(ctx, 14), sc8(ctx, 14), sc8(ctx, 15), ALU.add),
                        reads=[sb8], writes=[sb8])
                    E.dve(lambda e, ctx=ctx: e.tensor_tensor(
                        sc8(ctx, 1), sc8(ctx, 12), sc8(ctx, 14), ALU.add),
                        reads=[sb8], writes=[sb8])
                    E.dve(lambda e, ctx=ctx: e.tensor_scalar(
                        sc8(ctx, 4), sc8(ctx, 0), 1.0 / D, None, ALU.mult),
                        reads=[sb8], writes=[sb8])
                    E.dve(lambda e, ctx=ctx: e.tensor_scalar(
                        sc8(ctx, 10), sc8(ctx, 1), 1.0 / D, None, ALU.mult),
                        reads=[sb8], writes=[sb8])
                    E.dve(lambda e, ctx=ctx: e.tensor_tensor(
                        sc8(ctx, 11), sc8(ctx, 4), sc8(ctx, 4), ALU.mult),
                        reads=[sb8], writes=[sb8])
                    E.dve(lambda e, ctx=ctx: e.tensor_tensor(
                        sc8(ctx, 11), sc8(ctx, 11), sc8(ctx, 10), ALU.subtract),
                        reads=[sb8], writes=[sb8])
                    E.act(lambda e, ctx=ctx: e.activation(
                        out=sc8(ctx, 11), in_=sc8(ctx, 11), func=AF.Ln,
                        scale=-1.0, bias=epsb[:]),
                        reads=[sb8, eps_b], writes=[sb8])
                    E.act(lambda e, ctx=ctx: e.activation(
                        out=sc8(ctx, 5), in_=sc8(ctx, 11), func=AF.Exp,
                        scale=-0.5), reads=[sb8], writes=[sb8])
                    E.dve(lambda e, ctx=ctx: e.tensor_tensor(
                        sc8(ctx, 8), sc8(ctx, 4), sc8(ctx, 5), ALU.mult),
                        reads=[sb8], writes=[sb8])
                    E.dve(lambda e, ctx=ctx: e.tensor_scalar(
                        sc8(ctx, 8), sc8(ctx, 8), -1.0, None, ALU.mult),
                        reads=[sb8], writes=[sb8])
                    for tb in range(2):
                        po = tb
                        bsl = slice(hof + tb * D, hof + (tb + 1) * D)
                        ssl = slice(po * D, (po + 1) * D)
                        E.act(lambda e, bsl=bsl, ssl=ssl, ctx=ctx, tb=tb:
                              e.activation(
                                  out=stg[:, ssl], in_=bigB[:, bsl],
                                  func=AF.Identity, scale=sc8b(ctx, 5, tb),
                                  bias=sc8b(ctx, 8, tb)),
                              reads=[bB, sb8], writes=[stg_p[tb]])
                    E.dma("pool" if k < NTF - 1 else "sp",
                      lambda e, tsl=tsl: e.dma_start(
                          out=zn_out.t[tsl, :].rearrange(
                              "(b p) d -> p b d", p=128),
                          in_=stg[:, 0:2 * D].rearrange(
                              "p (b d) -> p b d", d=D)),
                      reads=[stg_p[0], stg_p[1]], writes=[zn_out])


            return pre_mm, tr, mm

        def ffn_phase(src_kind, with_ln2, dst_hbm, zn_out):
            pre_mm, tr, mm = ffn_parts(src_kind, with_ln2, dst_hbm, zn_out)
            pre_mm(0)
            for j in range(8):
                tr(0, j)
            pre_mm(1)
            for k in range(NTF):
                nxt = k + 1
                if nxt < NTF:
                    mm(k, embed=lambda j, nxt=nxt: tr(nxt, j))
                else:
                    mm(k)
                if k + 2 < NTF:
                    pre_mm(k + 2)

        pre1, tr1, mm1 = ffn_parts("x", True, s_x1, s_zn)
        pre1(0)
        for c in range(KD):
            emit_quant(c)
            if c == 2:
                for j in range(4):
                    tr1(0, j)
            if c == 3:
                for j in range(4, 8):
                    tr1(0, j)
        merge(bigB_h[0], bigB_b, bigBh[0], bigBh[1])
        pre1(1)
        for k in range(NTF):
            nxt = k + 1
            if nxt < NTF:
                mm1(k, embed=lambda j, nxt=nxt: tr1(nxt, j))
            else:
                mm1(k)
            if k + 2 < NTF:
                pre1(k + 2)
        merge(stg_b, stg_p[1])

        merge(wgh[0], xq_b, *xq_h, *rnd_p)
        merge(wgh[1], xq_b, *xq_h, *rnd_p)
        merge(stg_b, stg_p[0], stg_p[1])
        merge(stg_p[0], stg_b)
        merge(stg_p[1], stg_b)
        for c in range(KD):
            E.dma("act" if c % 2 else "sp", lambda e, c=c: e.dma_start(
                out=bigA[:, c * T:(c + 1) * T],
                in_=s_zn.t[:, c * 128:(c + 1) * 128], transpose=True),
                reads=[s_zn], writes=[bigA_b], war=[bigA_c[c]])

        def wl_phase(W, evac, out_hbm, conv):
            ng = H // 512
            for g in range(ng):
                half = g % 2
                wslice = slice(half * 8192, half * 8192 + 8192)
                E.dma("pool", lambda e, g=g, wslice=wslice, W=W: e.dma_start(
                    out=wg[:, wslice].rearrange("p (c h) -> p c h", h=512),
                    in_=W[:, g * 512:(g + 1) * 512].rearrange(
                        "(c p) h -> p c h", p=128)),
                    writes=[wgh[half]])
                for hb in range(4):
                    blkidx = g * 4 + hb
                    for k in range(NT):
                        p = pmm_b[k % 4]
                        for i in range(KD):
                            off = half * 8192 + i * 512 + hb * 128
                            E.pe(lambda e, off=off, i=i, k=k, p=p: e.matmul(
                                p.t[:], wg[:, off:off + 128],
                                bigA[:, i * T + k * TT: i * T + (k + 1) * TT],
                                start=(i == 0), stop=(i == KD - 1)),
                                reads=[wgh[half], bigA_b],
                                war=[p] if i == 0 else (),
                                wend=[p] if i == KD - 1 else ())
                        evac(blkidx, k, p)
                    if conv is not None:
                        conv(blkidx)

        def left_evac(blkidx, k, p):
            st = stg[:, (k % 4) * TT:(k % 4) * TT + TT]
            E.act(lambda e, blkidx=blkidx, p=p, st=st: e.activation(
                out=st, in_=p.t[:], func=AF.Gelu_apprx_tanh,
                bias=vcol(VBL, blkidx)),
                reads=[p, vecs_b], writes=[stg_b])
            E.dma("sp", lambda e, blkidx=blkidx, k=k, st=st: e.dma_start(
                out=s_left.t[blkidx * 128:(blkidx + 1) * 128,
                              k * TT:(k + 1) * TT], in_=st),
                reads=[stg_b], writes=[s_left])

        wl_phase(Wl, left_evac, s_left, None)

        def rin_evac(blkidx, k, p):
            E.act(lambda e, blkidx=blkidx, k=k, p=p: e.activation(
                out=sc0[:, 3 + k * TT: 3 + (k + 1) * TT], in_=p.t[:],
                func=AF.Identity, bias=vcol(VBR, blkidx)),
                reads=[p, vecs_b], writes=[sc[0]])

        ng = H // 512
        for g in range(ng):
            half = g % 2
            wslice = slice(half * 8192, half * 8192 + 8192)
            E.dma("pool", lambda e, g=g, wslice=wslice: e.dma_start(
                out=wg[:, wslice].rearrange("p (c h) -> p c h", h=512),
                in_=Wr[:, g * 512:(g + 1) * 512].rearrange(
                    "(c p) h -> p c h", p=128)),
                writes=[wgh[half]])
            for hb in range(4):
                blkidx = g * 4 + hb
                bp = blkidx % 2
                rinT = sc0 if bp == 0 else sc1
                rinB = sc[0] if bp == 0 else sc[1]
                uacc = sc2 if bp == 0 else sc3
                uaccB = sc[2] if bp == 0 else sc[3]
                E.dve(lambda e, rinT=rinT: e.memset(rinT[:, 0:3], 0.0),
                      writes=[rinB])
                for k in range(NT):
                    p = pmm_b[k % 4]
                    for i in range(KD):
                        off = half * 8192 + i * 512 + hb * 128
                        E.pe(lambda e, off=off, i=i, k=k, p=p: e.matmul(
                            p.t[:], wg[:, off:off + 128],
                            bigA[:, i * T + k * TT: i * T + (k + 1) * TT],
                            start=(i == 0), stop=(i == KD - 1)),
                            reads=[wgh[half], bigA_b],
                            war=[p] if i == 0 else (),
                            wend=[p] if i == KD - 1 else ())
                    E.act(lambda e, blkidx=blkidx, k=k, p=p, rinT=rinT:
                          e.activation(
                              out=rinT[:, 3 + k * TT: 3 + (k + 1) * TT],
                              in_=p.t[:], func=AF.Identity,
                              bias=vcol(VBR, blkidx)),
                          reads=[p, vecs_b], writes=[rinB])
                E.act(lambda e, blkidx=blkidx, rinT=rinT, uacc=uacc:
                      e.activation(
                          out=uacc[:, 0:T], in_=rinT[:, 0:T], func=AF.Identity,
                          scale=vcol(VCW, blkidx), bias=vcol(VCB, blkidx)),
                      reads=[rinB, vecs_b], writes=[uaccB])
                for kk in (1, 2, 3):
                    E.dve(lambda e, kk=kk, blkidx=blkidx, rinT=rinT, uacc=uacc:
                          e.scalar_tensor_tensor(
                              uacc[:, 0:T], rinT[:, kk:kk + T],
                              vcol(VCW, kk * 20 + blkidx),
                              uacc[:, 0:T], ALU.mult, ALU.add),
                          reads=[rinB, vecs_b, uaccB], writes=[uaccB])
                stu = stg[:, bp * T:bp * T + T]
                E.dve(lambda e, stu=stu, uacc=uacc: e.tensor_copy(
                    stu, uacc[:, 0:T]),
                    reads=[uaccB], writes=[stg_p[bp]])
                E.dma("sp", lambda e, blkidx=blkidx, stu=stu: e.dma_start(
                    out=s_u.t[blkidx * 128:(blkidx + 1) * 128, :], in_=stu),
                    reads=[stg_p[bp]], writes=[s_u])

        merge(stg_p[0], stg_b, stg_p[1])
        merge(stg_p[1], stg_b, stg_p[0])
        merge(rnd_p[0], wgh[0], wgh[1])
        E.dma("act", lambda e: e.dma_start(
            out=bigA[:, 0:KH * T].rearrange("p (c t) -> p c t", t=T),
            in_=s_u.t[:].rearrange("(c p) t -> p c t", p=128)),
            reads=[s_u], writes=[bigA_b])
        for b in range(KH):
            half = b % 2
            wa_sl = slice(half * 5120, half * 5120 + 2560)
            wi_sl = slice(half * 5120 + 2560, half * 5120 + 5120)
            E.dma("pool", lambda e, b=b, wa_sl=wa_sl: e.dma_start(
                out=wg[:, wa_sl].rearrange("p (c h) -> p c h", h=128),
                in_=Wa[:, b * 128:(b + 1) * 128].rearrange(
                    "(c p) h -> p c h", p=128)), writes=[wgh[half]])
            E.dma("pool", lambda e, b=b, wi_sl=wi_sl: e.dma_start(
                out=wg[:, wi_sl].rearrange("p (c h) -> p c h", h=128),
                in_=Wi[:, b * 128:(b + 1) * 128].rearrange(
                    "(c p) h -> p c h", p=128)), writes=[wgh[half]])
            stl = stg[:, (b % 2) * T:(b % 2) * T + T]
            stlb = stg_p[b % 2]
            E.dma("sp", lambda e, b=b, stl=stl: e.dma_start(
                out=stl, in_=s_left.t[b * 128:(b + 1) * 128, :]),
                reads=[s_left], writes=[stlb])
            for k in range(NT):
                pr = pmm_b[(2 * k) % 4]
                pi = pmm_b[(2 * k + 1) % 4]
                for c in range(KH):
                    off = half * 5120 + c * 128
                    E.pe(lambda e, off=off, c=c, pr=pr, k=k: e.matmul(
                        pr.t[:], wg[:, off:off + 128],
                        bigA[:, c * T + k * TT: c * T + (k + 1) * TT],
                        start=(c == 0), stop=(c == KH - 1)),
                        reads=[wgh[half], bigA_b],
                        war=[pr] if c == 0 else (),
                        wend=[pr] if c == KH - 1 else ())
                for c in range(KH):
                    off = half * 5120 + 2560 + c * 128
                    E.pe(lambda e, off=off, c=c, pi=pi, k=k: e.matmul(
                        pi.t[:], wg[:, off:off + 128],
                        bigA[:, c * T + k * TT: c * T + (k + 1) * TT],
                        start=(c == 0), stop=(c == KH - 1)),
                        reads=[wgh[half], bigA_b],
                        war=[pi] if c == 0 else (),
                        wend=[pi] if c == KH - 1 else ())
                ksl = slice(k * TT, (k + 1) * TT)
                E.act(lambda e, b=b, pr=pr, ksl=ksl: e.activation(
                    out=sc0[:, ksl], in_=pr.t[:], func=AF.Tanh, scale=0.5,
                    bias=vcol(VHBA, b)), reads=[pr, vecs_b], writes=[sc[0]])
                E.act(lambda e, b=b, pi=pi, ksl=ksl: e.activation(
                    out=sc1[:, ksl], in_=pi.t[:], func=AF.Tanh, scale=0.5,
                    bias=vcol(VHBI, b)), reads=[pi, vecs_b], writes=[sc[1]])
            E.act(lambda e, b=b: e.activation(
                out=sc2[:], in_=sc0[:, 0:T], func=AF.Exp,
                scale=vcol(VMS2, b), bias=vcol(VMS2, b)),
                reads=[sc[0], vecs_b], writes=[sc[2]])
            E.act(lambda e, b=b: e.activation(
                out=sc3[:], in_=sc0[:, 0:T], func=AF.Exp,
                scale=vcol(VMS1, b), bias=vcol(VMS1, b)),
                reads=[sc[0], vecs_b], writes=[sc[3]])
            E.dve(lambda e: e.tensor_scalar(sc3[:], sc3[:], -1.0, 1.0,
                                            ALU.mult, ALU.add),
                  reads=[sc[3]], writes=[sc[3]])
            E.dve(lambda e: e.tensor_scalar(sc3[:], sc3[:], 1e-12, None,
                                            ALU.max),
                  reads=[sc[3]], writes=[sc[3]])
            E.act(lambda e: e.activation(out=sc0[:, 0:T], in_=sc3[:],
                                         func=AF.Sqrt, scale=0.25),
                  reads=[sc[3]], writes=[sc[0]])
            E.dve(lambda e: e.tensor_scalar(sc1[:, 0:T], sc1[:, 0:T], 1.0,
                                            None, ALU.add),
                  reads=[sc[1]], writes=[sc[1]])
            E.dve(lambda e, b=b: e.tensor_tensor(
                sc1[:, 0:T], sc1[:, 0:T], bigA[:, b * T:(b + 1) * T],
                ALU.mult),
                reads=[sc[1], bigA_b], writes=[sc[1]])
            E.dve(lambda e: e.tensor_tensor(sc1[:, 0:T], sc1[:, 0:T],
                                            sc0[:, 0:T], ALU.mult),
                  reads=[sc[1], sc[0]], writes=[sc[1]])
            E.dve(lambda e: e.tensor_tensor_scan(sc3[:], sc2[:], sc1[:, 0:T],
                                                 zero1[:], ALU.mult, ALU.add),
                  reads=[sc[2], sc[1], zero1_b], writes=[sc[3]])
            stgo = wg[:, 10240:10240 + T]
            E.dve(lambda e, stl=stl, stgo=stgo, stlb=stlb: e.tensor_tensor(
                stgo, sc3[:], stl, ALU.mult),
                reads=[sc[3], stlb], writes=[rnd_p[0]])
            E.dma("sp", lambda e, b=b, stgo=stgo: e.dma_start(
                out=s_gat.t[b * 128:(b + 1) * 128, :], in_=stgo),
                reads=[rnd_p[0]], writes=[s_gat])

        E.dma("act", lambda e: e.dma_start(
            out=bigA[:, 0:KH * T].rearrange("p (c t) -> p c t", t=T),
            in_=s_gat.t[:].rearrange("(c p) t -> p c t", p=128)),
            reads=[s_gat], writes=[bigA_b])
        for os in range(4):
            osl = slice(os * 512, (os + 1) * 512)
            E.dma("pool", lambda e, osl=osl: e.dma_start(
                out=wg[:, 0:KH * 512].rearrange("p (c h) -> p c h", h=512),
                in_=Wo[:, osl].rearrange("(c p) h -> p c h", p=128)),
                writes=[wgh[0], wgh[1]], war=[rnd_p[0]])
            for tb in range(16):
                p = pmm_b[tb % 4]
                xb = xs_[tb % 2]
                E.dma("sp", lambda e, tb=tb, osl=osl, xb=xb: e.dma_start(
                    out=xb.t[:], in_=s_x1.t[tb * 128:(tb + 1) * 128, osl]),
                    reads=[s_x1], writes=[xb])
                for c in range(KH):
                    E.pe(lambda e, c=c, tb=tb, p=p: e.matmul(
                        p.t[:], bigA[:, c * T + tb * 128: c * T + tb * 128 + 128],
                        wg[:, c * 512:(c + 1) * 512],
                        start=(c == 0), stop=False),
                        reads=[bigA_b, wgh[0]],
                        war=[p] if c == 0 else ())
                E.pe(lambda e, osl=osl, p=p: e.matmul(
                    p.t[:], onesr[:], borow[0:1, osl],
                    start=False, stop=True),
                    reads=[onesr_b, borow_b], wend=[p])
                tbuf = t1_[tb % 2]
                E.dve(lambda e, p=p, xb=xb, tbuf=tbuf: e.tensor_tensor(
                    tbuf.t[:], p.t[:], xb.t[:], ALU.add),
                    reads=[p, xb], writes=[tbuf])
                E.dma("pool", lambda e, tb=tb, osl=osl, tbuf=tbuf: e.dma_start(
                    out=s_x2.t[tb * 128:(tb + 1) * 128, osl], in_=tbuf.t[:]),
                    reads=[tbuf], writes=[s_x2r[tb]])

        for _xb in xq_h:
            merge(_xb, wgh[0], wgh[1])
        for _xb in rnd_p:
            merge(_xb, wgh[0], wgh[1])
        merge(stg_p[0], stg_b)
        merge(stg_p[1], stg_b)
        E.dma("act", lambda e: e.dma_start(
            out=bigA[:, 0:KD * D].rearrange("p (c o) -> p c o", o=D),
            in_=s_wqT.t[:].rearrange("(c p) o -> p c o", p=128)),
            reads=[s_wqT], writes=[bigA_b] + bigA_c)
        ffn_phase("x2", False, out_b, None)

        final = [(n, E.cnt[n]) for n in E.cnt]
        E.barrier_wait("sp", final)
        E._flush()

    return nc


def kernel(L0_attn_x, L0_attn_Wl, L0_attn_bl, L0_attn_Wr, L0_attn_br,
           L0_attn_conv_w, L0_attn_conv_b, L0_attn_Wa, L0_attn_ba,
           L0_attn_Wi, L0_attn_bi, L0_attn_lam, L0_attn_Wo, L0_attn_bo,
           L0_ffn_weight):
    x = np.asarray(L0_attn_x)
    n = 8
    nc = build_nc()
    WfT = np.ascontiguousarray(np.asarray(L0_ffn_weight).T)
    cw = np.ascontiguousarray(np.asarray(L0_attn_conv_w).reshape(4, H))
    common = {
        "Wl": np.ascontiguousarray(L0_attn_Wl),
        "Wr": np.ascontiguousarray(L0_attn_Wr),
        "Wa": np.ascontiguousarray(L0_attn_Wa),
        "Wi": np.ascontiguousarray(L0_attn_Wi),
        "Wo": np.ascontiguousarray(L0_attn_Wo),
        "WfT": WfT,
        "vbl": np.ascontiguousarray(L0_attn_bl),
        "vbr": np.ascontiguousarray(L0_attn_br),
        "vba": np.ascontiguousarray(L0_attn_ba),
        "vbi": np.ascontiguousarray(L0_attn_bi),
        "vcb": np.ascontiguousarray(L0_attn_conv_b),
        "vlam": np.ascontiguousarray(L0_attn_lam),
        "vcw": cw,
        "vbo": np.ascontiguousarray(L0_attn_bo),
    }
    in_maps = []
    for b in range(n):
        m = dict(common)
        m["xN"] = np.ascontiguousarray(x[b])
        in_maps.append(m)
    res = run_bass_kernel_spmd(nc, in_maps, core_ids=list(range(n)))
    out = np.empty((n, T, D), np.float32)
    for b in range(n):
        out[b] = res.results[b]["out"]
    return out


# version 29 (final)
# speedup vs baseline: 1.0297x; 1.0002x over previous
import numpy as np
from contextlib import ExitStack

import concourse.bass as bass
import concourse.mybir as mybir
from concourse.bass_utils import run_bass_kernel_spmd

F32 = mybir.dt.float32
BF16 = mybir.dt.bfloat16
AF = mybir.ActivationFunctionType
ALU = mybir.AluOpType

T = 2048
D = 2048
H = 2560
KD = D // 128
KH = H // 128
TT = 512
NT = T // TT
C32 = float(np.float32(1.5 * 2.0 ** 23))
C64 = float(np.float32(1.5 * 2.0 ** 52))
X_AX = mybir.AxisListType.X


class Buf:
    __slots__ = ("t", "w", "rs")

    def __init__(self, t):
        self.t = t
        self.w = None
        self.rs = []


class Emit:

    ENGSEC = {"pe": "tensor", "dve": "vector", "act": "scalar",
              "pool": "gpsimd", "sp": "sync"}

    def __init__(self, nc, blk, sems):
        self.nc = nc
        self.blk = blk
        self.sems = sems
        self.cnt = {k: 0 for k in sems}
        self.waited = {e: {} for e in self.ENGSEC}
        self.pend = {e: [] for e in self.ENGSEC}
        self.qcnt = {}

    def _flush(self):
        for engname in self.ENGSEC:
            thunks = self.pend[engname]
            if not thunks:
                continue
            self.pend[engname] = []
            sec = getattr(self.blk, self.ENGSEC[engname])

            def body(eng, thunks=thunks):
                for th in thunks:
                    th(eng)
            sec(body)

    def _op(self, engname, fn, reads=(), writes=(), sem=None, inc=1,
            war=(), wend=()):
        sem = sem or engname
        toks = []
        for b in reads:
            if b.w:
                toks.append(b.w)
        for b in writes:
            if b.w:
                toks.append(b.w)
            toks.extend(b.rs)
        for b in war:
            if b.w:
                toks.append(b.w)
            toks.extend(b.rs)
        need = {}
        for s, v in toks:
            if v > need.get(s, 0):
                need[s] = v
        wmap = self.waited[engname]
        need = {s: v for s, v in need.items()
                if v > wmap.get(s, 0) and not (s == sem and False)}
        self.cnt[sem] += inc
        tok = (sem, self.cnt[sem])
        semh = self.sems[sem]
        semmap = self.sems

        def th(eng):
            for s, v in need.items():
                eng.wait_ge(semmap[s], v)
            instr = fn(eng)
            instr.then_inc(semh, inc)
        for s, v in need.items():
            if v > wmap.get(s, 0):
                wmap[s] = v
        self.pend[engname].append(th)
        for b in reads:
            b.rs.append(tok)
        for b in writes:
            b.w = tok
            b.rs = []
        for b in wend:
            b.w = tok
            b.rs = []
        return tok

    def pe(self, fn, reads=(), writes=(), war=(), wend=()):
        return self._op("pe", fn, reads, writes, war=war, wend=wend)

    def dve(self, fn, reads=(), writes=()):
        return self._op("dve", fn, reads, writes)

    def act(self, fn, reads=(), writes=()):
        return self._op("act", fn, reads, writes)

    def pool(self, fn, reads=(), writes=()):
        return self._op("pool", fn, reads, writes)

    def dma(self, q, fn, reads=(), writes=(), war=(), wend=()):
        j = self.qcnt.get(q, 0)
        self.qcnt[q] = j + 1
        semname = "d%s%d" % (q, j % 4)
        prev = self.cnt[semname]
        if prev:
            self.barrier_wait(q, [(semname, prev)])
        return self._op(q, fn, reads, writes, sem=semname, inc=16, war=war,
                        wend=wend)

    def barrier_wait(self, engname, toks):
        need = {}
        for s, v in toks:
            if v and v > need.get(s, 0):
                need[s] = v
        wmap = self.waited[engname]
        need = {s: v for s, v in need.items() if v > wmap.get(s, 0)}
        if not need:
            return
        semmap = self.sems

        def th(eng):
            for s, v in need.items():
                eng.wait_ge(semmap[s], v)
        for s, v in need.items():
            wmap[s] = v
        self.pend[engname].append(th)


def build_nc():
    nc = bass.Bass("TRN2", target_bir_lowering=False)
    dt = nc.dram_tensor
    xN = dt("xN", [T, D], F32, kind="ExternalInput").ap()
    Wl = dt("Wl", [D, H], F32, kind="ExternalInput").ap()
    Wr = dt("Wr", [D, H], F32, kind="ExternalInput").ap()
    Wa = dt("Wa", [H, H], F32, kind="ExternalInput").ap()
    Wi = dt("Wi", [H, H], F32, kind="ExternalInput").ap()
    Wo = dt("Wo", [H, D], F32, kind="ExternalInput").ap()
    WfT = dt("WfT", [D, D], F32, kind="ExternalInput").ap()
    vbl = dt("vbl", [H], F32, kind="ExternalInput").ap()
    vbr = dt("vbr", [H], F32, kind="ExternalInput").ap()
    vba = dt("vba", [H], F32, kind="ExternalInput").ap()
    vbi = dt("vbi", [H], F32, kind="ExternalInput").ap()
    vcb = dt("vcb", [H], F32, kind="ExternalInput").ap()
    vlam = dt("vlam", [H], F32, kind="ExternalInput").ap()
    vcw = dt("vcw", [4, H], F32, kind="ExternalInput").ap()
    vbo = dt("vbo", [D], F32, kind="ExternalInput").ap()
    OUT = dt("out", [T, D], F32, kind="ExternalOutput").ap()
    s_x1 = Buf(dt("s_x1", [T, D], F32, kind="ExternalOutput").ap())
    s_zn = Buf(dt("s_zn", [T, D], BF16, kind="ExternalOutput").ap())
    s_left = Buf(dt("s_left", [H, T], BF16, kind="ExternalOutput").ap())
    s_u = Buf(dt("s_u", [H, T], BF16, kind="ExternalOutput").ap())
    s_gat = Buf(dt("s_gat", [H, T], BF16, kind="ExternalOutput").ap())
    s_x2 = Buf(dt("s_x2", [T, D], F32, kind="ExternalOutput").ap())
    s_x2r = [Buf(s_x2.t) for _ in range(16)]
    s_wqT = Buf(dt("s_wqT", [D, D], BF16, kind="ExternalOutput").ap())
    out_b = Buf(OUT)

    with ExitStack() as ctx:
        E_ = ctx.enter_context
        sb = nc.sbuf_tensor
        bigA = E_(sb("bigA", [128, KH * T], BF16))
        bigB = E_(sb("bigB", [128, KD * TT], F32))
        wg = E_(sb("wg", [128, 16384], BF16))
        xq = wg[:, 0:KD * TT]
        stg = E_(sb("stg", [128, 4096], BF16))
        sc0 = E_(sb("sc0", [128, T + 8], F32))
        sc1 = E_(sb("sc1", [128, T + 8], F32))
        sc2 = E_(sb("sc2", [128, T], F32))
        sc3 = E_(sb("sc3", [128, T], F32))
        tiny4 = E_(sb("tiny4", [1, 4], F32))
        zero2 = E_(sb("zero2", [128, 2], F32))
        epsb = E_(sb("epsb", [128, 1], F32))
        t1a = E_(sb("t1a", [128, TT], F32))
        t1b = E_(sb("t1b", [128, TT], F32))
        xsa = E_(sb("xsa", [128, TT], F32))
        xsb = E_(sb("xsb", [128, TT], F32))
        borow = E_(sb("borow", [1, D], F32))
        stt8 = E_(sb("stt8", [128, 80], F32))
        vecs = E_(sb("vecs", [128, 256], F32))
        acc1 = E_(sb("acc1", [128, 1], F32))
        red1 = E_(sb("red1", [128, 1], F32))
        onesc = E_(sb("onesc", [128, 1], F32))
        onesr = E_(sb("onesr", [1, 128], F32))
        zero1 = E_(sb("zero1", [128, 1], F32))
        g0b = E_(sb("g0b", [128, 1], F32))
        c32n = E_(sb("c32n", [128, 1], F32))
        ghb = E_(sb("ghb", [128, 1], F32))
        g1b = E_(sb("g1b", [128, 1], F32))
        h1b = E_(sb("h1b", [128, 1], F32))
        alr = E_(sb("alr", [1, 1], F32))
        ra128 = E_(sb("ra128", [128, 1], F32))
        ha128 = E_(sb("ha128", [128, 1], F32))
        identb = E_(sb("identb", [128, 128], BF16))
        pmm = [E_(nc.psum_tensor("pmm%d" % i, [128, 512], F32)) for i in range(4)]
        pstat = E_(nc.psum_tensor("pstat", [1, 512], F32))
        pbc = E_(nc.psum_tensor("pbc", [128, 512], F32))
        ptrb = [E_(nc.psum_tensor("ptrb%d" % i, [128, 128], BF16))
                for i in range(2)]
        semnames = (["pe", "dve", "act", "pool"] +
                    ["d%s%d" % (q, i) for q in ("sp", "act", "pool")
                     for i in range(4)])
        sems = {n: E_(nc.semaphore(n)) for n in semnames}
        blk = E_(nc.Block())
        E = Emit(nc, blk, sems)

        bigA_b, bigB_b, wg_b, xq_b = Buf(bigA), Buf(bigB), Buf(wg), Buf(wg)
        wgh = [Buf(wg), Buf(wg)]
        stg_b = Buf(stg)

        def merge(dst, *srcs):
            for sbuf_ in srcs:
                if sbuf_.w:
                    dst.rs.append(sbuf_.w)
                dst.rs.extend(sbuf_.rs)
        sc = [Buf(x) for x in (sc0, sc1, sc2, sc3)]
        wfb = [bigB_b, bigB_b]
        t1_ = [Buf(t1a), Buf(t1b)]
        xs_ = [Buf(xsa), Buf(xsb)]
        borow_b = Buf(borow)
        stt8_b = Buf(stt8)
        vecs_b = Buf(vecs)
        tiny_b = Buf(tiny4)
        zero2_b = Buf(zero2)
        acc1_b, red1_b = Buf(acc1), Buf(red1)
        onesc_b, onesr_b, zero1_b = Buf(onesc), Buf(onesr), Buf(zero1)
        bigBh = [Buf(bigB), Buf(bigB)]
        g0_b, gh_b, g1_b, h1_b = Buf(g0b), Buf(ghb), Buf(g1b), Buf(h1b)
        c32n_b = Buf(c32n)
        eps_b = Buf(epsb)
        alr_b, ra_b, ha_b = Buf(alr), Buf(ra128), Buf(ha128)
        idb_b = Buf(identb)
        pmm_b = [Buf(p) for p in pmm]
        pstat_b, pbc_b = Buf(pstat), Buf(pbc)
        ptrb_b = [Buf(p) for p in ptrb]

        VBL, VBR, VHBA, VHBI, VCB = 0, 20, 40, 60, 80
        VMS2, VMS1, VBO, VCW, VSP = 100, 120, 140, 156, 236

        def vcol(off, j):
            return vecs[:, off + j:off + j + 1]

        E.dma("sp", lambda e: e.dma_start(
            out=vecs[:, VBL:VBL + 20], in_=vbl.rearrange("(c p) -> p c", p=128),
            allow_slow_non_contiguous=True),
            writes=[vecs_b])
        E.dma("sp", lambda e: e.dma_start(
            out=vecs[:, VBR:VBR + 20], in_=vbr.rearrange("(c p) -> p c", p=128),
            allow_slow_non_contiguous=True),
            writes=[vecs_b])
        E.dma("sp", lambda e: e.dma_start(
            out=vecs[:, VHBA:VHBA + 20], in_=vba.rearrange("(c p) -> p c", p=128),
            allow_slow_non_contiguous=True),
            writes=[vecs_b])
        E.dma("sp", lambda e: e.dma_start(
            out=vecs[:, VHBI:VHBI + 20], in_=vbi.rearrange("(c p) -> p c", p=128),
            allow_slow_non_contiguous=True),
            writes=[vecs_b])
        E.dma("sp", lambda e: e.dma_start(
            out=vecs[:, VCB:VCB + 20], in_=vcb.rearrange("(c p) -> p c", p=128),
            allow_slow_non_contiguous=True),
            writes=[vecs_b])
        E.dma("sp", lambda e: e.dma_start(
            out=vecs[:, VMS1:VMS1 + 20], in_=vlam.rearrange("(c p) -> p c", p=128),
            allow_slow_non_contiguous=True),
            writes=[vecs_b])
        E.dma("sp", lambda e: e.dma_start(
            out=borow[:], in_=vbo.rearrange("(u d) -> u d", u=1)),
            writes=[borow_b])
        E.dma("sp", lambda e: e.dma_start(
            out=vecs[:, VCW:VCW + 80],
            in_=vcw.rearrange("k (c p) -> p (k c)", p=128),
            allow_slow_non_contiguous=True),
            writes=[vecs_b])
        E.act(lambda e: e.activation(out=vecs[:, VSP:VSP + 20],
                                     in_=vecs[:, VMS1:VMS1 + 20],
                                     func=AF.Exp, scale=-1.0),
              reads=[vecs_b], writes=[vecs_b])
        E.dve(lambda e: e.tensor_scalar(vecs[:, VSP:VSP + 20],
                                        vecs[:, VSP:VSP + 20], 1.0, None,
                                        ALU.add),
              reads=[vecs_b], writes=[vecs_b])
        E.act(lambda e: e.activation(out=vecs[:, VSP:VSP + 20],
                                     in_=vecs[:, VSP:VSP + 20], func=AF.Ln),
              reads=[vecs_b], writes=[vecs_b])
        E.dve(lambda e: e.tensor_scalar(vecs[:, VMS1:VMS1 + 20],
                                        vecs[:, VSP:VSP + 20], -8.0, None, ALU.mult),
              reads=[vecs_b], writes=[vecs_b])
        E.dve(lambda e: e.tensor_scalar(vecs[:, VMS2:VMS2 + 20],
                                        vecs[:, VSP:VSP + 20], -4.0, None, ALU.mult),
              reads=[vecs_b], writes=[vecs_b])
        E.dve(lambda e: e.tensor_scalar(vecs[:, VHBA:VHBA + 20],
                                        vecs[:, VHBA:VHBA + 20], 0.5, None, ALU.mult),
              reads=[vecs_b], writes=[vecs_b])
        E.dve(lambda e: e.tensor_scalar(vecs[:, VHBI:VHBI + 20],
                                        vecs[:, VHBI:VHBI + 20], 0.5, None, ALU.mult),
              reads=[vecs_b], writes=[vecs_b])
        E.dve(lambda e: e.memset(onesc[:], 1.0), writes=[onesc_b])
        E.dve(lambda e: e.memset(onesr[:], 1.0), writes=[onesr_b])
        E.dve(lambda e: e.memset(zero1[:], 0.0), writes=[zero1_b])
        E.dve(lambda e: e.memset(zero2[:], 0.0), writes=[zero2_b])
        E.dve(lambda e: e.memset(g0b[:], C32), writes=[g0_b])
        E.dve(lambda e: e.memset(c32n[:], -C32), writes=[c32n_b])
        E.dve(lambda e: e.memset(epsb[:], 1e-5), writes=[eps_b])
        E.dve(lambda e: e.tensor_scalar(ghb[:], g0b[:], 0.5, None, ALU.add),
              reads=[g0_b], writes=[gh_b])
        E.dve(lambda e: e.tensor_tensor(g1b[:], ghb[:], g0b[:], ALU.subtract),
              reads=[gh_b, g0_b], writes=[g1_b])
        E.dve(lambda e: e.tensor_scalar(g1b[:], g1b[:], 2.0, None, ALU.mult),
              reads=[g1_b], writes=[g1_b])
        E.dve(lambda e: e.tensor_scalar(h1b[:], g1b[:], -1.0, 1.0,
                                        ALU.mult, ALU.add),
              reads=[g1_b], writes=[h1_b])
        E.pool(lambda e: e.memset(identb[:], 0.0), writes=[idb_b])
        E.pool(lambda e: e.affine_select(out=identb[:], in_=identb[:],
                                         compare_op=ALU.not_equal, fill=1.0,
                                         base=0, pattern=[[-1, 128]],
                                         channel_multiplier=1),
               reads=[idb_b], writes=[idb_b])

        for c in range(KD):
            b = bigBh[c % 2]
            hsl = slice((c % 2) * D, (c % 2) * D + D)
            E.dma("sp", lambda e, c=c, hsl=hsl: e.dma_start(
                out=bigB[:, hsl], in_=WfT[c * 128:(c + 1) * 128, :]), writes=[b])
            E.dve(lambda e, b=b, hsl=hsl: e.tensor_reduce(
                red1[:], bigB[:, hsl], axis=X_AX, op=ALU.add,
                apply_absolute_value=True), reads=[b], writes=[red1_b])
            if c == 0:
                E.dve(lambda e: e.tensor_copy(acc1[:], red1[:]),
                      reads=[red1_b], writes=[acc1_b])
            else:
                E.dve(lambda e: e.tensor_tensor(acc1[:], acc1[:], red1[:], ALU.add),
                      reads=[acc1_b, red1_b], writes=[acc1_b])
        E.pe(lambda e: e.matmul(pstat[0:1, 0:1], onesc[:], acc1[:],
                                start=True, stop=True),
             reads=[onesc_b, acc1_b], writes=[pstat_b])
        E.act(lambda e: e.activation(out=alr[:], in_=pstat[0:1, 0:1],
                                     func=AF.Copy, scale=1.0 / (D * D)),
              reads=[pstat_b], writes=[alr_b])
        E.dve(lambda e: e.tensor_scalar(alr[:], alr[:], 1e-8, None, ALU.max),
              reads=[alr_b], writes=[alr_b])
        E.act(lambda e: e.activation(out=tiny4[0:1, 0:1], in_=alr[:],
                                     func=AF.Copy, scale=0.5 / 127.0),
              reads=[alr_b], writes=[tiny_b])
        E.pe(lambda e: e.matmul(pbc[0:128, 0:1], onesr[:], tiny4[0:1, 0:1],
                                start=True, stop=True),
             reads=[tiny_b, onesr_b], writes=[pbc_b])
        E.act(lambda e: e.activation(out=ha128[:], in_=pbc[0:128, 0:1],
                                     func=AF.Copy), reads=[pbc_b], writes=[ha_b])
        E.dve(lambda e: e.reciprocal(tiny4[0:1, 1:2], alr[:]),
              reads=[alr_b], writes=[tiny_b])
        E.pe(lambda e: e.matmul(pbc[0:128, 0:1], onesr[:], tiny4[0:1, 1:2],
                                start=True, stop=True),
             reads=[onesr_b, tiny_b], writes=[pbc_b])
        E.act(lambda e: e.activation(out=ra128[:], in_=pbc[0:128, 0:1],
                                     func=AF.Copy), reads=[pbc_b], writes=[ra_b])

        def emit_quant(c):
            bq = bigBh[c % 2]
            hof = (c % 2) * D
            E.dma("sp", lambda e, c=c, hof=hof: e.dma_start(
                out=bigB[:, hof:hof + D], in_=WfT[c * 128:(c + 1) * 128, :]),
                writes=[bq])
            E.act(lambda e, bq=bq, hof=hof: e.activation(
                out=sc1[:, 0:D], in_=bigB[:, hof:hof + D], func=AF.Copy,
                scale=ra128[:]), reads=[bq, ra_b], writes=[sc[1]])
            E.act(lambda e: e.activation(out=sc2[:], in_=sc1[:, 0:D],
                                         func=AF.Identity, bias=g0b[:]),
                  reads=[sc[1], g0_b], writes=[sc[2]])
            E.act(lambda e: e.activation(out=sc2[:], in_=sc2[:],
                                         func=AF.Identity, bias=c32n[:]),
                  reads=[sc[2], c32n_b], writes=[sc[2]])
            E.dve(lambda e: e.tensor_scalar(sc3[:], sc1[:, 0:D], C64, -C64,
                                            ALU.add, ALU.add),
                  reads=[sc[1]], writes=[sc[3]])
            E.dve(lambda e: e.scalar_tensor_tensor(sc3[:], sc2[:], h1b[:], sc3[:],
                                                   ALU.mult, ALU.add),
                  reads=[sc[2], h1_b, sc[3]], writes=[sc[3]])
            E.dve(lambda e, c=c: e.tensor_scalar(
                bigA[:, c * D:(c + 1) * D], sc3[:], 1.0, -1.0, ALU.min, ALU.max),
                reads=[sc[3]], writes=[bigA_c[c]])
            E.dma("pool", lambda e, c=c: e.dma_start(
                out=s_wqT.t[c * 128:(c + 1) * 128, :],
                in_=bigA[:, c * D:(c + 1) * D]),
                reads=[bigA_c[c]], writes=[s_wqT])
        merge(bigB_b, bigBh[0], bigBh[1])

        TB = 256
        NTF = T // TB
        stt8_p = [stt8_b, Buf(stt8)]
        stg_p = [stg_b, Buf(stg)]
        bigB_h = [Buf(bigB), Buf(bigB)]
        xq_h = [Buf(wg), Buf(wg)]

        def sc8(ctx, j):
            return stt8[:, ctx * 32 + j * 2: ctx * 32 + j * 2 + 2]

        def sc8b(ctx, j, tb):
            return stt8[:, ctx * 32 + j * 2 + tb: ctx * 32 + j * 2 + tb + 1]

        rnd_p = [Buf(wg), Buf(wg)]
        bigA_c = [Buf(bigA) for _ in range(KD)]

        def ffn_parts(src_kind, with_ln2, dst_hbm, zn_out):

            def pre_mm(k):
                hp = (k + 1) % 2
                ctx = hp
                bB = bigB_h[hp]
                sb8 = stt8_p[ctx]
                hof = hp * 2 * D
                tsl = slice(k * TB, (k + 1) * TB)
                src = xN if src_kind == "x" else s_x2.t
                if src_kind == "x":
                    rd = []
                else:
                    rd = [s_x2r[2 * k], s_x2r[2 * k + 1]]
                bh = [Buf(bigB), Buf(bigB)]
                for tb in range(2):
                    rowsl = slice(k * TB + tb * 128, k * TB + (tb + 1) * 128)
                    E.dma("sp", lambda e, rowsl=rowsl, src=src, hof=hof, tb=tb:
                          e.dma_start(
                              out=bigB[:, hof + tb * D:hof + (tb + 1) * D],
                              in_=src[rowsl, :]),
                          reads=rd, war=[bB], wend=[bh[tb]])
                for tb in range(2):
                    po = tb
                    bsl = slice(hof + tb * D, hof + (tb + 1) * D)
                    E.dve(lambda e, bsl=bsl, ctx=ctx, tb=tb: e.reduce_max(
                        sc8b(ctx, 2, tb), bigB[:, bsl], axis=X_AX),
                        reads=[bh[tb]], writes=[sb8])
                    E.dve(lambda e, bsl=bsl, ctx=ctx, tb=tb: e.tensor_reduce(
                        sc8b(ctx, 3, tb), bigB[:, bsl], axis=X_AX, op=ALU.min),
                        reads=[bh[tb]], writes=[sb8])
                    E.act(lambda e, bsl=bsl, ctx=ctx, po=po, tb=tb:
                          e.activation(
                              out=stg[:, po * D:(po + 1) * D],
                              in_=bigB[:, bsl], func=AF.Square,
                              accum_out=sc8b(ctx, 1, tb)),
                          reads=[bh[tb]], writes=[stg_p[po], sb8])
                    E.act(lambda e, bsl=bsl, ctx=ctx, po=po, tb=tb:
                          e.activation(
                              out=stg[:, po * D:(po + 1) * D],
                              in_=bigB[:, bsl], func=AF.Copy,
                              accum_out=sc8b(ctx, 0, tb)),
                          reads=[bh[tb]], writes=[stg_p[po], sb8])
                merge(bB, bh[0], bh[1])
                E.dve(lambda e, ctx=ctx: e.tensor_scalar(
                    sc8(ctx, 4), sc8(ctx, 0), 1.0 / D, None, ALU.mult),
                    reads=[sb8], writes=[sb8])
                E.dve(lambda e, ctx=ctx: e.tensor_scalar(
                    sc8(ctx, 10), sc8(ctx, 1), 1.0 / D, None, ALU.mult),
                    reads=[sb8], writes=[sb8])
                E.dve(lambda e, ctx=ctx: e.tensor_tensor(
                    sc8(ctx, 11), sc8(ctx, 4), sc8(ctx, 4), ALU.mult),
                    reads=[sb8], writes=[sb8])
                E.dve(lambda e, ctx=ctx: e.tensor_tensor(
                    sc8(ctx, 11), sc8(ctx, 11), sc8(ctx, 10), ALU.subtract),
                    reads=[sb8], writes=[sb8])
                E.act(lambda e, ctx=ctx: e.activation(
                    out=sc8(ctx, 11), in_=sc8(ctx, 11), func=AF.Ln,
                    scale=-1.0, bias=epsb[:]),
                    reads=[sb8, eps_b], writes=[sb8])
                E.act(lambda e, ctx=ctx: e.activation(
                    out=sc8(ctx, 5), in_=sc8(ctx, 11), func=AF.Exp,
                    scale=-0.5), reads=[sb8], writes=[sb8])
                E.dve(lambda e, ctx=ctx: e.tensor_tensor(
                    sc8(ctx, 2), sc8(ctx, 2), sc8(ctx, 4), ALU.subtract),
                    reads=[sb8], writes=[sb8])
                E.dve(lambda e, ctx=ctx: e.tensor_tensor(
                    sc8(ctx, 3), sc8(ctx, 4), sc8(ctx, 3), ALU.subtract),
                    reads=[sb8], writes=[sb8])
                E.dve(lambda e, ctx=ctx: e.tensor_tensor(
                    sc8(ctx, 6), sc8(ctx, 2), sc8(ctx, 3), ALU.max),
                    reads=[sb8], writes=[sb8])
                E.dve(lambda e, ctx=ctx: e.tensor_scalar(
                    sc8(ctx, 6), sc8(ctx, 6), 1.0 / 127.0, 7.9e-33,
                    ALU.mult, ALU.max),
                    reads=[sb8], writes=[sb8])
                E.dve(lambda e, ctx=ctx: e.reciprocal(
                    sc8(ctx, 7), sc8(ctx, 6)), reads=[sb8], writes=[sb8])
                E.dve(lambda e, ctx=ctx: e.tensor_tensor(
                    sc8(ctx, 8), sc8(ctx, 4), sc8(ctx, 7), ALU.mult),
                    reads=[sb8], writes=[sb8])
                E.dve(lambda e, ctx=ctx: e.tensor_scalar(
                    sc8(ctx, 8), sc8(ctx, 8), -1.0, None, ALU.mult),
                    reads=[sb8], writes=[sb8])
                E.dve(lambda e, ctx=ctx: e.tensor_tensor(
                    sc8(ctx, 10), sc8(ctx, 5), sc8(ctx, 6), ALU.mult),
                    reads=[sb8], writes=[sb8])
                E.dve(lambda e, ctx=ctx: e.scalar_tensor_tensor(
                    sc8(ctx, 9), sc8(ctx, 10), ha128[:], zero2[:, 0:2],
                    ALU.mult, ALU.subtract),
                    reads=[sb8, ha_b, zero2_b], writes=[sb8])
                E.dve(lambda e, ctx=ctx: e.tensor_scalar(
                    sc8(ctx, 9), sc8(ctx, 9), 127.0, None, ALU.mult),
                    reads=[sb8], writes=[sb8])
                for tb in range(2):
                    po = tb
                    rG = rnd_p[po]
                    bsl = slice(hof + tb * D, hof + (tb + 1) * D)
                    rsl = slice(8192 + po * D, 8192 + (po + 1) * D)
                    zt = sc1 if po == 0 else sc2
                    zb = sc[1] if po == 0 else sc[2]
                    E.act(lambda e, bsl=bsl, ctx=ctx, tb=tb, zt=zt:
                          e.activation(
                              out=zt[:, 0:D], in_=bigB[:, bsl],
                              func=AF.Identity, scale=sc8b(ctx, 7, tb),
                              bias=sc8b(ctx, 8, tb)),
                          reads=[bB, sb8], writes=[zb])
                    ab = sc[3] if po == 0 else sc[0]
                    at = sc3 if po == 0 else sc0
                    E.dve(lambda e, zt=zt, at=at: e.tensor_scalar(
                        at[:, 0:D], zt[:, 0:D], C32, -C32, ALU.add, ALU.add),
                        reads=[zb], writes=[ab])
                    E.dve(lambda e, zt=zt: e.tensor_scalar(
                        zt[:, 0:D], zt[:, 0:D], C64, -C64, ALU.add, ALU.add),
                        reads=[zb], writes=[zb])
                    E.dve(lambda e, rsl=rsl, at=at, zt=zt:
                          e.scalar_tensor_tensor(
                              wg[:, rsl], at[:, 0:D], h1b[:], zt[:, 0:D],
                              ALU.mult, ALU.add),
                          reads=[ab, h1_b, zb], writes=[rG])

            def tr(k, jgrp):
                hp = (k + 1) % 2
                xQ = xq_h[hp]
                xof = hp * KD * TB
                base = jgrp * 4
                for u in range(base, base + 4):
                    tb, i = divmod(u, KD)
                    po = tb
                    rG = rnd_p[po]
                    rsl0 = 8192 + po * D
                    pb = ptrb_b[u % 2]
                    E.pe(lambda e, rsl0=rsl0, i=i, pb=pb: e.transpose(
                        pb.t[:], wg[:, rsl0 + i * 128: rsl0 + (i + 1) * 128],
                        identb[:]),
                        reads=[rG, idb_b], writes=[pb])
                    off = xof + i * TB + tb * 128
                    E.dve(lambda e, off=off, pb=pb: e.tensor_copy(
                        wg[:, off:off + 128], pb.t[:]),
                        reads=[pb], writes=[xQ])

            def mm(k, embed=None):
                hp = (k + 1) % 2
                ctx = hp
                sb8 = stt8_p[ctx]
                bB = bigB_h[hp]
                xQ = xq_h[hp]
                hof = hp * 2 * D
                xof = hp * KD * TB
                tsl = slice(k * TB, (k + 1) * TB)
                grp = 0
                for tb in range(2):
                    po = tb
                    for os in range(4):
                        p = pmm_b[(tb * 4 + os) % 4]
                        for i in range(KD):
                            E.pe(lambda e, i=i, tb=tb, os=os, p=p, xof=xof:
                                 e.matmul(
                                     p.t[:],
                                     wg[:, xof + i * TB + tb * 128:
                                         xof + i * TB + tb * 128 + 128],
                                     bigA[:, i * D + os * 512:
                                          i * D + (os + 1) * 512],
                                     start=(i == 0), stop=(i == KD - 1)),
                                 reads=[xQ, bigA_c[i]],
                                 war=[p] if i == 0 else (),
                                 wend=[p] if i == KD - 1 else ())
                        if embed is not None and grp >= 4:
                            embed((grp - 4) * 2)
                            embed((grp - 4) * 2 + 1)
                        grp += 1
                        xo = slice(hof + tb * D + os * 512,
                                   hof + tb * D + (os + 1) * 512)
                        if with_ln2:
                            E.dve(lambda e, p=p, ctx=ctx, tb=tb, os=os, xo=xo:
                                  e.scalar_tensor_tensor(
                                      bigB[:, xo], p.t[:], sc8b(ctx, 9, tb),
                                      bigB[:, xo], ALU.mult, ALU.add,
                                      accum_out=sc8b(ctx, 16 + os, tb)),
                                  reads=[p, sb8, bB], writes=[bB, sb8])
                            sqs = slice(tb * D + os * 512,
                                        tb * D + (os + 1) * 512)
                            E.act(lambda e, ctx=ctx, tb=tb, os=os, xo=xo,
                                  sqs=sqs: e.activation(
                                      out=stg[:, sqs], in_=bigB[:, xo],
                                      func=AF.Square,
                                      accum_out=sc8b(ctx, 12 + os, tb)),
                                  reads=[bB], writes=[stg_p[tb], sb8])
                        else:
                            E.dve(lambda e, p=p, ctx=ctx, tb=tb, xo=xo:
                                  e.scalar_tensor_tensor(
                                      bigB[:, xo], p.t[:], sc8b(ctx, 9, tb),
                                      bigB[:, xo], ALU.mult, ALU.add),
                                  reads=[p, sb8, bB], writes=[bB])
                E.dma("pool" if k < NTF - 1 else "sp",
                      lambda e, tsl=tsl, hof=hof: e.dma_start(
                          out=dst_hbm.t[tsl, :].rearrange(
                              "(b p) d -> p b d", p=128),
                          in_=bigB[:, hof:hof + 2 * D].rearrange(
                              "p (b d) -> p b d", d=D)),
                      reads=[bB], writes=[dst_hbm])
                if with_ln2:
                    E.dve(lambda e, ctx=ctx: e.tensor_tensor(
                        sc8(ctx, 16), sc8(ctx, 16), sc8(ctx, 17), ALU.add),
                        reads=[sb8], writes=[sb8])
                    E.dve(lambda e, ctx=ctx: e.tensor_tensor(
                        sc8(ctx, 18), sc8(ctx, 18), sc8(ctx, 19), ALU.add),
                        reads=[sb8], writes=[sb8])
                    E.dve(lambda e, ctx=ctx: e.tensor_tensor(
                        sc8(ctx, 0), sc8(ctx, 16), sc8(ctx, 18), ALU.add),
                        reads=[sb8], writes=[sb8])
                    E.dve(lambda e, ctx=ctx: e.tensor_tensor(
                        sc8(ctx, 12), sc8(ctx, 12), sc8(ctx, 13), ALU.add),
                        reads=[sb8], writes=[sb8])
                    E.dve(lambda e, ctx=ctx: e.tensor_tensor(
                        sc8(ctx, 14), sc8(ctx, 14), sc8(ctx, 15), ALU.add),
                        reads=[sb8], writes=[sb8])
                    E.dve(lambda e, ctx=ctx: e.tensor_tensor(
                        sc8(ctx, 1), sc8(ctx, 12), sc8(ctx, 14), ALU.add),
                        reads=[sb8], writes=[sb8])
                    E.dve(lambda e, ctx=ctx: e.tensor_scalar(
                        sc8(ctx, 4), sc8(ctx, 0), 1.0 / D, None, ALU.mult),
                        reads=[sb8], writes=[sb8])
                    E.dve(lambda e, ctx=ctx: e.tensor_scalar(
                        sc8(ctx, 10), sc8(ctx, 1), 1.0 / D, None, ALU.mult),
                        reads=[sb8], writes=[sb8])
                    E.dve(lambda e, ctx=ctx: e.tensor_tensor(
                        sc8(ctx, 11), sc8(ctx, 4), sc8(ctx, 4), ALU.mult),
                        reads=[sb8], writes=[sb8])
                    E.dve(lambda e, ctx=ctx: e.tensor_tensor(
                        sc8(ctx, 11), sc8(ctx, 11), sc8(ctx, 10), ALU.subtract),
                        reads=[sb8], writes=[sb8])
                    E.act(lambda e, ctx=ctx: e.activation(
                        out=sc8(ctx, 11), in_=sc8(ctx, 11), func=AF.Ln,
                        scale=-1.0, bias=epsb[:]),
                        reads=[sb8, eps_b], writes=[sb8])
                    E.act(lambda e, ctx=ctx: e.activation(
                        out=sc8(ctx, 5), in_=sc8(ctx, 11), func=AF.Exp,
                        scale=-0.5), reads=[sb8], writes=[sb8])
                    E.dve(lambda e, ctx=ctx: e.tensor_tensor(
                        sc8(ctx, 8), sc8(ctx, 4), sc8(ctx, 5), ALU.mult),
                        reads=[sb8], writes=[sb8])
                    E.dve(lambda e, ctx=ctx: e.tensor_scalar(
                        sc8(ctx, 8), sc8(ctx, 8), -1.0, None, ALU.mult),
                        reads=[sb8], writes=[sb8])
                    for tb in range(2):
                        po = tb
                        bsl = slice(hof + tb * D, hof + (tb + 1) * D)
                        ssl = slice(po * D, (po + 1) * D)
                        E.act(lambda e, bsl=bsl, ssl=ssl, ctx=ctx, tb=tb:
                              e.activation(
                                  out=stg[:, ssl], in_=bigB[:, bsl],
                                  func=AF.Identity, scale=sc8b(ctx, 5, tb),
                                  bias=sc8b(ctx, 8, tb)),
                              reads=[bB, sb8], writes=[stg_p[tb]])
                    E.dma("pool" if k < NTF - 1 else "sp",
                      lambda e, tsl=tsl: e.dma_start(
                          out=zn_out.t[tsl, :].rearrange(
                              "(b p) d -> p b d", p=128),
                          in_=stg[:, 0:2 * D].rearrange(
                              "p (b d) -> p b d", d=D)),
                      reads=[stg_p[0], stg_p[1]], writes=[zn_out])


            return pre_mm, tr, mm

        def ffn_phase(src_kind, with_ln2, dst_hbm, zn_out):
            pre_mm, tr, mm = ffn_parts(src_kind, with_ln2, dst_hbm, zn_out)
            pre_mm(0)
            for j in range(8):
                tr(0, j)
            pre_mm(1)
            for k in range(NTF):
                nxt = k + 1
                if nxt < NTF:
                    mm(k, embed=lambda j, nxt=nxt: tr(nxt, j))
                else:
                    mm(k)
                if k + 2 < NTF:
                    pre_mm(k + 2)

        pre1, tr1, mm1 = ffn_parts("x", True, s_x1, s_zn)
        for c in range(KD):
            emit_quant(c)
            if c == 9:
                pre1(0)
            if c == 12:
                for j in range(8):
                    tr1(0, j)
        merge(bigB_h[0], bigB_b, bigBh[0], bigBh[1])
        pre1(1)
        for k in range(NTF):
            nxt = k + 1
            if nxt < NTF:
                mm1(k, embed=lambda j, nxt=nxt: tr1(nxt, j))
            else:
                mm1(k)
            if k + 2 < NTF:
                pre1(k + 2)
        merge(stg_b, stg_p[1])

        merge(wgh[0], xq_b, *xq_h, *rnd_p)
        merge(wgh[1], xq_b, *xq_h, *rnd_p)
        merge(stg_b, stg_p[0], stg_p[1])
        merge(stg_p[0], stg_b)
        merge(stg_p[1], stg_b)
        for c in range(KD):
            E.dma("act" if c % 2 else "sp", lambda e, c=c: e.dma_start(
                out=bigA[:, c * T:(c + 1) * T],
                in_=s_zn.t[:, c * 128:(c + 1) * 128], transpose=True),
                reads=[s_zn], writes=[bigA_b], war=[bigA_c[c]])

        def wl_phase(W, evac, out_hbm, conv):
            ng = H // 512
            for g in range(ng):
                half = g % 2
                wslice = slice(half * 8192, half * 8192 + 8192)
                E.dma("pool", lambda e, g=g, wslice=wslice, W=W: e.dma_start(
                    out=wg[:, wslice].rearrange("p (c h) -> p c h", h=512),
                    in_=W[:, g * 512:(g + 1) * 512].rearrange(
                        "(c p) h -> p c h", p=128)),
                    writes=[wgh[half]])
                for hb in range(4):
                    blkidx = g * 4 + hb
                    for k in range(NT):
                        p = pmm_b[k % 4]
                        for i in range(KD):
                            off = half * 8192 + i * 512 + hb * 128
                            E.pe(lambda e, off=off, i=i, k=k, p=p: e.matmul(
                                p.t[:], wg[:, off:off + 128],
                                bigA[:, i * T + k * TT: i * T + (k + 1) * TT],
                                start=(i == 0), stop=(i == KD - 1)),
                                reads=[wgh[half], bigA_b],
                                war=[p] if i == 0 else (),
                                wend=[p] if i == KD - 1 else ())
                        evac(blkidx, k, p)
                    if conv is not None:
                        conv(blkidx)

        def left_evac(blkidx, k, p):
            st = stg[:, (k % 4) * TT:(k % 4) * TT + TT]
            E.act(lambda e, blkidx=blkidx, p=p, st=st: e.activation(
                out=st, in_=p.t[:], func=AF.Gelu_apprx_tanh,
                bias=vcol(VBL, blkidx)),
                reads=[p, vecs_b], writes=[stg_b])
            E.dma("sp", lambda e, blkidx=blkidx, k=k, st=st: e.dma_start(
                out=s_left.t[blkidx * 128:(blkidx + 1) * 128,
                              k * TT:(k + 1) * TT], in_=st),
                reads=[stg_b], writes=[s_left])

        wl_phase(Wl, left_evac, s_left, None)

        def rin_evac(blkidx, k, p):
            E.act(lambda e, blkidx=blkidx, k=k, p=p: e.activation(
                out=sc0[:, 3 + k * TT: 3 + (k + 1) * TT], in_=p.t[:],
                func=AF.Identity, bias=vcol(VBR, blkidx)),
                reads=[p, vecs_b], writes=[sc[0]])

        ng = H // 512
        for g in range(ng):
            half = g % 2
            wslice = slice(half * 8192, half * 8192 + 8192)
            E.dma("pool", lambda e, g=g, wslice=wslice: e.dma_start(
                out=wg[:, wslice].rearrange("p (c h) -> p c h", h=512),
                in_=Wr[:, g * 512:(g + 1) * 512].rearrange(
                    "(c p) h -> p c h", p=128)),
                writes=[wgh[half]])
            for hb in range(4):
                blkidx = g * 4 + hb
                bp = blkidx % 2
                rinT = sc0 if bp == 0 else sc1
                rinB = sc[0] if bp == 0 else sc[1]
                uacc = sc2 if bp == 0 else sc3
                uaccB = sc[2] if bp == 0 else sc[3]
                E.dve(lambda e, rinT=rinT: e.memset(rinT[:, 0:3], 0.0),
                      writes=[rinB])
                for k in range(NT):
                    p = pmm_b[k % 4]
                    for i in range(KD):
                        off = half * 8192 + i * 512 + hb * 128
                        E.pe(lambda e, off=off, i=i, k=k, p=p: e.matmul(
                            p.t[:], wg[:, off:off + 128],
                            bigA[:, i * T + k * TT: i * T + (k + 1) * TT],
                            start=(i == 0), stop=(i == KD - 1)),
                            reads=[wgh[half], bigA_b],
                            war=[p] if i == 0 else (),
                            wend=[p] if i == KD - 1 else ())
                    E.act(lambda e, blkidx=blkidx, k=k, p=p, rinT=rinT:
                          e.activation(
                              out=rinT[:, 3 + k * TT: 3 + (k + 1) * TT],
                              in_=p.t[:], func=AF.Identity,
                              bias=vcol(VBR, blkidx)),
                          reads=[p, vecs_b], writes=[rinB])
                E.act(lambda e, blkidx=blkidx, rinT=rinT, uacc=uacc:
                      e.activation(
                          out=uacc[:, 0:T], in_=rinT[:, 0:T], func=AF.Identity,
                          scale=vcol(VCW, blkidx), bias=vcol(VCB, blkidx)),
                      reads=[rinB, vecs_b], writes=[uaccB])
                for kk in (1, 2, 3):
                    E.dve(lambda e, kk=kk, blkidx=blkidx, rinT=rinT, uacc=uacc:
                          e.scalar_tensor_tensor(
                              uacc[:, 0:T], rinT[:, kk:kk + T],
                              vcol(VCW, kk * 20 + blkidx),
                              uacc[:, 0:T], ALU.mult, ALU.add),
                          reads=[rinB, vecs_b, uaccB], writes=[uaccB])
                stu = stg[:, bp * T:bp * T + T]
                E.dve(lambda e, stu=stu, uacc=uacc: e.tensor_copy(
                    stu, uacc[:, 0:T]),
                    reads=[uaccB], writes=[stg_p[bp]])
                E.dma("sp", lambda e, blkidx=blkidx, stu=stu: e.dma_start(
                    out=s_u.t[blkidx * 128:(blkidx + 1) * 128, :], in_=stu),
                    reads=[stg_p[bp]], writes=[s_u])

        merge(stg_p[0], stg_b, stg_p[1])
        merge(stg_p[1], stg_b, stg_p[0])
        merge(rnd_p[0], wgh[0], wgh[1])
        E.dma("act", lambda e: e.dma_start(
            out=bigA[:, 0:KH * T].rearrange("p (c t) -> p c t", t=T),
            in_=s_u.t[:].rearrange("(c p) t -> p c t", p=128)),
            reads=[s_u], writes=[bigA_b])
        for b in range(KH):
            half = b % 2
            wa_sl = slice(half * 5120, half * 5120 + 2560)
            wi_sl = slice(half * 5120 + 2560, half * 5120 + 5120)
            E.dma("pool", lambda e, b=b, wa_sl=wa_sl: e.dma_start(
                out=wg[:, wa_sl].rearrange("p (c h) -> p c h", h=128),
                in_=Wa[:, b * 128:(b + 1) * 128].rearrange(
                    "(c p) h -> p c h", p=128)), writes=[wgh[half]])
            E.dma("pool", lambda e, b=b, wi_sl=wi_sl: e.dma_start(
                out=wg[:, wi_sl].rearrange("p (c h) -> p c h", h=128),
                in_=Wi[:, b * 128:(b + 1) * 128].rearrange(
                    "(c p) h -> p c h", p=128)), writes=[wgh[half]])
            stl = stg[:, (b % 2) * T:(b % 2) * T + T]
            stlb = stg_p[b % 2]
            E.dma("sp", lambda e, b=b, stl=stl: e.dma_start(
                out=stl, in_=s_left.t[b * 128:(b + 1) * 128, :]),
                reads=[s_left], writes=[stlb])
            for k in range(NT):
                pr = pmm_b[(2 * k) % 4]
                pi = pmm_b[(2 * k + 1) % 4]
                for c in range(KH):
                    off = half * 5120 + c * 128
                    E.pe(lambda e, off=off, c=c, pr=pr, k=k: e.matmul(
                        pr.t[:], wg[:, off:off + 128],
                        bigA[:, c * T + k * TT: c * T + (k + 1) * TT],
                        start=(c == 0), stop=(c == KH - 1)),
                        reads=[wgh[half], bigA_b],
                        war=[pr] if c == 0 else (),
                        wend=[pr] if c == KH - 1 else ())
                for c in range(KH):
                    off = half * 5120 + 2560 + c * 128
                    E.pe(lambda e, off=off, c=c, pi=pi, k=k: e.matmul(
                        pi.t[:], wg[:, off:off + 128],
                        bigA[:, c * T + k * TT: c * T + (k + 1) * TT],
                        start=(c == 0), stop=(c == KH - 1)),
                        reads=[wgh[half], bigA_b],
                        war=[pi] if c == 0 else (),
                        wend=[pi] if c == KH - 1 else ())
                ksl = slice(k * TT, (k + 1) * TT)
                E.act(lambda e, b=b, pr=pr, ksl=ksl: e.activation(
                    out=sc0[:, ksl], in_=pr.t[:], func=AF.Tanh, scale=0.5,
                    bias=vcol(VHBA, b)), reads=[pr, vecs_b], writes=[sc[0]])
                E.act(lambda e, b=b, pi=pi, ksl=ksl: e.activation(
                    out=sc1[:, ksl], in_=pi.t[:], func=AF.Tanh, scale=0.5,
                    bias=vcol(VHBI, b)), reads=[pi, vecs_b], writes=[sc[1]])
            E.act(lambda e, b=b: e.activation(
                out=sc2[:], in_=sc0[:, 0:T], func=AF.Exp,
                scale=vcol(VMS2, b), bias=vcol(VMS2, b)),
                reads=[sc[0], vecs_b], writes=[sc[2]])
            E.act(lambda e, b=b: e.activation(
                out=sc3[:], in_=sc0[:, 0:T], func=AF.Exp,
                scale=vcol(VMS1, b), bias=vcol(VMS1, b)),
                reads=[sc[0], vecs_b], writes=[sc[3]])
            E.dve(lambda e: e.tensor_scalar(sc3[:], sc3[:], -1.0, 1.0,
                                            ALU.mult, ALU.add),
                  reads=[sc[3]], writes=[sc[3]])
            E.dve(lambda e: e.tensor_scalar(sc3[:], sc3[:], 1e-12, None,
                                            ALU.max),
                  reads=[sc[3]], writes=[sc[3]])
            E.act(lambda e: e.activation(out=sc0[:, 0:T], in_=sc3[:],
                                         func=AF.Sqrt, scale=0.25),
                  reads=[sc[3]], writes=[sc[0]])
            E.dve(lambda e: e.tensor_scalar(sc1[:, 0:T], sc1[:, 0:T], 1.0,
                                            None, ALU.add),
                  reads=[sc[1]], writes=[sc[1]])
            E.dve(lambda e, b=b: e.tensor_tensor(
                sc1[:, 0:T], sc1[:, 0:T], bigA[:, b * T:(b + 1) * T],
                ALU.mult),
                reads=[sc[1], bigA_b], writes=[sc[1]])
            E.dve(lambda e: e.tensor_tensor(sc1[:, 0:T], sc1[:, 0:T],
                                            sc0[:, 0:T], ALU.mult),
                  reads=[sc[1], sc[0]], writes=[sc[1]])
            E.dve(lambda e: e.tensor_tensor_scan(sc3[:], sc2[:], sc1[:, 0:T],
                                                 zero1[:], ALU.mult, ALU.add),
                  reads=[sc[2], sc[1], zero1_b], writes=[sc[3]])
            stgo = wg[:, 10240:10240 + T]
            E.dve(lambda e, stl=stl, stgo=stgo, stlb=stlb: e.tensor_tensor(
                stgo, sc3[:], stl, ALU.mult),
                reads=[sc[3], stlb], writes=[rnd_p[0]])
            E.dma("sp", lambda e, b=b, stgo=stgo: e.dma_start(
                out=s_gat.t[b * 128:(b + 1) * 128, :], in_=stgo),
                reads=[rnd_p[0]], writes=[s_gat])

        E.dma("act", lambda e: e.dma_start(
            out=bigA[:, 0:KH * T].rearrange("p (c t) -> p c t", t=T),
            in_=s_gat.t[:].rearrange("(c p) t -> p c t", p=128)),
            reads=[s_gat], writes=[bigA_b])
        for os in range(4):
            osl = slice(os * 512, (os + 1) * 512)
            E.dma("pool", lambda e, osl=osl: e.dma_start(
                out=wg[:, 0:KH * 512].rearrange("p (c h) -> p c h", h=512),
                in_=Wo[:, osl].rearrange("(c p) h -> p c h", p=128)),
                writes=[wgh[0], wgh[1]], war=[rnd_p[0]])
            for tb in range(16):
                p = pmm_b[tb % 4]
                xb = xs_[tb % 2]
                E.dma("sp", lambda e, tb=tb, osl=osl, xb=xb: e.dma_start(
                    out=xb.t[:], in_=s_x1.t[tb * 128:(tb + 1) * 128, osl]),
                    reads=[s_x1], writes=[xb])
                for c in range(KH):
                    E.pe(lambda e, c=c, tb=tb, p=p: e.matmul(
                        p.t[:], bigA[:, c * T + tb * 128: c * T + tb * 128 + 128],
                        wg[:, c * 512:(c + 1) * 512],
                        start=(c == 0), stop=False),
                        reads=[bigA_b, wgh[0]],
                        war=[p] if c == 0 else ())
                E.pe(lambda e, osl=osl, p=p: e.matmul(
                    p.t[:], onesr[:], borow[0:1, osl],
                    start=False, stop=True),
                    reads=[onesr_b, borow_b], wend=[p])
                tbuf = t1_[tb % 2]
                E.dve(lambda e, p=p, xb=xb, tbuf=tbuf: e.tensor_tensor(
                    tbuf.t[:], p.t[:], xb.t[:], ALU.add),
                    reads=[p, xb], writes=[tbuf])
                E.dma("pool", lambda e, tb=tb, osl=osl, tbuf=tbuf: e.dma_start(
                    out=s_x2.t[tb * 128:(tb + 1) * 128, osl], in_=tbuf.t[:]),
                    reads=[tbuf], writes=[s_x2r[tb]])

        for _xb in xq_h:
            merge(_xb, wgh[0], wgh[1])
        for _xb in rnd_p:
            merge(_xb, wgh[0], wgh[1])
        merge(stg_p[0], stg_b)
        merge(stg_p[1], stg_b)
        E.dma("act", lambda e: e.dma_start(
            out=bigA[:, 0:KD * D].rearrange("p (c o) -> p c o", o=D),
            in_=s_wqT.t[:].rearrange("(c p) o -> p c o", p=128)),
            reads=[s_wqT], writes=[bigA_b] + bigA_c)
        ffn_phase("x2", False, out_b, None)

        final = [(n, E.cnt[n]) for n in E.cnt]
        E.barrier_wait("sp", final)
        E._flush()

    return nc


def kernel(L0_attn_x, L0_attn_Wl, L0_attn_bl, L0_attn_Wr, L0_attn_br,
           L0_attn_conv_w, L0_attn_conv_b, L0_attn_Wa, L0_attn_ba,
           L0_attn_Wi, L0_attn_bi, L0_attn_lam, L0_attn_Wo, L0_attn_bo,
           L0_ffn_weight):
    x = np.asarray(L0_attn_x)
    n = 8
    nc = build_nc()
    WfT = np.ascontiguousarray(np.asarray(L0_ffn_weight).T)
    cw = np.ascontiguousarray(np.asarray(L0_attn_conv_w).reshape(4, H))
    common = {
        "Wl": np.ascontiguousarray(L0_attn_Wl),
        "Wr": np.ascontiguousarray(L0_attn_Wr),
        "Wa": np.ascontiguousarray(L0_attn_Wa),
        "Wi": np.ascontiguousarray(L0_attn_Wi),
        "Wo": np.ascontiguousarray(L0_attn_Wo),
        "WfT": WfT,
        "vbl": np.ascontiguousarray(L0_attn_bl),
        "vbr": np.ascontiguousarray(L0_attn_br),
        "vba": np.ascontiguousarray(L0_attn_ba),
        "vbi": np.ascontiguousarray(L0_attn_bi),
        "vcb": np.ascontiguousarray(L0_attn_conv_b),
        "vlam": np.ascontiguousarray(L0_attn_lam),
        "vcw": cw,
        "vbo": np.ascontiguousarray(L0_attn_bo),
    }
    in_maps = []
    for b in range(n):
        m = dict(common)
        m["xN"] = np.ascontiguousarray(x[b])
        in_maps.append(m)
    res = run_bass_kernel_spmd(nc, in_maps, core_ids=list(range(n)))
    out = np.empty((n, T, D), np.float32)
    for b in range(n):
        out[b] = res.results[b]["out"]
    return out
